# version 12
# baseline (speedup 1.0000x reference)
import numpy as np
import ml_dtypes
import concourse.bass as bass
import concourse.mybir as mybir
from concourse.bass_utils import run_bass_kernel_spmd

F32 = mybir.dt.float32
BF16 = mybir.dt.bfloat16
AF = mybir.ActivationFunctionType
ALU = mybir.AluOpType
AX = mybir.AxisListType

NT = 2048
D = 1024
NTC = 16
NFC = 8
H = 16
HD = 64
EPS = 1e-5

ENGINES = ("sync", "tensor", "vector", "scalar", "gpsimd")



def _bucket_1d():
    rel = np.arange(4095) - 2047
    nb = 16
    buckets = (rel > 0).astype(np.int64) * nb
    rp = np.abs(rel)
    max_exact = nb // 2
    is_small = rp < max_exact
    large = max_exact + (np.log(np.maximum(rp, 1) / max_exact) /
                         np.log(128 / max_exact) * (nb - max_exact)).astype(np.int64)
    large = np.minimum(large, nb - 1)
    return buckets + np.where(is_small, rp, large)


def host_constants():
    c = {}
    b = _bucket_1d()
    oh = np.zeros((32, 4095), np.float32)
    oh[b[::-1], np.arange(4095)] = 1.0
    c["c_oh"] = oh.astype(ml_dtypes.bfloat16)
    jk = np.outer(np.arange(D, dtype=np.float64), np.arange(D, dtype=np.float64))
    ang = 2.0 * np.pi * (jk % D) / D
    c["c_cos"] = np.cos(ang).astype(np.float32).astype(ml_dtypes.bfloat16)
    c["c_sin"] = np.sin(ang).astype(np.float32).astype(ml_dtypes.bfloat16)
    c["c_id"] = np.eye(128, dtype=np.float32).astype(ml_dtypes.bfloat16)
    return c



class Bld:

    def __init__(self, nc):
        self.nc = nc
        self.q = {e: [] for e in ENGINES}
        self.cnt = {}
        self._cms = []

    def sem(self, name):
        cm = self.nc.semaphore(name)
        h = cm.__enter__()
        self._cms.append(cm)
        self.cnt[h] = 0
        return h

    def close_sems(self):
        for cm in reversed(self._cms):
            cm.__exit__(None, None, None)

    def op(self, eng, method, *args, waits=(), inc=None, inc_by=1, lp=None, **kw):
        after = None
        if inc is not None:
            self.cnt[inc] += inc_by
            after = self.cnt[inc]
        wl = tuple(waits)

        def f(e, method=method, args=args, kw=kw, wl=wl, inc=inc, inc_by=inc_by, lp=lp):
            for s, v in wl:
                e.wait_ge(s, v)
            if lp:
                with self.nc.allow_low_precision(reason=lp):
                    ins = getattr(e, method)(*args, **kw)
            else:
                ins = getattr(e, method)(*args, **kw)
            if inc is not None:
                ins.then_inc(inc, inc_by)
        self.q[eng].append(f)
        return after

    def wait(self, eng, s, v):
        def f(e, s=s, v=v):
            e.wait_ge(s, v)
        self.q[eng].append(f)

    def raw(self, eng, fn):
        self.q[eng].append(fn)

    def render(self):
        nc = self.nc
        with nc.Block() as block:
            @block.sync
            def _(e):
                for f in self.q["sync"]:
                    f(e)

            @block.tensor
            def _(e):
                for f in self.q["tensor"]:
                    f(e)

            @block.vector
            def _(e):
                for f in self.q["vector"]:
                    f(e)

            @block.scalar
            def _(e):
                for f in self.q["scalar"]:
                    f(e)

            @block.gpsimd
            def _(e):
                for f in self.q["gpsimd"]:
                    f(e)


def w_chunk_dma(b, dst_ap_args, src, rows, cols, sem, split=2, eng="gpsimd"):
    t, col0 = dst_ap_args
    nch = rows // 128
    per = max(1, nch // split)
    total = 0
    ch = 0
    while ch < nch:
        n = min(per, nch - ch)
        dst = bass.AP(t, col0 + cols * ch, [[t.shape[1], 128], [cols, n], [1, cols]])
        sap = bass.AP(src, 128 * cols * ch, [[cols, 128], [128 * cols, n], [1, cols]])
        b.op(eng, "dma_start", dst, sap, inc=sem, inc_by=16)
        total += 16
        ch += n
    return total



def build_program(n_sub=4, dbg=False, cut=None):
    nc = bass.Bass("TRN2", target_bir_lowering=False)
    b = Bld(nc)

    dr = {}
    dr["x"] = nc.dram_tensor("x", [NT, D], F32, kind="ExternalInput")
    dr["y"] = nc.dram_tensor("y", [NT, D], F32, kind="ExternalOutput")
    for nm, shp, dt in [("c_oh", [32, 4095], BF16), ("c_cos", [D, D], BF16),
                        ("c_sin", [D, D], BF16), ("c_id", [128, 128], BF16)]:
        dr[nm] = nc.dram_tensor(nm, shp, dt, kind="ExternalInput")
    WNAMES = [
        ("L0_attn_wg_w", [D, D]), ("L0_attn_wg_b", [D]),
        ("L0_attn_wq_w", [D, D]), ("L0_attn_wq_b", [D]),
        ("L0_attn_wk_w", [D, D]), ("L0_attn_wk_b", [D]),
        ("L0_attn_wv_w", [D, D]), ("L0_attn_wv_b", [D]),
        ("L0_attn_out_w", [D, D]), ("L0_attn_out_b", [D]),
        ("L0_ffn_w_freq_real", [D, D]), ("L0_ffn_w_freq_imag", [D, D]),
        ("L0_ffn_bias", [D]),
        ("L1_attn_Wq", [D, D]), ("L1_attn_bq", [D]),
        ("L1_attn_Wk", [D, D]), ("L1_attn_bk", [D]),
        ("L1_attn_Wv", [D, D]), ("L1_attn_bv", [D]),
        ("L1_attn_Wo", [D, D]), ("L1_attn_bo", [D]),
        ("L1_attn_rel_embed", [32, 16]), ("L1_attn_gru_w", [64, 8]),
        ("L1_attn_gru_b", [8]), ("L1_attn_gru_const", [1, 16, 1, 1]),
        ("L1_ffn_Wr", [D, 8]), ("L1_ffn_Wg", [8, D, 512]),
        ("L1_ffn_Wu", [8, D, 512]), ("L1_ffn_Wd", [8, 512, D]),
        ("L1_ffn_Sg", [D, D]), ("L1_ffn_Su", [D, D]), ("L1_ffn_Sd", [D, D]),
        ("L1_ffn_expert_bias", [8]),
    ]
    for nm, shp in WNAMES:
        dr[nm] = nc.dram_tensor(nm, shp, F32, kind="ExternalInput")
    dr["h_pb"] = nc.dram_tensor("h_pb", [16, 4095], BF16, kind="Internal")
    dr["h_PT"] = nc.dram_tensor("h_PT", [D, D], BF16, kind="Internal")
    dr["h_QT"] = nc.dram_tensor("h_QT", [D, D], BF16, kind="Internal")
    dr["h_M"] = nc.dram_tensor("h_M", [D, D], BF16, kind="Internal")
    dr["h_x0"] = nc.dram_tensor("h_x0", [NT, D], F32, kind="Internal")
    dr["h_x1"] = nc.dram_tensor("h_x1", [NT, D], F32, kind="Internal")
    dr["h_qt"] = nc.dram_tensor("h_qt", [D, NT], BF16, kind="Internal")
    dr["h_avt"] = nc.dram_tensor("h_avt", [D, NT], BF16, kind="Internal")
    dr["h_v"] = nc.dram_tensor("h_v", [NT, H * 65], BF16, kind="Internal")
    dr["h_gr"] = nc.dram_tensor("h_gr", [48, NT], BF16, kind="Internal")

    import contextlib
    ctx = contextlib.ExitStack()

    def sb(name, shape, dt):
        return ctx.enter_context(nc.sbuf_tensor(name, shape, dt))

    def psum(name, shape, dt=F32):
        return ctx.enter_context(nc.psum_tensor(name, shape, dt))

    RES = sb("RES", [128, NTC * D], F32)
    RESB = RES.bitcast(BF16)
    XNB = sb("XNB", [128, 16640], BF16)
    POOL = sb("POOL", [128, 35072], BF16)
    SMALL = sb("SMALL", [128, 3072], F32)
    STAT = sb("STAT", [128, 64], F32)
    RW = sb("RW", [128, 256], F32)
    BROW = sb("BROW", [128, 2048], BF16)
    ONES = sb("ONES", [128, 128], BF16)
    IDBF = sb("IDBF", [128, 128], BF16)
    KVS = sb("KVS", [128, 520], BF16)
    KVACC = sb("KVACC", [128, 1040], F32)
    BIASC = sb("BIASC", [128, 16], F32)
    GW2B = sb("GW2B", [128, 2], BF16)
    GB2R = sb("GB2R", [128, 4], BF16)
    BD = sb("BD", [128, 4], BF16)
    CREP = sb("CREP", [128, 16], F32)
    GATE = sb("GATE", [128, 256], BF16)
    GREP = sb("GREP", [128, 4096], BF16)
    VH = sb("VH", [128, 2080], BF16)
    AVST = sb("AVST", [128, 1024], BF16)
    RECB = sb("RECB", [128, 512], BF16)
    REPS = sb("REPS", [128, 512], BF16)
    GRUW = sb("GRUW", [128, 8], F32)
    GRUB = sb("GRUB", [128, 8], F32)
    C16 = sb("C16", [128, 16], F32)

    P = [psum(f"P{i}", [128, 512]) for i in range(6)]
    T0 = psum("T0", [128, 128], BF16)
    T1 = psum("T1", [128, 128], BF16)
    TPS = [T0, T1]

    S = {n: b.sem(n) for n in ["dw", "da", "pe", "ve", "ac", "gp"]}
    dw, da, pe, ve, ac = S["dw"], S["da"], S["pe"], S["ve"], S["ac"]

    st = dict(nc=nc, b=b, dr=dr, S=S, RES=RES, RESB=RESB, XNB=XNB, POOL=POOL,
              SMALL=SMALL, STAT=STAT, RW=RW, BROW=BROW, ONES=ONES, IDBF=IDBF,
              KVS=KVS, KVACC=KVACC, BIASC=BIASC, GW2B=GW2B, GB2R=GB2R, BD=BD,
              CREP=CREP, GATE=GATE, GREP=GREP, VH=VH, AVST=AVST,
              RECB=RECB, REPS=REPS, GRUW=GRUW, GRUB=GRUB, C16=C16,
              P=P, TPS=TPS)

    if dbg:
        dr["o_xnb"] = nc.dram_tensor("o_xnb", [128, 16384], BF16, kind="ExternalOutput")
        dr["o_kvs"] = nc.dram_tensor("o_kvs", [128, 520], BF16, kind="ExternalOutput")
        dr["o_kvacc"] = nc.dram_tensor("o_kvacc", [128, 1040], F32, kind="ExternalOutput")
        dr["o_rw"] = nc.dram_tensor("o_rw", [128, 256], F32, kind="ExternalOutput")
        dr["o_out"] = nc.dram_tensor("o_out", [NT, D], F32, kind="ExternalOutput")
        if dbg == 2:
            dr["o_vb"] = nc.dram_tensor("o_vb", [128, 16640], BF16, kind="ExternalOutput")
            dr["o_kq"] = nc.dram_tensor("o_kq", [128, 1024], BF16, kind="ExternalOutput")
            dr["o_brow"] = nc.dram_tensor("o_brow", [128, 2048], BF16, kind="ExternalOutput")
            dr["o_w1"] = nc.dram_tensor("o_w1", [128, 8192], BF16, kind="ExternalOutput")
    st["dbg"] = dbg
    st["n_sub"] = n_sub
    st["cut"] = cut

    prog(st, n_sub)

    if dbg:
        vv, aa, pp = b.cnt[S["ve"]], b.cnt[S["ac"]], b.cnt[S["pe"]]
        b.wait("gpsimd", S["ve"], vv)
        b.wait("gpsimd", S["ac"], aa)
        b.wait("gpsimd", S["pe"], pp)
        dd = b.op("gpsimd", "dma_start", dr["o_xnb"].ap(), XNB[:, 0:16384],
                  inc=S["dw"], inc_by=16)
        dd = b.op("gpsimd", "dma_start", dr["o_kvs"].ap(), KVS[:, :], inc=S["dw"], inc_by=16)
        dd = b.op("gpsimd", "dma_start", dr["o_kvacc"].ap(), st["KVACC"][:, :], inc=S["dw"], inc_by=16)
        dd = b.op("gpsimd", "dma_start", dr["o_rw"].ap(), st["RW"][:, :], inc=S["dw"], inc_by=16)
        for i in range(4):
            dd = b.op("gpsimd", "dma_start",
                      bass.AP(dr["o_out"], 4096 * 128 * i if False else 4 * 128 * 1024 * i,
                              [[1024, 128], [128 * 1024, 4], [1, 1024]]),
                      bass.AP(RES, 4096 * i, [[RES.shape[1], 128], [1024, 4], [1, 1024]]),
                      inc=S["dw"], inc_by=16)
        if dbg == 2:
            dd = b.op("gpsimd", "dma_start", dr["o_vb"].ap(),
                      bass.AP(POOL, 0, [[POOL.shape[1], 128], [1, 16640]]),
                      inc=S["dw"], inc_by=16)
            dd = b.op("gpsimd", "dma_start", dr["o_kq"].ap(),
                      bass.AP(POOL, 33024, [[POOL.shape[1], 128], [1, 1024]]),
                      inc=S["dw"], inc_by=16)
            dd = b.op("gpsimd", "dma_start", dr["o_brow"].ap(), st["BROW"][:, :],
                      inc=S["dw"], inc_by=16)
            dd = b.op("gpsimd", "dma_start", dr["o_w1"].ap(),
                      bass.AP(POOL, 16640, [[POOL.shape[1], 128], [1, 8192]]),
                      inc=S["dw"], inc_by=16)
        b.wait("gpsimd", S["dw"], dd)

    b.render()
    ctx.close()
    b.close_sems()
    return nc



def ln_stats(b, S, STAT, src_ap_halves, waits_v=()):
    ve, ac = S["ve"], S["ac"]
    b.op("vector", "bn_stats", STAT[:, 4:10], src_ap_halves[0], waits=waits_v)
    b.op("vector", "bn_stats", STAT[:, 10:16], src_ap_halves[1])
    b.op("vector", "drain")
    b.op("vector", "bn_aggr", STAT[:, 0:2], STAT[:, 4:16])
    b.op("vector", "drain")
    b.op("vector", "tensor_scalar_add", STAT[:, 2:3], STAT[:, 1:2], EPS)
    b.op("vector", "drain")
    b.op("vector", "reciprocal", STAT[:, 2:3], STAT[:, 2:3])
    v1 = b.op("vector", "drain", inc=ve)
    a1 = b.op("scalar", "activation", STAT[:, 2:3], STAT[:, 2:3], AF.Sqrt,
              waits=[(ve, v1)], inc=ac)
    b.op("vector", "scalar_tensor_tensor", STAT[:, 3:4], STAT[:, 0:2][:, 0:1], -1.0,
         STAT[:, 2:3], ALU.mult, ALU.mult, waits=[(ac, a1)])
    v2 = b.op("vector", "drain", inc=ve)
    return v2


def preln_batch(b, st, x_src, tile_ap, xnb_dst_fn):
    S, STAT, RW, XNB = st["S"], st["STAT"], st["RW"], st["XNB"]
    ve, ac, pe = S["ve"], S["ac"], S["pe"]
    for t in range(NTC):
        b.op("vector", "bn_stats", RW[:, 12 * t:12 * t + 6],
             x_src[:, 1024 * t:1024 * t + 512])
        b.op("vector", "bn_stats", RW[:, 12 * t + 6:12 * t + 12],
             x_src[:, 1024 * t + 512:1024 * t + 1024])
    b.op("vector", "drain")
    for t in range(NTC):
        b.op("vector", "bn_aggr", STAT[:, 2 * t:2 * t + 2], RW[:, 12 * t:12 * t + 12])
    b.op("vector", "drain")
    varv = bass.AP(STAT, 1, [[STAT.shape[1], 128], [2, 16]])
    meanv = bass.AP(STAT, 0, [[STAT.shape[1], 128], [2, 16]])
    b.op("vector", "tensor_scalar_add", STAT[:, 32:48], varv, EPS)
    b.op("vector", "drain")
    b.op("vector", "reciprocal", STAT[:, 32:48], STAT[:, 32:48])
    v1 = b.op("vector", "drain", inc=ve)
    a1 = b.op("scalar", "activation", STAT[:, 32:48], STAT[:, 32:48], AF.Sqrt,
              waits=[(ve, v1)], inc=ac)
    b.op("vector", "scalar_tensor_tensor", STAT[:, 48:64], meanv, -1.0,
         STAT[:, 32:48], ALU.mult, ALU.mult, waits=[(ac, a1)])
    b.op("vector", "drain")
    for t in range(NTC):
        v3 = b.op("vector", "tensor_scalar", tile_ap[:, 0:1024],
                  x_src[:, 1024 * t:1024 * t + 1024],
                  STAT[:, 32 + t:33 + t], STAT[:, 48 + t:49 + t], ALU.mult, ALU.add,
                  waits=[(pe, st.get("_tp_pe", 0))], inc=ve)
        for f in range(NFC):
            wl = [(ve, v3)]
            if f >= 2:
                wl.append((ac, st["_tp_ac"][f % 2]))
            a = transpose_tile(b, st, tile_ap[:, 128 * f:128 * f + 128],
                               xnb_dst_fn(t, f), f, waits_t=wl)
            st["_tp_ac"][f % 2] = a


def transpose_tile(b, st, src_bf16_ap, dst_bf16_ap, tp_idx, waits_t=(), m=128):
    S, TPS, IDBF = st["S"], st["TPS"], st["IDBF"]
    pe, ac = S["pe"], S["ac"]
    tb = TPS[tp_idx % 2]
    p1 = b.op("tensor", "transpose", tb[0:m, 0:128], src_bf16_ap, IDBF[:, :],
              waits=waits_t, inc=pe)
    a1 = b.op("scalar", "copy", dst_bf16_ap, tb[0:m, 0:128], waits=[(pe, p1)], inc=ac)
    st["_tp_pe"] = p1
    return a1


def preln_chunk(b, st, t, x_src, xn_dst_tile, emit_transposes=True):
    S, STAT, XNB = st["S"], st["STAT"], st["XNB"]
    ve, ac = S["ve"], S["ac"]
    v2 = ln_stats(b, S, STAT,
                  (x_src[:, 1024 * t:1024 * t + 512], x_src[:, 1024 * t + 512:1024 * t + 1024]))
    v3 = b.op("vector", "tensor_scalar", xn_dst_tile[:, 0:1024],
              x_src[:, 1024 * t:1024 * t + 1024],
              STAT[:, 2:3], STAT[:, 3:4], ALU.mult, ALU.add,
              waits=[(S["pe"], st.get("_tp_pe", 0))], inc=ve)
    if emit_transposes:
        last = None
        for f in range(NFC):
            waits_t = [(ve, v3)]
            if f >= 2:
                waits_t.append((S["ac"], st["_tp_ac"][f % 2]))
            a = transpose_tile(b, st, xn_dst_tile[:, 128 * f:128 * f + 128],
                               XNB[:, 2048 * f + 128 * t:2048 * f + 128 * t + 128],
                               f, waits_t=waits_t)
            st["_tp_ac"][f % 2] = a
    return v3


def postln_residual_chunk(b, st, t, psum_pair, waits_v=()):
    S, STAT, SMALL, RES = st["S"], st["STAT"], st["SMALL"], st["RES"]
    ve = S["ve"]
    Pa, Pb = psum_pair
    v2 = ln_stats(b, S, STAT, (Pa[:, 0:512], Pb[:, 0:512]), waits_v=waits_v)
    b.op("vector", "tensor_scalar", SMALL[:, 0:512], Pa[:, 0:512],
         STAT[:, 2:3], STAT[:, 3:4], ALU.mult, ALU.add)
    v3 = b.op("vector", "tensor_scalar", SMALL[:, 512:1024], Pb[:, 0:512],
              STAT[:, 2:3], STAT[:, 3:4], ALU.mult, ALU.add, inc=ve)
    v4 = b.op("vector", "tensor_tensor", RES[:, 1024 * t:1024 * t + 1024],
              RES[:, 1024 * t:1024 * t + 1024], SMALL[:, 0:1024], ALU.add, inc=ve)
    return v3, v4



def prog(st, n_sub):
    b, S, dr = st["b"], st["S"], st["dr"]
    nc = st["nc"]
    P, TPS = st["P"], st["TPS"]
    RES, RESB, XNB, POOL, SMALL, STAT = (st["RES"], st["RESB"], st["XNB"],
                                         st["POOL"], st["SMALL"], st["STAT"])
    ONES, IDBF, BROW = st["ONES"], st["IDBF"], st["BROW"]
    dw, da, pe, ve, ac = S["dw"], S["da"], S["pe"], S["ve"], S["ac"]
    st["_tp_ac"] = [0, 0]

    WA, WB, WC = 0, 8192, 16384
    REST = 24576

    xin = 0
    for i in range(4):
        dst = bass.AP(RES, 4096 * i, [[RES.shape[1], 128], [1024, 4], [1, 1024]])
        src = bass.AP(dr["x"], 4 * 128 * 1024 * i, [[1024, 128], [128 * 1024, 4], [1, 1024]])
        xin = b.op("sync", "dma_start", dst, src, inc=da, inc_by=16)
    b.op("sync", "dma_start", IDBF[:, :], dr["c_id"].ap(), inc=da, inc_by=16)
    cosin = 0
    for i in range(2):
        dst = bass.AP(POOL, WA + 4096 * i, [[POOL.shape[1], 128], [1024, 4], [1, 1024]])
        src = bass.AP(dr["c_cos"], 4 * 128 * 1024 * i, [[1024, 128], [128 * 1024, 4], [1, 1024]])
        b.op("sync", "dma_start", dst, src, inc=da, inc_by=16)
        dst = bass.AP(POOL, WB + 4096 * i, [[POOL.shape[1], 128], [1024, 4], [1, 1024]])
        src = bass.AP(dr["c_sin"], 4 * 128 * 1024 * i, [[1024, 128], [128 * 1024, 4], [1, 1024]])
        cosin = b.op("sync", "dma_start", dst, src, inc=da, inc_by=16)
    ohc = b.op("sync", "dma_start",
               bass.AP(POOL, REST, [[POOL.shape[1], 32], [1, 4095]]),
               dr["c_oh"].ap(), inc=da, inc_by=16)
    relc = b.op("gpsimd", "dma_start",
                bass.AP(POOL, REST + 4096, [[POOL.shape[1], 32], [1, 16]]),
                dr["L1_attn_rel_embed"].ap(), inc=dw, inc_by=16)
    w_wr = 0
    w_wr += w_chunk_dma(b, (POOL, WC), dr["L0_ffn_w_freq_real"], D, D, dw)
    w_wi_base = REST + 8192
    w_wr += w_chunk_dma(b, (XNB, 0), dr["L0_ffn_w_freq_imag"], D, D, dw)
    wr_done = b.cnt[dw]
    vones = b.op("vector", "memset", ONES[:, :], 1.0, inc=ve)

    PBB0 = REST + 5120
    pbecnt = []
    for i in range(8):
        nwide = 512 if i < 7 else 4095 - 7 * 512
        bank = P[i % 2]
        waits = [(dw, relc), (da, ohc)]
        if i >= 2:
            waits.append((ac, pbecnt[i - 2]))
        pcur = b.op("tensor", "matmul", bank[0:16, 0:nwide],
                    bass.AP(POOL, REST + 4096, [[POOL.shape[1], 32], [1, 16]]),
                    bass.AP(POOL, REST + 512 * i, [[POOL.shape[1], 32], [1, nwide]]),
                    start=True, stop=True, waits=waits, inc=pe)
        a = b.op("scalar", "copy",
                 bass.AP(POOL, PBB0 + 512 * i, [[POOL.shape[1], 16], [1, nwide]]),
                 bank[0:16, 0:nwide], waits=[(pe, pcur)], inc=ac)
        pbecnt.append(a)
    pbh = b.op("sync", "dma_start", dr["h_pb"].ap(),
               bass.AP(POOL, PBB0, [[POOL.shape[1], 16], [1, 4095]]),
               waits=[(ac, pbecnt[-1])], inc=da, inc_by=16)
    st["pbh"] = pbh

    g1 = b.op("gpsimd", "dma_start", st["GRUW"][0:64, 0:8], dr["L1_attn_gru_w"].ap(),
              inc=dw, inc_by=16)
    g2 = b.op("gpsimd", "dma_start", st["GRUW"][64:128, 0:8], dr["L1_attn_gru_w"].ap(),
              inc=dw, inc_by=16)
    g3 = b.op("gpsimd", "dma_start", st["GRUB"][0:1, 0:8],
              bass.AP(dr["L1_attn_gru_b"], 0, [[8, 1], [1, 8]]), inc=dw, inc_by=16)
    g4 = b.op("gpsimd", "dma_start", st["C16"][0:1, 0:16],
              bass.AP(dr["L1_attn_gru_const"], 0, [[16, 1], [1, 16]]), inc=dw, inc_by=16)
    vg = b.op("vector", "tensor_reduce", st["GW2B"][:, 0:2],
              bass.AP(st["GRUW"], 0, [[st["GRUW"].shape[1], 128], [4, 2], [1, 4]]),
              axis=AX.X, op=ALU.add, waits=[(dw, g2)], lp="pair sums to bf16")
    b.op("vector", "drain")
    vg2 = b.op("vector", "tensor_reduce", st["STAT"][0:1, 32:34],
               bass.AP(st["GRUB"], 0, [[GRUBROW := st["GRUB"].shape[1], 1], [4, 2], [1, 4]]),
               axis=AX.X, op=ALU.add, waits=[(dw, g3)])
    vg3 = b.op("vector", "drain", inc=ve)
    a_g1 = b.op("scalar", "copy", st["GB2R"][0:1, 0:2], st["STAT"][0:1, 32:34],
                waits=[(ve, vg3)], lp="gb2 bf16")
    a_g2 = b.op("scalar", "copy", st["GB2R"][0:1, 2:4], st["STAT"][0:1, 32:34],
                lp="gb2 bf16")
    vb0 = b.op("vector", "memset", st["BD"][:, :], 0.0, inc=ve)
    b.op("vector", "drain")
    a_bd1 = b.op("scalar", "copy", st["BD"][0:64, 0:2], st["GW2B"][0:64, 0:2],
                 waits=[(ve, b.cnt[ve])])
    a_bd = b.op("scalar", "copy", st["BD"][64:128, 2:4], st["GW2B"][64:128, 0:2], inc=ac)
    st["a_bd"] = a_bd
    a_c16 = b.op("scalar", "copy", st["GREP"][0:1, 0:16], st["C16"][0:1, 0:16],
                 waits=[(dw, g4)], inc=ac, lp="c16 bf16")
    pc = b.op("tensor", "matmul", P[2][0:128, 0:16], ONES[0:1, 0:128],
              st["GREP"][0:1, 0:16], start=True, stop=True,
              waits=[(ac, a_c16), (ve, vones)], inc=pe)
    a_crep = b.op("scalar", "copy", st["CREP"][:, 0:16], P[2][0:128, 0:16],
                  waits=[(pe, pc)], inc=ac)
    st["a_crep"] = a_crep

    MST = REST + 4096
    def mb_stage(src_banks, dst_hbm, m, wait_extra=()):
        a1 = b.op("scalar", "copy",
                  bass.AP(POOL, REST, [[POOL.shape[1], 128], [1, 512]]),
                  src_banks[0][:, 0:512], waits=list(wait_extra), inc=ac)
        a2 = b.op("scalar", "copy",
                  bass.AP(POOL, REST + 512, [[POOL.shape[1], 128], [1, 512]]),
                  src_banks[1][:, 0:512], inc=ac)
        d = b.op("sync", "dma_start", dst_hbm[128 * m:128 * m + 128, :],
                 bass.AP(POOL, REST, [[POOL.shape[1], 128], [1, 1024]]),
                 waits=[(ac, a2)], inc=da, inc_by=16)
        return a2, d

    if n_sub >= 2:
        last_stage_d = 0
        pt_evac = {}
        for m in range(8):
            for half in range(2):
                bank = P[half]
                for f in range(8):
                    wl = []
                    if f == 0 and half == 0:
                        wl = [(dw, wr_done), (da, cosin)]
                        if m == 0:
                            wl.append((ac, pbecnt[-1]))
                        if m >= 1:
                            wl.append((ac, pt_evac[m - 1]))
                    b.op("tensor", "matmul", bank[:, 0:512],
                         bass.AP(POOL, WC + 1024 * f + 128 * m,
                                 [[POOL.shape[1], 128], [1, 128]]),
                         bass.AP(POOL, WA + 1024 * f + 512 * half,
                                 [[POOL.shape[1], 128], [1, 512]]),
                         start=(f == 0), stop=False, waits=wl)
                    pcur = b.op("tensor", "matmul", bank[:, 0:512],
                                bass.AP(XNB, 1024 * f + 128 * m,
                                        [[XNB.shape[1], 128], [1, 128]]),
                                bass.AP(POOL, WB + 1024 * f + 512 * half,
                                        [[POOL.shape[1], 128], [1, 512]]),
                                start=False, stop=(f == 7), inc=(pe if f == 7 else None))
                b.wait("scalar", pe, pcur)
            a2, last_stage_d = mb_stage([P[0], P[1]], dr["h_PT"], m,
                                        wait_extra=([(da, last_stage_d)] if m >= 1 else []))
            pt_evac[m] = a2
        qt_v = {}
        for m in range(8):
            for half in range(2):
                for f in range(8):
                    wl = []
                    if f == 0 and half == 0 and m >= 1:
                        wl.append((ve, qt_v[m - 1]))
                    b.op("tensor", "matmul", P[2 + half][:, 0:512],
                         bass.AP(XNB, 1024 * f + 128 * m, [[XNB.shape[1], 128], [1, 128]]),
                         bass.AP(POOL, WA + 1024 * f + 512 * half,
                                 [[POOL.shape[1], 128], [1, 512]]),
                         start=(f == 0), stop=(f == 7), waits=wl)
                    pcur = b.op("tensor", "matmul", P[4 + half][:, 0:512],
                                bass.AP(POOL, WC + 1024 * f + 128 * m,
                                        [[POOL.shape[1], 128], [1, 128]]),
                                bass.AP(POOL, WB + 1024 * f + 512 * half,
                                        [[POOL.shape[1], 128], [1, 512]]),
                                start=(f == 0), stop=(f == 7),
                                inc=(pe if f == 7 else None))
            b.wait("vector", pe, pcur)
            b.wait("vector", da, last_stage_d)
            a1 = b.op("scalar", "copy", SMALL[:, 0:512], P[2][:, 0:512],
                      waits=[(pe, pcur)], inc=ac)
            a2 = b.op("scalar", "copy", SMALL[:, 512:1024], P[3][:, 0:512], inc=ac)
            b.wait("vector", ac, a2)
            b.op("vector", "scalar_tensor_tensor",
                 bass.AP(POOL, REST, [[POOL.shape[1], 128], [1, 512]]),
                 P[4][:, 0:512], -1.0, SMALL[:, 0:512], ALU.mult, ALU.add)
            v = b.op("vector", "scalar_tensor_tensor",
                     bass.AP(POOL, REST + 512, [[POOL.shape[1], 128], [1, 512]]),
                     P[5][:, 0:512], -1.0, SMALL[:, 512:1024], ALU.mult, ALU.add, inc=ve)
            qt_v[m] = v
            last_stage_d = b.op("sync", "dma_start", dr["h_QT"][128 * m:128 * m + 128, :],
                                bass.AP(POOL, REST, [[POOL.shape[1], 128], [1, 1024]]),
                                waits=[(ve, v)], inc=da, inc_by=16)
        for m in range(8):
            dl = []
            for which, hbm in ((0, dr["h_PT"]), (1, dr["h_QT"])):
                for g in range(2):
                    dd = b.op("sync", "dma_start",
                              bass.AP(POOL, REST + 1024 + 1024 * which + 512 * g,
                                      [[POOL.shape[1], 128], [128, 4], [1, 128]]),
                              bass.AP(hbm, 128 * m + 4 * g * 128 * 1024,
                                      [[1024, 128], [128 * 1024, 4], [1, 128]]),
                              waits=[(da, last_stage_d)] if (which == 0 and g == 0) else [],
                              inc=da, inc_by=16)
                    dl.append(dd)
            for half in range(2):
                for f in range(8):
                    wl = [(da, dl[-1])] if f == 0 and half == 0 else []
                    b.op("tensor", "matmul", P[0 + half][:, 0:512],
                         bass.AP(POOL, REST + 1024 + 128 * f,
                                 [[POOL.shape[1], 128], [1, 128]]),
                         bass.AP(POOL, WA + 1024 * f + 512 * half,
                                 [[POOL.shape[1], 128], [1, 512]]),
                         start=(f == 0), stop=(f == 7), waits=wl)
                    pcur = b.op("tensor", "matmul", P[2 + half][:, 0:512],
                                bass.AP(POOL, REST + 2048 + 128 * f,
                                        [[POOL.shape[1], 128], [1, 128]]),
                                bass.AP(POOL, WB + 1024 * f + 512 * half,
                                        [[POOL.shape[1], 128], [1, 512]]),
                                start=(f == 0), stop=(f == 7),
                                inc=(pe if f == 7 else None))
            b.wait("scalar", pe, pcur)
            a1 = b.op("scalar", "activation", SMALL[:, 0:512], P[0][:, 0:512],
                      AF.Copy, bias=0.0, scale=1.0 / D, waits=[(pe, pcur)], inc=ac)
            a2 = b.op("scalar", "activation", SMALL[:, 512:1024], P[1][:, 0:512],
                      AF.Copy, bias=0.0, scale=1.0 / D, inc=ac)
            b.wait("vector", ac, a2)
            b.op("vector", "scalar_tensor_tensor",
                 bass.AP(POOL, REST, [[POOL.shape[1], 128], [1, 512]]),
                 P[2][:, 0:512], -1.0 / D, SMALL[:, 0:512], ALU.mult, ALU.add)
            v = b.op("vector", "scalar_tensor_tensor",
                     bass.AP(POOL, REST + 512, [[POOL.shape[1], 128], [1, 512]]),
                     P[3][:, 0:512], -1.0 / D, SMALL[:, 512:1024], ALU.mult, ALU.add,
                     inc=ve)
            last_stage_d = b.op("sync", "dma_start", dr["h_M"][128 * m:128 * m + 128, :],
                                bass.AP(POOL, REST, [[POOL.shape[1], 128], [1, 1024]]),
                                waits=[(ve, v)], inc=da, inc_by=16)
        st["m_done_da"] = last_stage_d
    else:
        st["m_done_da"] = 0
    mb_pe = b.cnt[pe]

    b.wait("vector", da, xin)
    if n_sub >= 2:
        b.wait("vector", pe, mb_pe)
        b.wait("scalar", pe, mb_pe)
        b.wait("gpsimd", pe, mb_pe)
    XT1 = 34048
    preln_batch(b, st, RES, bass.AP(POOL, XT1, [[POOL.shape[1], 128], [1, 1024]]),
                lambda t, f: XNB[:, 2048 * f + 128 * t:2048 * f + 128 * t + 128])
    st["preln0_ac"] = b.cnt[ac]

    l0_attn(st, WA, WB, WC, REST)
    if n_sub >= 2:
        l0_ffn(st, WA, WB, WC, REST)
    if n_sub >= 3:
        l1_attn(st, WA, WB, WC, REST)
    if n_sub >= 4:
        l1_ffn(st, WA, WB, WC, REST)

    if n_sub >= 3:
        pass
    else:
        vv = b.cnt[ve]
        dy = 0
        for i in range(4):
            src = bass.AP(RES, 4096 * i, [[RES.shape[1], 128], [1024, 4], [1, 1024]])
            dst = bass.AP(dr["y"], 4 * 128 * 1024 * i, [[1024, 128], [128 * 1024, 4], [1, 1024]])
            dy = b.op("sync", "dma_start", dst, src, waits=[(ve, vv)] if i == 0 else [],
                      inc=da, inc_by=16)
        b.wait("sync", da, dy)



def l0_attn(st, WA, WB, WC, REST):
    b, S, dr, P = st["b"], st["S"], st["dr"], st["P"]
    POOL, XNB, RES, SMALL, STAT = st["POOL"], st["XNB"], st["RES"], st["SMALL"], st["STAT"]
    ONES, BROW, KVS, KVACC = st["ONES"], st["BROW"], st["KVS"], st["KVACC"]
    dw, da, pe, ve, ac = S["dw"], S["da"], S["pe"], S["ve"], S["ac"]
    nc = st["nc"]

    VB = REST
    TIL = REST + 16640
    T_KQ = TIL
    T_QT = TIL + 1024
    T_SO = TIL + 2048
    T_G = TIL + 3072
    T_GS = TIL + 4096
    T_GT = TIL + 5120
    VB = 0
    W1 = 16640
    W2 = 24832
    TIL = 33024
    T_KQ = TIL
    T_QT = TIL + 1024
    T_G = VB
    T_GS = VB + 1024
    T_GT = VB + 2048
    T_SO = VB + 3072

    def pool_ap(col, width, parts=128, p0=0):
        return bass.AP(POOL, p0 * POOL.shape[1] + col,
                       [[POOL.shape[1], parts], [1, width]])

    b.wait("vector", da, st["pbh"])
    b.wait("scalar", da, st["pbh"])
    b.wait("gpsimd", da, st["pbh"])
    b.wait("gpsimd", da, st["m_done_da"])

    wk_d = w_chunk_dma(b, (POOL, W1), dr["L0_attn_wk_w"], D, D, dw)
    wk_done = b.cnt[dw]
    bk_d = b.op("gpsimd", "dma_start", BROW[0:1, 0:1024],
                bass.AP(dr["L0_attn_wk_b"], 0, [[D, 1], [1, D]]), inc=dw, inc_by=16)
    w_chunk_dma(b, (POOL, W2), dr["L0_attn_wv_w"], D, D, dw)
    wv_done = b.cnt[dw]
    bv_d = b.op("gpsimd", "dma_start", BROW[0:1, 1024:2048],
                bass.AP(dr["L0_attn_wv_b"], 0, [[D, 1], [1, D]]), inc=dw, inc_by=16)

    vvb = b.op("vector", "memset",
               bass.AP(POOL, VB + 64, [[POOL.shape[1], 128], [1040, 16], [65, 16], [1, 1]]),
               1.0, inc=ve)

    kvacc_init = [False] * 8
    evac_k_ve = 0
    evac_v_ac = 0
    kv_dve = {}
    for t in range(NTC):
        for half in range(2):
            for f in range(NFC):
                wl = []
                if t == 0 and half == 0 and f == 0:
                    wl = [(dw, wk_done + 16), (ac, st["preln0_ac"])]
                if t > 0 and half == 0 and f == 0:
                    wl = [(ve, kv_dve.get(("kdone", t - 1), 0))]
                b.op("tensor", "matmul", P[half][:, 0:512],
                     bass.AP(XNB, 2048 * f + 128 * t, [[XNB.shape[1], 128], [1, 128]]),
                     bass.AP(POOL, W1 + 1024 * f + 512 * half,
                             [[POOL.shape[1], 128], [1, 512]]),
                     start=(f == 0), stop=False, waits=wl)
            pk = b.op("tensor", "matmul", P[half][:, 0:512], ONES[0:1, 0:128],
                      BROW[0:1, 512 * half:512 * half + 512],
                      start=False, stop=True, inc=pe)
        b.wait("vector", pe, pk)
        b.op("vector", "tensor_scalar_min", SMALL[:, 0:512], P[0][:, 0:512], 0.0)
        v1 = b.op("vector", "tensor_scalar_min", SMALL[:, 512:1024], P[1][:, 0:512],
                  0.0, inc=ve)
        a1 = b.op("scalar", "activation", SMALL[:, 1024:1536], SMALL[:, 0:512],
                  AF.Exp, waits=[(ve, v1)])
        a2 = b.op("scalar", "activation", SMALL[:, 1536:2048], SMALL[:, 512:1024],
                  AF.Exp, inc=ac)
        b.wait("vector", ac, a2)
        b.op("vector", "scalar_tensor_tensor", pool_ap(T_KQ, 512),
             P[0][:, 0:512], 0.0, SMALL[:, 1024:1536], ALU.max, ALU.add,
             lp="bf16 matmul operand")
        v2 = b.op("vector", "scalar_tensor_tensor", pool_ap(T_KQ + 512, 512),
                  P[1][:, 0:512], 0.0, SMALL[:, 1536:2048], ALU.max, ALU.add,
                  inc=ve, lp="bf16 matmul operand")
        kv_dve[("kphi", t)] = v2
        kv_dve[("kdone", t)] = v2
        for half in range(2):
            for f in range(NFC):
                wl = []
                if t == 0 and half == 0 and f == 0:
                    wl = [(dw, wv_done + 16)]
                if t > 0 and half == 0 and f == 0:
                    wl = [(ac, kv_dve.get(("vdone", t - 1), 0))]
                b.op("tensor", "matmul", P[2 + half][:, 0:512],
                     bass.AP(XNB, 2048 * f + 128 * t, [[XNB.shape[1], 128], [1, 128]]),
                     bass.AP(POOL, W2 + 1024 * f + 512 * half,
                             [[POOL.shape[1], 128], [1, 512]]),
                     start=(f == 0), stop=False, waits=wl)
            pv = b.op("tensor", "matmul", P[2 + half][:, 0:512], ONES[0:1, 0:128],
                      BROW[0:1, 1024 + 512 * half:1536 + 512 * half],
                      start=False, stop=True, inc=pe)
        wl = [(pe, pv)]
        if t == 0:
            wl.append((ve, vvb))
        b.op("scalar", "copy",
             bass.AP(POOL, VB + 1040 * t, [[POOL.shape[1], 128], [65, 8], [1, 64]]),
             P[2][:, 0:512], waits=wl, lp="bf16 v")
        av = b.op("scalar", "copy",
                  bass.AP(POOL, VB + 1040 * t + 520, [[POOL.shape[1], 128], [65, 8], [1, 64]]),
                  P[3][:, 0:512], inc=ac, lp="bf16 v")
        kv_dve[("vdone", t)] = av
        for j in range(8):
            bank = P[4 + (j % 2)]
            wl = [(ve, v2), (ac, av)]
            if (t * 8 + j) >= 2:
                wl.append((ve, kv_dve[("kvadd", t * 8 + j - 2)]))
            pj = b.op("tensor", "matmul", bank[:, 0:130],
                      pool_ap(T_KQ + 128 * j, 128),
                      bass.AP(POOL, VB + 1040 * t + 130 * j,
                              [[POOL.shape[1], 128], [1, 130]]),
                      start=True, stop=True, waits=wl, inc=pe)
            if not kvacc_init[j]:
                vv = b.op("vector", "tensor_copy", KVACC[:, 130 * j:130 * j + 130],
                          bank[:, 0:130], waits=[(pe, pj)], inc=ve)
                kvacc_init[j] = True
            else:
                vv = b.op("vector", "tensor_tensor", KVACC[:, 130 * j:130 * j + 130],
                          KVACC[:, 130 * j:130 * j + 130], bank[:, 0:130], ALU.add,
                          waits=[(pe, pj)], inc=ve)
            kv_dve[("kvadd", t * 8 + j)] = vv
    vlast = kv_dve[("kvadd", NTC * 8 - 1)]
    b.wait("scalar", ve, vlast)
    for j in range(8):
        b.op("scalar", "copy", KVS[0:64, 65 * j:65 * j + 65],
             KVACC[0:64, 130 * j:130 * j + 65], lp="bf16 kv")
        akv = b.op("scalar", "copy", KVS[64:128, 65 * j:65 * j + 65],
                   KVACC[64:128, 130 * j + 65:130 * j + 130], inc=ac, lp="bf16 kv")

    if st.get("dbg") == 2:
        st["l0a_done_ve"] = b.cnt[ve]
        return
    last_pe_pass1 = b.cnt[pe]
    b.wait("gpsimd", pe, last_pe_pass1)
    w_chunk_dma(b, (POOL, W1), dr["L0_attn_wq_w"], D, D, dw)
    wq_done = b.cnt[dw]
    b.op("gpsimd", "dma_start", BROW[0:1, 0:1024],
         bass.AP(dr["L0_attn_wq_b"], 0, [[D, 1], [1, D]]), inc=dw, inc_by=16)
    w_chunk_dma(b, (POOL, W2), dr["L0_attn_wg_w"], D, D, dw)
    wg_done = b.cnt[dw]
    b.op("gpsimd", "dma_start", BROW[0:1, 1024:2048],
         bass.AP(dr["L0_attn_wg_b"], 0, [[D, 1], [1, D]]), inc=dw, inc_by=16)
    OW = VB + 4096
    b.wait("gpsimd", ac, akv)
    w_chunk_dma(b, (POOL, OW), dr["L0_attn_out_w"], D, D, dw)
    ow_done = b.cnt[dw]
    ob_done = b.op("gpsimd", "dma_start", st["GREP"][0:1, 2048:3072],
                   bass.AP(dr["L0_attn_out_b"], 0, [[D, 1], [1, D]]), inc=dw, inc_by=16)

    p2state = {}
    for t in range(NTC):
        for half in range(2):
            for f in range(NFC):
                wl = []
                if t == 0 and half == 0 and f == 0:
                    wl = [(dw, wq_done + 16), (ve, vlast)]
                if t > 0 and half == 0 and f == 0:
                    wl = [(ac, p2state.get(("qT", t - 1), 0))]
                b.op("tensor", "matmul", P[half][:, 0:512],
                     bass.AP(XNB, 2048 * f + 128 * t, [[XNB.shape[1], 128], [1, 128]]),
                     bass.AP(POOL, W1 + 1024 * f + 512 * half,
                             [[POOL.shape[1], 128], [1, 512]]),
                     start=(f == 0), stop=False, waits=wl)
            pq = b.op("tensor", "matmul", P[half][:, 0:512], ONES[0:1, 0:128],
                      BROW[0:1, 512 * half:512 * half + 512],
                      start=False, stop=True, inc=pe)
        b.wait("vector", pe, pq)
        b.op("vector", "tensor_scalar_min", SMALL[:, 0:512], P[0][:, 0:512], 0.0)
        v1 = b.op("vector", "tensor_scalar_min", SMALL[:, 512:1024], P[1][:, 0:512],
                  0.0, inc=ve)
        a1 = b.op("scalar", "activation", SMALL[:, 1024:1536], SMALL[:, 0:512],
                  AF.Exp, waits=[(ve, v1)])
        a2 = b.op("scalar", "activation", SMALL[:, 1536:2048], SMALL[:, 512:1024],
                  AF.Exp, inc=ac)
        b.wait("vector", ac, a2)
        b.op("vector", "scalar_tensor_tensor", pool_ap(T_KQ, 512),
             P[0][:, 0:512], 0.0, SMALL[:, 1024:1536], ALU.max, ALU.add,
             lp="bf16 matmul operand")
        v2 = b.op("vector", "scalar_tensor_tensor", pool_ap(T_KQ + 512, 512),
                  P[1][:, 0:512], 0.0, SMALL[:, 1536:2048], ALU.max, ALU.add,
                  inc=ve, lp="bf16 matmul operand")
        for f in range(NFC):
            wl = [(ve, v2)]
            if f >= 2:
                wl.append((ac, st["_tp_ac"][f % 2]))
            a = transpose_tile(b, st, pool_ap(T_KQ + 128 * f, 128),
                               pool_ap(T_QT + 128 * f, 128), f, waits_t=wl)
            st["_tp_ac"][f % 2] = a
        aqT = st["_tp_ac"][(NFC - 1) % 2]
        aqT2 = st["_tp_ac"][(NFC - 2) % 2]
        p2state[("qT", t)] = max(aqT, aqT2)
        for half in range(2):
            for f in range(NFC):
                wl = []
                if t == 0 and half == 0 and f == 0:
                    wl = [(dw, wg_done + 16)]
                if t > 0 and half == 0 and f == 0:
                    wl = [(ve, p2state.get(("gs", t - 1), 0))]
                b.op("tensor", "matmul", P[2 + half][:, 0:512],
                     bass.AP(XNB, 2048 * f + 128 * t, [[XNB.shape[1], 128], [1, 128]]),
                     bass.AP(POOL, W2 + 1024 * f + 512 * half,
                             [[POOL.shape[1], 128], [1, 512]]),
                     start=(f == 0), stop=False, waits=wl)
            pg = b.op("tensor", "matmul", P[2 + half][:, 0:512], ONES[0:1, 0:128],
                      BROW[0:1, 1024 + 512 * half:1536 + 512 * half],
                      start=False, stop=True, inc=pe)
        b.wait("scalar", pe, pg)
        b.op("scalar", "activation", pool_ap(T_G, 512), P[2][:, 0:512], AF.Sigmoid,
             lp="bf16 gate")
        ag = b.op("scalar", "activation", pool_ap(T_G + 512, 512), P[3][:, 0:512],
                  AF.Sigmoid, inc=ac, lp="bf16 gate")
        sc_evac = {}
        for h in range(16):
            bank = P[2 + (h % 2)]
            p0 = 64 * (h % 2)
            hh = h // 2
            wl = [(ac, ag)]
            if h >= 2:
                wl.append((ac, sc_evac[h - 2]))
            psc = b.op("tensor", "matmul", bank[:, 0:65],
                       pool_ap(T_QT + 128 * hh, 128, parts=64, p0=p0),
                       bass.AP(KVS, p0 * KVS.shape[1] + 65 * hh,
                               [[KVS.shape[1], 64], [1, 65]]),
                       start=True, stop=True, waits=wl, inc=pe)
            sc_evac[h] = b.op("scalar", "copy", SMALL[:, 65 * h:65 * h + 65],
                              bank[:, 0:65], waits=[(pe, psc)], inc=ac)
        b.wait("vector", ac, sc_evac[15])
        densv = bass.AP(SMALL, 64, [[SMALL.shape[1], 128], [65, 16]])
        b.op("vector", "tensor_scalar_add", STAT[:, 20:36], densv, 1e-6)
        b.op("vector", "drain")
        b.op("vector", "reciprocal", STAT[:, 20:36], STAT[:, 20:36])
        b.op("vector", "drain")
        for h in range(16):
            vs = b.op("vector", "tensor_scalar", pool_ap(T_SO + 64 * h, 64),
                      SMALL[:, 65 * h:65 * h + 64], STAT[:, 20 + h:21 + h], None,
                      ALU.mult, inc=(ve if h == 15 else None), lp="bf16 so")
        vgs = b.op("vector", "tensor_tensor", pool_ap(T_GS, 1024), pool_ap(T_G, 1024),
                   pool_ap(T_SO, 1024), ALU.mult, inc=ve, lp="bf16 gs")
        p2state[("gs", t)] = vgs
        for f in range(NFC):
            wl = [(ve, vgs)]
            if f >= 2:
                wl.append((ac, st["_tp_ac"][f % 2]))
            a = transpose_tile(b, st, pool_ap(T_GS + 128 * f, 128),
                               pool_ap(T_GT + 128 * f, 128), f, waits_t=wl)
            st["_tp_ac"][f % 2] = a
        agT = max(st["_tp_ac"][0], st["_tp_ac"][1])
        for half in range(2):
            for f in range(NFC):
                wl = [(ac, agT), (dw, ow_done + 16)] if f == 0 else []
                if f == 0 and t > 0:
                    wl.append((ve, p2state.get(("postv3", t - 1), 0)))
                b.op("tensor", "matmul", P[4 + half][:, 0:512],
                     pool_ap(T_GT + 128 * f, 128),
                     bass.AP(POOL, OW + 1024 * f + 512 * half,
                             [[POOL.shape[1], 128], [1, 512]]),
                     start=(f == 0), stop=False, waits=wl)
            po = b.op("tensor", "matmul", P[4 + half][:, 0:512], ONES[0:1, 0:128],
                      st["GREP"][0:1, 2048 + 512 * half:2560 + 512 * half],
                      start=False, stop=True, inc=pe)
        b.wait("vector", pe, po)
        v3, v4 = postln_residual_chunk(b, st, t, (P[4], P[5]))
        p2state[("postv3", t)] = v3
        p2state[("postln", t)] = v4
    st["l0a_done_ve"] = b.cnt[ve]



def l0_ffn(st, WA, WB, WC, REST):
    b, S, dr, P = st["b"], st["S"], st["dr"], st["P"]
    POOL, XNB, RES, SMALL = st["POOL"], st["XNB"], st["RES"], st["SMALL"]
    ONES, BROW = st["ONES"], st["BROW"]
    dw, da, pe, ve, ac = S["dw"], S["da"], S["pe"], S["ve"], S["ac"]

    b.wait("sync", ve, st["l0a_done_ve"])
    b.wait("sync", da, st["m_done_da"])
    md = 0
    for i in range(2):
        dst = bass.AP(POOL, 4096 * i, [[POOL.shape[1], 128], [1024, 4], [1, 1024]])
        src = bass.AP(dr["h_M"], 4 * 128 * 1024 * i, [[1024, 128], [128 * 1024, 4], [1, 1024]])
        md = b.op("sync", "dma_start", dst, src, inc=da, inc_by=16)
    fb = b.op("gpsimd", "dma_start", BROW[0:1, 0:1024],
              bass.AP(dr["L0_ffn_bias"], 0, [[D, 1], [1, D]]),
              waits=[(ve, st["l0a_done_ve"])], inc=dw, inc_by=16)

    XT1 = 8192
    post = {}
    for t in range(NTC):
        preln_chunk(b, st, t, RES, bass.AP(POOL, XT1, [[POOL.shape[1], 128], [1, 1024]]))
        pair = (P[0], P[1]) if t % 2 == 0 else (P[2], P[3])
        for half in range(2):
            for f in range(NFC):
                wl = []
                if f == 0:
                    wl = [(ac, st["_tp_ac"][0]), (ac, st["_tp_ac"][1])]
                    if t == 0 and half == 0:
                        wl += [(da, md), (dw, fb), (ve, st["l0a_done_ve"])]
                    if t == 1 and half == 0:
                        wl.append((ve, st["l0a_done_ve"]))
                    if t >= 2 and half == 0:
                        wl.append((ve, post[t - 2]))
                b.op("tensor", "matmul", pair[half][:, 0:512],
                     bass.AP(XNB, 2048 * f + 128 * t, [[XNB.shape[1], 128], [1, 128]]),
                     bass.AP(POOL, 1024 * f + 512 * half, [[POOL.shape[1], 128], [1, 512]]),
                     start=(f == 0), stop=False, waits=wl)
            pm = b.op("tensor", "matmul", pair[half][:, 0:512], ONES[0:1, 0:128],
                      BROW[0:1, 512 * half:512 * half + 512],
                      start=False, stop=True, inc=pe)
        b.wait("vector", pe, pm)
        v3, v4 = postln_residual_chunk(b, st, t, pair)
        post[t] = v4
    st["l0f_done_ve"] = b.cnt[ve]




def l1_attn(st, WA, WB, WC, REST):
    b, S, dr, P = st["b"], st["S"], st["dr"], st["P"]
    POOL, XNB, RES, RESB = st["POOL"], st["XNB"], st["RES"], st["RESB"]
    SMALL, STAT, RW = st["SMALL"], st["STAT"], st["RW"]
    ONES, BROW, GREP, VH, AVST = st["ONES"], st["BROW"], st["GREP"], st["VH"], st["AVST"]
    RECB, REPS, BD, GB2R, CREP, GATE = (st["RECB"], st["REPS"], st["BD"], st["GB2R"],
                                        st["CREP"], st["GATE"])
    dw, da, pe, ve, ac = S["dw"], S["da"], S["pe"], S["ve"], S["ac"]
    n_sub = st["n_sub"]
    SMALLB = SMALL.bitcast(BF16)

    def pool_ap(col, width, parts=128, p0=0):
        return bass.AP(POOL, p0 * POOL.shape[1] + col,
                       [[POOL.shape[1], parts], [1, width]])

    A0, A1 = 0, 8192
    ROLL = [16384, 20352]
    WSLOT = 24576
    XT1 = 32768
    ABUF = [A0, A1]

    start_ve = st["l0f_done_ve"]
    b.wait("vector", ve, start_ve)

    sp = 0
    for i in range(4):
        src_ap = bass.AP(RES, 4096 * i, [[RES.shape[1], 128], [1024, 4], [1, 1024]])
        dst = bass.AP(dr["h_x0"], 4 * 128 * 1024 * i,
                      [[1024, 128], [128 * 1024, 4], [1, 1024]])
        sp = b.op("sync", "dma_start", dst, src_ap,
                  waits=[(ve, start_ve)] if i == 0 else [], inc=da, inc_by=16)
    st["x0_spill_da"] = sp

    preln_batch(b, st, RES, bass.AP(POOL, XT1, [[POOL.shape[1], 128], [1, 1024]]),
                lambda t, f: XNB[:, 2048 * f + 128 * t:2048 * f + 128 * t + 128])
    preln2_ac = b.cnt[ac]

    bqc = b.op("sync", "dma_start", st["BIASC"][:, 0:8],
               bass.AP(dr["L1_attn_bq"], 0, [[1, 128], [128, 8]]), inc=da, inc_by=16,
               allow_slow_non_contiguous=True)
    bkc = b.op("sync", "dma_start", st["BIASC"][:, 8:16],
               bass.AP(dr["L1_attn_bk"], 0, [[1, 128], [128, 8]]), inc=da, inc_by=16,
               allow_slow_non_contiguous=True)
    wq_d0 = b.cnt[dw]
    w_chunk_dma(b, (POOL, WSLOT), dr["L1_attn_Wq"], D, D, dw)
    wq_done = b.cnt[dw]

    cn_d = b.op("gpsimd", "dma_start", st["KVACC"][0:16, 0:1],
                bass.AP(dr["h_pb"], 3000, [[4095, 16], [1, 1]]),
                waits=[(da, st["pbh"])], inc=dw, inc_by=16,
                allow_slow_non_contiguous=True)
    cn_d = b.op("gpsimd", "dma_start", st["KVACC"][0:16, 1:2],
                bass.AP(dr["h_pb"], 1000, [[4095, 16], [1, 1]]),
                inc=dw, inc_by=16, allow_slow_non_contiguous=True)
    st["cn_dw"] = cn_d

    gate_ac = {}
    sg_evac = {}
    for t in range(NTC):
        for f in range(NFC):
            bank = P[f % 6] if f < 6 else P[f - 6]
            wl = []
            if t == 0 and f == 0:
                wl = [(ac, st["a_bd"]), (ac, st["a_crep"]),
                      (ve, st["l0f_done_ve"])]
            if f >= 6:
                wl.append((ac, sg_evac[(t, f - 6)]))
            elif t > 0 and f == 0:
                wl.append((ac, sg_evac[(t - 1, 7)]))
            b.op("tensor", "matmul", bank[:, 0:4],
                 bass.AP(XNB, 2048 * f + 128 * t, [[XNB.shape[1], 128], [1, 128]]),
                 BD[:, 0:4], start=True, stop=False, waits=wl)
            pg = b.op("tensor", "matmul", bank[:, 0:4], ONES[0:1, 0:128],
                      GB2R[0:1, 0:4], start=False, stop=True, inc=pe)
            a = b.op("scalar", "activation", RW[:, f * 4:f * 4 + 4], bank[:, 0:4],
                     AF.Sigmoid, waits=[(pe, pg)], inc=ac)
            sg_evac[(t, f)] = a
        b.wait("vector", ac, sg_evac[(t, 7)])
        sa = bass.AP(RW, 0, [[RW.shape[1], 128], [2, 16]])
        sb = bass.AP(RW, 1, [[RW.shape[1], 128], [2, 16]])
        b.op("vector", "tensor_tensor", STAT[:, 32:48], sa, sb, ALU.mult)
        b.op("vector", "drain")
        b.op("vector", "tensor_tensor", STAT[:, 32:48], STAT[:, 32:48], CREP[:, 0:16],
             ALU.mult)
        b.op("vector", "drain")
        b.op("vector", "tensor_tensor", STAT[:, 32:48], STAT[:, 32:48], sa,
             ALU.subtract)
        b.op("vector", "drain")
        vgt = b.op("vector", "tensor_scalar_add", GATE[:, 16 * t:16 * t + 16],
                   STAT[:, 32:48], 2.0, inc=ve, lp="bf16 gate")
        wl = [(ve, vgt)]
        if t >= 2:
            wl.append((ac, st["_tp_ac"][t % 2]))
        a = transpose_tile(b, st, GATE[:, 16 * t:16 * t + 16],
                           bass.AP(GREP, 128 * t, [[GREP.shape[1], 16], [1, 128]]),
                           t, waits_t=wl, m=16)
        st["_tp_ac"][t % 2] = a
        gate_ac[t] = a
    if st["cut"] == "gates":
        st["l1a_done_ve"] = b.cnt[ve]
        st["l1a_preln3_ac"] = b.cnt[ac]
        return
    b.op("vector", "tensor_scalar", GREP[0:16, 2048:4096],
         bass.AP(GREP, 0, [[GREP.shape[1], 16], [1, 2048]]),
         st["KVACC"][0:16, 0:1], None, ALU.mult,
         waits=[(ac, gate_ac[NTC - 1]), (dw, st["cn_dw"])], lp="bf16 gcn")
    hgr0 = b.op("sync", "dma_start", dr["h_gr"][0:16, :],
                bass.AP(GREP, 0, [[GREP.shape[1], 16], [1, 2048]]),
                waits=[(ac, gate_ac[NTC - 1])], inc=da, inc_by=16)
    vgcp = b.op("vector", "tensor_scalar", GREP[0:16, 0:2048],
                bass.AP(GREP, 0, [[GREP.shape[1], 16], [1, 2048]]),
                st["KVACC"][0:16, 1:2], None, ALU.mult,
                waits=[(da, hgr0)], inc=ve, lp="bf16 gcp")
    hgr = b.op("sync", "dma_start", dr["h_gr"][16:32, :],
               bass.AP(GREP, 2048, [[GREP.shape[1], 16], [1, 2048]]),
               waits=[(ve, vgcp)], inc=da, inc_by=16)
    hgr = b.op("sync", "dma_start", dr["h_gr"][32:48, :],
               bass.AP(GREP, 0, [[GREP.shape[1], 16], [1, 2048]]),
               waits=[(ve, vgcp)], inc=da, inc_by=16)

    proj_ac = 0
    for which, wname, bias_c0, dst0, wdone_pre in (
            (0, "L1_attn_Wq", 0, 0, wq_done),
            (1, "L1_attn_Wk", 8, 16384, None)):
        if which == 1:
            b.wait("gpsimd", pe, st["_l1_lastq_pe"])
            w_chunk_dma(b, (POOL, WSLOT), dr["L1_attn_Wk"], D, D, dw)
            wdone_pre = b.cnt[dw]
        for m in range(NFC):
            for g in range(4):
                bank = P[(m * 4 + g) % 4]
                for f in range(NFC):
                    wl = []
                    if f == 0:
                        idx = m * 4 + g
                        if idx == 0:
                            wl = [(dw, wdone_pre), (ac, preln2_ac),
                                  (da, st["x0_spill_da"]), (da, bqc), (da, bkc),
                                  (ac, gate_ac[NTC - 1])]
                        if idx >= 4:
                            wl.append((ac, st[f"_l1_proj_ac{(idx - 4) % 4}"]))
                    b.op("tensor", "matmul", bank[:, 0:512],
                         bass.AP(POOL, WSLOT + 1024 * f + 128 * m,
                                 [[POOL.shape[1], 128], [1, 128]]),
                         bass.AP(XNB, 2048 * f + 512 * g, [[XNB.shape[1], 128], [1, 512]]),
                         start=(f == 0), stop=(f == 7),
                         inc=(pe if f == 7 else None), waits=wl)
                pq = b.cnt[pe]
                a = b.op("vector", "tensor_scalar",
                         bass.AP(RESB, dst0 + 2048 * m + 512 * g,
                                 [[RESB.shape[1], 128], [1, 512]]),
                         bank[:, 0:512],
                         st["BIASC"][:, bias_c0 + m:bias_c0 + m + 1], None, ALU.add,
                         waits=[(pe, pq)], inc=ac, lp="bf16 qkT")
                st[f"_l1_proj_ac{(m * 4 + g) % 4}"] = a
        st["_l1_lastq_pe"] = b.cnt[pe]
    proj_ac = b.cnt[ac]
    b.wait("gpsimd", pe, st["_l1_lastq_pe"])
    w_chunk_dma(b, (POOL, WSLOT), dr["L1_attn_Wv"], D, D, dw)
    wv_done = b.cnt[dw]
    bv_d = b.op("gpsimd", "dma_start", BROW[0:1, 0:1024],
                bass.AP(dr["L1_attn_bv"], 0, [[D, 1], [1, D]]), inc=dw, inc_by=16)
    vmem = b.op("vector", "memset",
                bass.AP(POOL, A0 + 64, [[POOL.shape[1], 128], [1040, 2], [65, 16], [1, 1]]),
                1.0, inc=ve)
    vdma = {}
    for t in range(NTC):
        stg = A0 + 1040 * (t % 2)
        pair = (P[4], P[5])
        for half in range(2):
            for f in range(NFC):
                wl = []
                if half == 0 and f == 0:
                    wl = [(dw, wv_done + 16)] if t == 0 else                          [(ac, st[f"_l1_v_ac{(t - 1) % 2}"])] if t >= 1 else []
                b.op("tensor", "matmul", pair[half][:, 0:512],
                     bass.AP(XNB, 2048 * f + 128 * t, [[XNB.shape[1], 128], [1, 128]]),
                     bass.AP(POOL, WSLOT + 1024 * f + 512 * half,
                             [[POOL.shape[1], 128], [1, 512]]),
                     start=(f == 0), stop=False, waits=wl)
            pv = b.op("tensor", "matmul", pair[half][:, 0:512], ONES[0:1, 0:128],
                      BROW[0:1, 512 * half:512 * half + 512],
                      start=False, stop=True, inc=pe)
        wl = [(pe, pv)]
        if t == 0:
            wl.append((ve, vmem))
        if t >= 2:
            wl.append((da, vdma[t - 2]))
        b.op("scalar", "copy",
             bass.AP(POOL, stg, [[POOL.shape[1], 128], [65, 8], [1, 64]]),
             P[4][:, 0:512], waits=wl, lp="bf16 v")
        a = b.op("scalar", "copy",
                 bass.AP(POOL, stg + 520, [[POOL.shape[1], 128], [65, 8], [1, 64]]),
                 P[5][:, 0:512], inc=ac, lp="bf16 v")
        st[f"_l1_v_ac{t % 2}"] = a
        vdma[t] = b.op("sync", "dma_start", dr["h_v"][128 * t:128 * t + 128, :],
                       bass.AP(POOL, stg, [[POOL.shape[1], 128], [1, 1040]]),
                       waits=[(ac, a)], inc=da, inc_by=16)
    if st["cut"] == "proj":
        st["l1a_done_ve"] = b.cnt[ve]
        st["l1a_preln3_ac"] = b.cnt[ac]
        return
    bo_d = b.op("gpsimd", "dma_start", BROW[0:1, 1024:2048],
                bass.AP(dr["L1_attn_bo"], 0, [[D, 1], [1, D]]), inc=dw, inc_by=16)

    SC = 1.0 / 8.0
    TB = 0
    SCBC = [1024, 1536, 2048, 2560]
    roll_d = {}
    grep_d = {}
    vh_d = {}

    def prefetch(h):
        if h >= 16:
            return
        wl_r = [(da, st["pbh"])]
        wl_g = [(da, hgr)]
        wl_v = [(da, vdma[NTC - 1])]
        if h >= 2:
            wl_r.append((ve, st.get(("tt1last", h - 2), 0)))
            wl_g.append((ve, st.get(("tt1last", h - 2), 0)))
            wl_v.append((pe, st.get(("avlast", "hq", (h - 2) * 4 + 3), 0)))
        roll_d[h] = b.op("sync", "dma_start",
                         pool_ap(ROLL[h % 2], 3968),
                         bass.AP(dr["h_pb"], 4095 * h, [[1, 128], [1, 3968]]),
                         waits=wl_r, inc=da, inc_by=16)
        grep_d[h] = b.op("sync", "dma_start", GREP[:, 2048 * (h % 2):2048 * (h % 2) + 2048],
                         bass.AP(dr["h_gr"], 2048 * h, [[0, 128], [1, 2048]]),
                         waits=wl_g, inc=da, inc_by=16)
        b.op("sync", "dma_start",
             bass.AP(POOL, WSLOT + 4096 * (h % 2), [[POOL.shape[1], 128], [1, 2048]]),
             bass.AP(dr["h_gr"], 2048 * (16 + h), [[0, 128], [1, 2048]]),
             waits=wl_g, inc=da, inc_by=16)
        grep_d[h] = b.op("sync", "dma_start",
                         bass.AP(POOL, WSLOT + 4096 * (h % 2) + 2048,
                                 [[POOL.shape[1], 128], [1, 2048]]),
                         bass.AP(dr["h_gr"], 2048 * (32 + h), [[0, 128], [1, 2048]]),
                         waits=wl_g, inc=da, inc_by=16)
        vh_d[h] = b.op("sync", "dma_start",
                       bass.AP(VH, 1040 * (h % 2), [[VH.shape[1], 128], [65, 16], [1, 65]]),
                       bass.AP(dr["h_v"], 65 * h, [[1040, 128], [128 * 1040, 16], [1, 65]]),
                       waits=wl_v, inc=da, inc_by=16)

    prefetch(0)
    prefetch(1)
    avst_d = {}
    exp_cnt = {}
    tt2_cnt = {}
    for h in range(16):
        p0 = 64 * (h % 2)
        hh = h // 2
        gslot = 2048 * (h % 2)
        b.wait("vector", da, roll_d[h])
        b.wait("vector", da, grep_d[h])
        def av_tail(hqx):
            hx = hqx // 4
            qgx = hqx % 4
            Abx = ABUF[hqx % 2]
            avb = P[4 + hqx % 2]
            repb = P[4 + (hqx + 1) % 2]
            for ktav in range(NTC):
                wl_av = []
                if ktav == 0:
                    wl_av = [(da, vh_d[hx]), (ac, exp_cnt[(hqx, NTC - 1)])]
                    if hqx >= 1:
                        wl_av.append((ac, st.get(("reps_ac", hqx - 1), 0)))
                pav = b.op("tensor", "matmul", avb[0:65, 0:512],
                           bass.AP(VH, 1040 * (hx % 2) + 65 * ktav,
                                   [[VH.shape[1], 128], [1, 65]]),
                           pool_ap(Abx + 512 * ktav, 512),
                           start=(ktav == 0), stop=(ktav == NTC - 1),
                           waits=wl_av, inc=(pe if ktav == NTC - 1 else None))
            st[("avlast", "hq", hqx)] = pav
            if qgx == 3:
                st[("avlast", hx)] = pav
            b.op("vector", "reciprocal", RECB[64:65, 0:512], avb[64:65, 0:512],
                 waits=[(pe, pav)], lp="bf16 recip")
            vr = b.op("vector", "drain", inc=ve)
            prep = b.op("tensor", "matmul", repb[0:64, 0:512], ONES[64:65, 0:64],
                        RECB[64:65, 0:512], start=True, stop=True,
                        waits=[(ve, vr)], inc=pe)
            arep = b.op("scalar", "copy", REPS[0:64, 0:512], repb[0:64, 0:512],
                        waits=[(pe, prep)], inc=ac, lp="bf16 reps")
            st[("reps_ac", hqx)] = arep
            aslot = 512 * (hqx % 2)
            wl_avt = [(ac, arep)]
            if hqx >= 2:
                wl_avt.append((da, avst_d[hqx - 2]))
            vavt = b.op("vector", "tensor_tensor", AVST[0:64, aslot:aslot + 512],
                        avb[0:64, 0:512], REPS[0:64, 0:512], ALU.mult,
                        waits=wl_avt, inc=ve, lp="bf16 avt")
            st[("avt_tt", hqx)] = vavt
            avst_d[hqx] = b.op("sync", "dma_start",
                               bass.AP(dr["h_avt"], 2048 * 64 * hx + 512 * qgx,
                                       [[2048, 64], [1, 512]]),
                               AVST[0:64, aslot:aslot + 512],
                               waits=[(ve, vavt)], inc=da, inc_by=16)

        for qg in range(4):
            hq = h * 4 + qg
            Ab = ABUF[hq % 2]
            for kt in range(NTC):
                bank = P[kt % 4]
                wl = []
                if kt >= 4:
                    wl.append((ve, tt2_cnt[(hq, kt - 4)]))
                elif hq >= 1:
                    wl.append((ve, tt2_cnt[(hq - 1, kt + 12)]))
                else:
                    wl.append((ac, proj_ac))
                if kt == 0 and hq >= 2:
                    wl.append((pe, st.get(("avlast", "hq", hq - 2), 0)))
                pqk = b.op("tensor", "matmul", bank[:, 0:512],
                           bass.AP(RESB, 16384 + 2048 * hh + 128 * kt + p0 * RESB.shape[1],
                                   [[RESB.shape[1], 64], [1, 128]]),
                           bass.AP(RESB, 2048 * hh + 512 * qg + p0 * RESB.shape[1],
                                   [[RESB.shape[1], 64], [1, 512]]),
                           start=True, stop=True, waits=wl, inc=pe)
                is_left = (128 * kt + 127) <= (512 * qg - 91)
                is_right = (128 * kt) >= (512 * qg + 511 + 91)
                if is_left or is_right:
                    coff = WSLOT + 4096 * (h % 2) + (0 if is_left else 2048)
                    bias_ap = bass.AP(POOL, coff + 512 * qg,
                                      [[POOL.shape[1], 128], [1, 512]])
                else:
                    rv = bass.AP(POOL, ROLL[h % 2] + 2047 + 128 * kt - 512 * qg,
                                 [[POOL.shape[1], 128], [-1, 512]])
                    wl_t = []
                    if hq >= 2 and kt == 0:
                        wl_t.append((ac, exp_cnt.get((hq - 2, NTC - 1), 0)))
                    b.op("vector", "tensor_tensor",
                         bass.AP(SMALLB, TB, [[SMALLB.shape[1], 128], [1, 512]]),
                         GREP[:, gslot + 512 * qg:gslot + 512 * qg + 512], rv, ALU.mult,
                         waits=wl_t, lp="bf16 bias")
                    bias_ap = bass.AP(SMALLB, TB, [[SMALLB.shape[1], 128], [1, 512]])
                wl2 = [(pe, pqk)]
                if kt >= 4:
                    wl2.append((ac, exp_cnt[(hq, kt - 4)]))
                elif hq >= 1:
                    wl2.append((ac, exp_cnt[(hq - 1, 12 + kt)]))
                v2 = b.op("vector", "scalar_tensor_tensor",
                          bass.AP(SMALLB, SCBC[kt % 4], [[SMALLB.shape[1], 128], [1, 512]]),
                          bank[:, 0:512], SC, bias_ap,
                          ALU.mult, ALU.add, inc=ve, lp="bf16 scores")
                tt2_cnt[(hq, kt)] = v2
                a = b.op("scalar", "activation",
                         pool_ap(Ab + 512 * kt, 512),
                         bass.AP(SMALLB, SCBC[kt % 4], [[SMALLB.shape[1], 128], [1, 512]]),
                         AF.Exp, waits=[(ve, v2)], inc=ac, lp="bf16 softmax")
                exp_cnt[(hq, kt)] = a
                if st["cut"] == "qkonly" and kt == NTC - 1:
                    st["l1a_done_ve"] = b.cnt[ve]
                    st["l1a_preln3_ac"] = b.cnt[ac]
                    return
            if qg == 3:
                st[("tt1last", h)] = tt2_cnt[(hq, NTC - 1)]
            if hq >= 1:
                av_tail(hq - 1)
            if qg == 0 and h >= 1:
                prefetch(h + 1)

    av_tail(63)
    b.wait("gpsimd", ve, b.cnt[ve])
    w_chunk_dma(b, (POOL, WSLOT), dr["L1_attn_Wo"], D, D, dw)
    wo_done = b.cnt[dw]
    if st["cut"] == "mainloop":
        st["l1a_done_ve"] = b.cnt[ve]
        st["l1a_preln3_ac"] = b.cnt[ac]
        return
    AVL = 0
    post = {}
    xstream = {}
    for t in range(NTC):
        wl = [(da, avst_d[63])] if t == 0 else []
        dls = []
        for g2 in range(2):
            dd = b.op("sync", "dma_start",
                      bass.AP(POOL, AVL + 1024 * (t % 4) + 512 * g2,
                              [[POOL.shape[1], 128], [128, 4], [1, 128]]),
                      bass.AP(dr["h_avt"], 128 * t + 4 * g2 * 128 * 2048,
                              [[2048, 128], [128 * 2048, 4], [1, 128]]),
                      waits=wl + ([(ve, post.get(t - 4, 0))] if t >= 4 else []),
                      inc=da, inc_by=16)
            dls.append(dd)
        dx = b.op("sync", "dma_start",
                  bass.AP(SMALL, 1024, [[SMALL.shape[1], 128], [1, 1024]]),
                  bass.AP(dr["h_x0"], 128 * 1024 * t, [[1024, 128], [1, 1024]]),
                  waits=[(ve, xstream.get(t - 1, 0))], inc=da, inc_by=16)
        pair = (P[0], P[1]) if t % 2 == 0 else (P[2], P[3])
        for half in range(2):
            for f in range(NFC):
                wl = []
                if f == 0:
                    wl = [(da, dls[-1])]
                    if t == 0 and half == 0:
                        wl += [(dw, wo_done)]
                    if t < 2 and half == 0:
                        wl.append((ve, st[("avt_tt", 63)]))
                    if t >= 2 and half == 0:
                        wl.append((ve, post.get(t - 2, 0)))
                b.op("tensor", "matmul", pair[half][:, 0:512],
                     bass.AP(POOL, AVL + 1024 * (t % 4) + 128 * f,
                             [[POOL.shape[1], 128], [1, 128]]),
                     bass.AP(POOL, WSLOT + 1024 * f + 512 * half,
                             [[POOL.shape[1], 128], [1, 512]]),
                     start=(f == 0), stop=False, waits=wl)
            po = b.op("tensor", "matmul", pair[half][:, 0:512], ONES[0:1, 0:128],
                      BROW[0:1, 1024 + 512 * half:1536 + 512 * half],
                      start=False, stop=True, inc=pe)
        b.wait("vector", pe, po)
        v2 = ln_stats(b, S, STAT, (pair[0][:, 0:512], pair[1][:, 0:512]))
        b.op("vector", "tensor_scalar", SMALL[:, 0:512], pair[0][:, 0:512],
             STAT[:, 2:3], STAT[:, 3:4], ALU.mult, ALU.add)
        vln = b.op("vector", "tensor_scalar", SMALL[:, 512:1024], pair[1][:, 0:512],
                   STAT[:, 2:3], STAT[:, 3:4], ALU.mult, ALU.add, inc=ve)
        b.wait("vector", da, dx)
        vxn = b.op("vector", "tensor_tensor", SMALL[:, 2048:3072], SMALL[:, 1024:2048],
                   SMALL[:, 0:1024], ALU.add, inc=ve)
        post[t] = vxn
        dst = dr["y"] if n_sub == 3 else dr["h_x1"]
        dxo = b.op("sync", "dma_start",
                   bass.AP(dst, 128 * 1024 * t, [[1024, 128], [1, 1024]]),
                   bass.AP(SMALL, 2048, [[SMALL.shape[1], 128], [1, 1024]]),
                   waits=[(ve, vxn)], inc=da, inc_by=16)
        xstream[t] = vxn
        st[("x1_dma", t)] = dxo
        if n_sub >= 4:
            v2b = ln_stats(b, S, STAT, (SMALL[:, 2048:2560], SMALL[:, 2560:3072]))
            v3 = b.op("vector", "tensor_scalar",
                      bass.AP(POOL, XT1, [[POOL.shape[1], 128], [1, 1024]]),
                      SMALL[:, 2048:3072], STAT[:, 2:3], STAT[:, 3:4],
                      ALU.mult, ALU.add,
                      waits=[(pe, st.get("_tp_pe", 0))], inc=ve)
            xstream[t] = v3
            for f in range(NFC):
                wl = [(ve, v3)]
                if f >= 2:
                    wl.append((ac, st["_tp_ac"][f % 2]))
                a = transpose_tile(b, st, bass.AP(POOL, XT1 + 128 * f,
                                                  [[POOL.shape[1], 128], [1, 128]]),
                                   XNB[:, 2048 * f + 128 * t:2048 * f + 128 * t + 128],
                                   f, waits_t=wl)
                st["_tp_ac"][f % 2] = a
    if n_sub == 3 and st["cut"] is None:
        last = st[("x1_dma", NTC - 1)]
        b.wait("sync", da, last)
    st["l1a_done_ve"] = b.cnt[ve]
    st["l1a_preln3_ac"] = b.cnt[ac]


def l1_ffn(st, WA, WB, WC, REST):
    b, S, dr, P = st["b"], st["S"], st["dr"], st["P"]
    POOL, XNB, RES, SMALL, STAT, RW = (st["POOL"], st["XNB"], st["RES"], st["SMALL"],
                                       st["STAT"], st["RW"])
    ONES, BROW = st["ONES"], st["BROW"]
    dw, da, pe, ve, ac = S["dw"], S["da"], S["pe"], S["ve"], S["ac"]
    OUT = RES

    WS1, WS2, HT, WRC = 0, 8192, 16384, 32768

    start_ve = st["l1a_done_ve"]
    wr_d = b.op("gpsimd", "dma_start",
                bass.AP(POOL, WRC, [[POOL.shape[1], 128], [8, 8], [1, 8]]),
                bass.AP(dr["L1_ffn_Wr"], 0, [[8, 128], [128 * 8, 8], [1, 8]]),
                waits=[(ve, start_ve)], inc=dw, inc_by=16)
    eb_d = b.op("gpsimd", "dma_start", BROW[0:1, 0:8],
                bass.AP(dr["L1_ffn_expert_bias"], 0, [[8, 1], [1, 8]]),
                inc=dw, inc_by=16)
    w_chunk_dma(b, (POOL, WS1), dr["L1_ffn_Sg"], D, D, dw)
    sg_done = b.cnt[dw]
    w_chunk_dma(b, (POOL, WS2), dr["L1_ffn_Su"], D, D, dw)
    su_done = b.cnt[dw]

    rt_evac = {}
    for t in range(NTC):
        bank = P[4 + t % 2]
        wl = [(dw, eb_d), (ac, st["l1a_preln3_ac"])] if t == 0 else              [(ve, rt_evac[t - 2])] if t >= 2 else []
        for f in range(NFC):
            b.op("tensor", "matmul", bank[:, 0:8],
                 bass.AP(XNB, 2048 * f + 128 * t, [[XNB.shape[1], 128], [1, 128]]),
                 bass.AP(POOL, WRC + 8 * f, [[POOL.shape[1], 128], [1, 8]]),
                 start=(f == 0), stop=False, waits=wl if f == 0 else [])
        pr = b.op("tensor", "matmul", bank[:, 0:8], ONES[0:1, 0:128],
                  BROW[0:1, 0:8], start=False, stop=True, inc=pe)
        vr = b.op("vector", "tensor_copy", RW[:, 8 * t:8 * t + 8], bank[:, 0:8],
                  waits=[(pe, pr)], inc=ve)
        rt_evac[t] = vr
    for t in range(NTC):
        L = RW[:, 8 * t:8 * t + 8]
        b.op("vector", "tensor_reduce", STAT[:, 32:33], L, axis=AX.X, op=ALU.max)
        b.op("vector", "drain")
        b.op("vector", "tensor_scalar", STAT[:, 34:42], L, STAT[:, 32:33], None,
             ALU.is_ge)
        b.op("vector", "drain")
        b.op("vector", "scalar_tensor_tensor", STAT[:, 42:50], STAT[:, 34:42], -1e30,
             L, ALU.mult, ALU.add)
        b.op("vector", "drain")
        b.op("vector", "tensor_reduce", STAT[:, 33:34], STAT[:, 42:50], axis=AX.X,
             op=ALU.max)
        b.op("vector", "drain")
        b.op("vector", "tensor_scalar", STAT[:, 34:42], L, STAT[:, 33:34], None,
             ALU.is_ge)
        b.op("vector", "tensor_scalar", STAT[:, 50:51], STAT[:, 32:33], -1.0, None,
             ALU.mult)
        v1 = b.op("vector", "drain", inc=ve)
        a1 = b.op("scalar", "activation", STAT[:, 42:50], L, AF.Exp,
                  bias=STAT[:, 50:51], scale=1.0, waits=[(ve, v1)], inc=ac)
        b.op("vector", "tensor_tensor", STAT[:, 42:50], STAT[:, 42:50],
             STAT[:, 34:42], ALU.mult, waits=[(ac, a1)])
        b.op("vector", "drain")
        b.op("vector", "tensor_reduce", STAT[:, 51:52], STAT[:, 42:50], axis=AX.X,
             op=ALU.add)
        b.op("vector", "drain")
        b.op("vector", "reciprocal", STAT[:, 51:52], STAT[:, 51:52])
        b.op("vector", "drain")
        vrw = b.op("vector", "tensor_scalar", RW[:, 128 + 8 * t:136 + 8 * t],
                   STAT[:, 42:50], STAT[:, 51:52], None, ALU.mult, inc=ve)
    rw_done = b.cnt[ve]

    h_done = {}
    for m in range(NFC):
        for g in range(4):
            pg_b = P[0 + (g % 2)]
            pu_b = P[2 + (g % 2)]
            idx = m * 4 + g
            for f in range(NFC):
                wl = []
                if f == 0:
                    if idx == 0:
                        wl = [(dw, sg_done), (ve, st["l1a_done_ve"])]
                    if idx >= 2:
                        wl.append((ve, h_done[idx - 2]))
                b.op("tensor", "matmul", pg_b[:, 0:512],
                     bass.AP(POOL, WS1 + 1024 * f + 128 * m,
                             [[POOL.shape[1], 128], [1, 128]]),
                     bass.AP(XNB, 2048 * f + 512 * g, [[XNB.shape[1], 128], [1, 512]]),
                     start=(f == 0), stop=(f == 7), waits=wl,
                     inc=(pe if f == 7 else None))
            pgc = b.cnt[pe]
            for f in range(NFC):
                wl = [(dw, su_done)] if (idx == 0 and f == 0) else []
                b.op("tensor", "matmul", pu_b[:, 0:512],
                     bass.AP(POOL, WS2 + 1024 * f + 128 * m,
                             [[POOL.shape[1], 128], [1, 128]]),
                     bass.AP(XNB, 2048 * f + 512 * g, [[XNB.shape[1], 128], [1, 512]]),
                     start=(f == 0), stop=(f == 7), waits=wl,
                     inc=(pe if f == 7 else None))
            puc = b.cnt[pe]
            sbuf0 = 512 * (idx % 2)
            wl_s = [(pe, pgc)]
            if idx >= 2:
                wl_s.append((ve, h_done[idx - 2]))
            a1 = b.op("scalar", "activation", SMALL[:, sbuf0:sbuf0 + 512],
                      pg_b[:, 0:512], AF.Silu, waits=wl_s, inc=ac)
            vh = b.op("vector", "tensor_tensor",
                      bass.AP(POOL, HT + 2048 * m + 512 * g,
                              [[POOL.shape[1], 128], [1, 512]]),
                      pu_b[:, 0:512], SMALL[:, sbuf0:sbuf0 + 512], ALU.mult,
                      waits=[(ac, a1), (pe, puc)], inc=ve, lp="bf16 h")
            h_done[idx] = vh
    sh_h_ve = b.cnt[ve]
    b.wait("gpsimd", pe, b.cnt[pe])
    w_chunk_dma(b, (POOL, WS2), dr["L1_ffn_Sd"], D, D, dw)
    sd_done = b.cnt[dw]
    dsh = {}
    for t in range(NTC):
        g, r = divmod(t, 4)
        pair = (P[4], P[5]) if t % 2 == 0 else (P[0], P[1])
        for half in range(2):
            for m in range(NFC):
                wl = []
                if m == 0:
                    if t < 2 and half == 0:
                        wl = [(dw, sd_done), (ve, sh_h_ve)]
                    if t >= 2 and half == 0:
                        wl.append((ac, dsh[t - 2]))
                b.op("tensor", "matmul", pair[half][:, 0:512],
                     bass.AP(POOL, HT + 2048 * m + 128 * t,
                             [[POOL.shape[1], 128], [1, 128]]),
                     bass.AP(POOL, WS2 + 1024 * m + 512 * half,
                             [[POOL.shape[1], 128], [1, 512]]),
                     start=(m == 0), stop=(m == 7),
                     inc=(pe if m == 7 else None), waits=wl)
        pd = b.cnt[pe]
        b.op("scalar", "copy", OUT[:, 1024 * t:1024 * t + 512], pair[0][:, 0:512],
             waits=[(pe, pd)])
        a = b.op("scalar", "copy", OUT[:, 1024 * t + 512:1024 * t + 1024],
                 pair[1][:, 0:512], inc=ac)
        dsh[t] = a
    out_init_ac = b.cnt[ac]

    ex_state = {}
    EWS = [0, 12288]
    HTE = 24576
    exw = {}

    def load_expert(e):
        if e >= 8:
            return
        base2 = EWS[e % 2]
        if e < 2:
            b.wait("gpsimd", pe, b.cnt[pe])
        else:
            b.wait("gpsimd", pe, exw[("lastpe", e - 2)])
        for nm, off in (("L1_ffn_Wg", base2), ("L1_ffn_Wu", base2 + 4096)):
            dstap = bass.AP(POOL, off, [[POOL.shape[1], 128], [512, 8], [1, 512]])
            srcap = bass.AP(dr[nm], e * D * 512, [[512, 128], [128 * 512, 8], [1, 512]])
            b.op("gpsimd", "dma_start", dstap, srcap, inc=dw, inc_by=16)
        wd_dst = bass.AP(POOL, base2 + 8192, [[POOL.shape[1], 128], [1024, 4], [1, 1024]])
        wd_src = bass.AP(dr["L1_ffn_Wd"], e * 512 * D,
                         [[1024, 128], [128 * 1024, 4], [1, 1024]])
        b.op("gpsimd", "dma_start", wd_dst, wd_src, inc=dw, inc_by=16)
        exw[("dw", e)] = b.cnt[dw]

    for e in range(8):
        base = EWS[e % 2]
        if e == 0:
            load_expert(0)
            load_expert(1)
        wg_d = exw[("dw", e)]
        wd_d = wg_d
        eh_done = {}
        for m in range(4):
            for g in range(4):
                idx = m * 4 + g
                pg_b = P[0 + (g % 2)]
                pu_b = P[2 + (g % 2)]
                for f in range(NFC):
                    wl = []
                    if f == 0:
                        if idx == 0:
                            wl = [(dw, wg_d)]
                            if e == 0:
                                wl.append((ac, out_init_ac))
                            else:
                                wl.append((ve, ex_state[e - 1]))
                        if idx >= 2:
                            wl.append((ve, eh_done[idx - 2]))
                    b.op("tensor", "matmul", pg_b[:, 0:512],
                         bass.AP(POOL, base + 512 * f + 128 * m,
                                 [[POOL.shape[1], 128], [1, 128]]),
                         bass.AP(XNB, 2048 * f + 512 * g, [[XNB.shape[1], 128], [1, 512]]),
                         start=(f == 0), stop=(f == 7), waits=wl,
                         inc=(pe if f == 7 else None))
                pgc = b.cnt[pe]
                for f in range(NFC):
                    b.op("tensor", "matmul", pu_b[:, 0:512],
                         bass.AP(POOL, base + 4096 + 512 * f + 128 * m,
                                 [[POOL.shape[1], 128], [1, 128]]),
                         bass.AP(XNB, 2048 * f + 512 * g, [[XNB.shape[1], 128], [1, 512]]),
                         start=(f == 0), stop=(f == 7),
                         inc=(pe if f == 7 else None))
                puc = b.cnt[pe]
                sbuf0 = 512 * (idx % 2)
                wl_s = [(pe, pgc)]
                if idx >= 2:
                    wl_s.append((ve, eh_done[idx - 2]))
                elif e == 0:
                    wl_s.append((ve, h_done[30]))
                else:
                    wl_s.append((ve, ex_state[e - 1]))
                a1 = b.op("scalar", "activation", SMALL[:, sbuf0:sbuf0 + 512],
                          pg_b[:, 0:512], AF.Silu, waits=wl_s, inc=ac)
                vh = b.op("vector", "tensor_tensor",
                          bass.AP(POOL, HTE + 2048 * m + 512 * g,
                                  [[POOL.shape[1], 128], [1, 512]]),
                          pu_b[:, 0:512], SMALL[:, sbuf0:sbuf0 + 512], ALU.mult,
                          waits=[(ac, a1), (pe, puc)], inc=ve, lp="bf16 h")
                eh_done[idx] = vh
        he_ve = b.cnt[ve]
        dacc = {}
        for t in range(NTC):
            pair = (P[4], P[5]) if t % 2 == 0 else (P[0], P[1])
            for half in range(2):
                for m in range(4):
                    wl = []
                    if m == 0:
                        if t < 2 and half == 0:
                            wl = [(dw, wd_d), (ve, he_ve)]
                        if t >= 2 and half == 0:
                            wl.append((ve, dacc[t - 2]))
                    b.op("tensor", "matmul", pair[half][:, 0:512],
                         bass.AP(POOL, HTE + 2048 * m + 128 * t,
                                 [[POOL.shape[1], 128], [1, 128]]),
                         bass.AP(POOL, base + 8192 + 1024 * m + 512 * half,
                                 [[POOL.shape[1], 128], [1, 512]]),
                         start=(m == 0), stop=(m == 3),
                         inc=(pe if m == 3 else None), waits=wl)
            pd = b.cnt[pe]
            wl = [(pe, pd)]
            if e == 0 and t < 2:
                wl.append((ve, rw_done))
            b.op("vector", "scalar_tensor_tensor", OUT[:, 1024 * t:1024 * t + 512],
                 pair[0][:, 0:512], RW[:, 128 + 8 * t + e:129 + 8 * t + e],
                 OUT[:, 1024 * t:1024 * t + 512], ALU.mult, ALU.add, waits=wl)
            vd = b.op("vector", "scalar_tensor_tensor",
                      OUT[:, 1024 * t + 512:1024 * t + 1024],
                      pair[1][:, 0:512], RW[:, 128 + 8 * t + e:129 + 8 * t + e],
                      OUT[:, 1024 * t + 512:1024 * t + 1024], ALU.mult, ALU.add,
                      inc=ve)
            dacc[t] = vd
        ex_state[e] = b.cnt[ve]
        exw[("lastpe", e)] = b.cnt[pe]
        load_expert(e + 2)

    ydma = {}
    for t in range(NTC):
        dx = b.op("sync", "dma_start",
                  bass.AP(SMALL, 1024, [[SMALL.shape[1], 128], [1, 1024]]),
                  bass.AP(dr["h_x1"], 128 * 1024 * t, [[1024, 128], [1, 1024]]),
                  waits=[(da, st[("x1_dma", t)]), (ve, ex_state[7])] if t == 0
                  else [(ve, ydma.get(t - 1, 0))], inc=da, inc_by=16)
        v2 = ln_stats(b, S, STAT,
                      (OUT[:, 1024 * t:1024 * t + 512],
                       OUT[:, 1024 * t + 512:1024 * t + 1024]),
                      waits_v=[(ve, ex_state[7])] if t == 0 else ())
        b.op("vector", "tensor_scalar", SMALL[:, 0:512], OUT[:, 1024 * t:1024 * t + 512],
             STAT[:, 2:3], STAT[:, 3:4], ALU.mult, ALU.add)
        b.op("vector", "tensor_scalar", SMALL[:, 512:1024],
             OUT[:, 1024 * t + 512:1024 * t + 1024],
             STAT[:, 2:3], STAT[:, 3:4], ALU.mult, ALU.add)
        b.wait("vector", da, dx)
        vy = b.op("vector", "tensor_tensor", SMALL[:, 2048:3072], SMALL[:, 1024:2048],
                  SMALL[:, 0:1024], ALU.add, inc=ve)
        dy = b.op("sync", "dma_start",
                  bass.AP(dr["y"], 128 * 1024 * t, [[1024, 128], [1, 1024]]),
                  bass.AP(SMALL, 2048, [[SMALL.shape[1], 128], [1, 1024]]),
                  waits=[(ve, vy)], inc=da, inc_by=16)
        ydma[t] = vy
        st[("y_dma", t)] = dy
    b.wait("sync", da, st[("y_dma", NTC - 1)])



def make_in_maps(inputs, n_cores=8):
    consts = host_constants()
    x = np.ascontiguousarray(inputs["L0_attn_x"], dtype=np.float32)
    base = {}
    for k, v in inputs.items():
        if k == "L0_attn_x":
            continue
        base[k] = np.ascontiguousarray(v, dtype=np.float32)
    for k, v in consts.items():
        base[k] = v
    maps = []
    for c in range(n_cores):
        m = dict(base)
        m["x"] = x[c]
        maps.append(m)
    return maps


def kernel(**inputs):
    nc = build_program(n_sub=4)
    in_maps = make_in_maps(inputs)
    res = run_bass_kernel_spmd(nc, in_maps, core_ids=list(range(8)))
    out = np.stack([np.asarray(r["y"], dtype=np.float32) for r in res.results], axis=0)
    return out


# version 13
# speedup vs baseline: 1.0299x; 1.0299x over previous
import numpy as np
import ml_dtypes
import concourse.bass as bass
import concourse.mybir as mybir
from concourse.bass_utils import run_bass_kernel_spmd

F32 = mybir.dt.float32
BF16 = mybir.dt.bfloat16
AF = mybir.ActivationFunctionType
ALU = mybir.AluOpType
AX = mybir.AxisListType

NT = 2048
D = 1024
NTC = 16
NFC = 8
H = 16
HD = 64
EPS = 1e-5

ENGINES = ("sync", "tensor", "vector", "scalar", "gpsimd")



def _bucket_1d():
    rel = np.arange(4095) - 2047
    nb = 16
    buckets = (rel > 0).astype(np.int64) * nb
    rp = np.abs(rel)
    max_exact = nb // 2
    is_small = rp < max_exact
    large = max_exact + (np.log(np.maximum(rp, 1) / max_exact) /
                         np.log(128 / max_exact) * (nb - max_exact)).astype(np.int64)
    large = np.minimum(large, nb - 1)
    return buckets + np.where(is_small, rp, large)


def host_constants():
    c = {}
    b = _bucket_1d()
    oh = np.zeros((32, 4095), np.float32)
    oh[b[::-1], np.arange(4095)] = 1.0
    c["c_oh"] = oh.astype(ml_dtypes.bfloat16)
    jk = np.outer(np.arange(D, dtype=np.float64), np.arange(D, dtype=np.float64))
    ang = 2.0 * np.pi * (jk % D) / D
    c["c_cos"] = np.cos(ang).astype(np.float32).astype(ml_dtypes.bfloat16)
    c["c_sin"] = np.sin(ang).astype(np.float32).astype(ml_dtypes.bfloat16)
    c["c_id"] = np.eye(128, dtype=np.float32).astype(ml_dtypes.bfloat16)
    return c



class Bld:

    def __init__(self, nc):
        self.nc = nc
        self.q = {e: [] for e in ENGINES}
        self.cnt = {}
        self._cms = []

    def sem(self, name):
        cm = self.nc.semaphore(name)
        h = cm.__enter__()
        self._cms.append(cm)
        self.cnt[h] = 0
        return h

    def close_sems(self):
        for cm in reversed(self._cms):
            cm.__exit__(None, None, None)

    def op(self, eng, method, *args, waits=(), inc=None, inc_by=1, lp=None, **kw):
        after = None
        if inc is not None:
            self.cnt[inc] += inc_by
            after = self.cnt[inc]
        wl = tuple(waits)

        def f(e, method=method, args=args, kw=kw, wl=wl, inc=inc, inc_by=inc_by, lp=lp):
            for s, v in wl:
                e.wait_ge(s, v)
            if lp:
                with self.nc.allow_low_precision(reason=lp):
                    ins = getattr(e, method)(*args, **kw)
            else:
                ins = getattr(e, method)(*args, **kw)
            if inc is not None:
                ins.then_inc(inc, inc_by)
        self.q[eng].append(f)
        return after

    def wait(self, eng, s, v):
        def f(e, s=s, v=v):
            e.wait_ge(s, v)
        self.q[eng].append(f)

    def raw(self, eng, fn):
        self.q[eng].append(fn)

    def render(self):
        nc = self.nc
        with nc.Block() as block:
            @block.sync
            def _(e):
                for f in self.q["sync"]:
                    f(e)

            @block.tensor
            def _(e):
                for f in self.q["tensor"]:
                    f(e)

            @block.vector
            def _(e):
                for f in self.q["vector"]:
                    f(e)

            @block.scalar
            def _(e):
                for f in self.q["scalar"]:
                    f(e)

            @block.gpsimd
            def _(e):
                for f in self.q["gpsimd"]:
                    f(e)


def w_chunk_dma(b, dst_ap_args, src, rows, cols, sem, split=2, eng="gpsimd"):
    t, col0 = dst_ap_args
    nch = rows // 128
    per = max(1, nch // split)
    total = 0
    ch = 0
    while ch < nch:
        n = min(per, nch - ch)
        dst = bass.AP(t, col0 + cols * ch, [[t.shape[1], 128], [cols, n], [1, cols]])
        sap = bass.AP(src, 128 * cols * ch, [[cols, 128], [128 * cols, n], [1, cols]])
        b.op(eng, "dma_start", dst, sap, inc=sem, inc_by=16)
        total += 16
        ch += n
    return total



def build_program(n_sub=4, dbg=False, cut=None):
    nc = bass.Bass("TRN2", target_bir_lowering=False)
    b = Bld(nc)

    dr = {}
    dr["x"] = nc.dram_tensor("x", [NT, D], F32, kind="ExternalInput")
    dr["y"] = nc.dram_tensor("y", [NT, D], F32, kind="ExternalOutput")
    for nm, shp, dt in [("c_oh", [32, 4095], BF16), ("c_cos", [D, D], BF16),
                        ("c_sin", [D, D], BF16), ("c_id", [128, 128], BF16)]:
        dr[nm] = nc.dram_tensor(nm, shp, dt, kind="ExternalInput")
    WNAMES = [
        ("L0_attn_wg_w", [D, D]), ("L0_attn_wg_b", [D]),
        ("L0_attn_wq_w", [D, D]), ("L0_attn_wq_b", [D]),
        ("L0_attn_wk_w", [D, D]), ("L0_attn_wk_b", [D]),
        ("L0_attn_wv_w", [D, D]), ("L0_attn_wv_b", [D]),
        ("L0_attn_out_w", [D, D]), ("L0_attn_out_b", [D]),
        ("L0_ffn_w_freq_real", [D, D]), ("L0_ffn_w_freq_imag", [D, D]),
        ("L0_ffn_bias", [D]),
        ("L1_attn_Wq", [D, D]), ("L1_attn_bq", [D]),
        ("L1_attn_Wk", [D, D]), ("L1_attn_bk", [D]),
        ("L1_attn_Wv", [D, D]), ("L1_attn_bv", [D]),
        ("L1_attn_Wo", [D, D]), ("L1_attn_bo", [D]),
        ("L1_attn_rel_embed", [32, 16]), ("L1_attn_gru_w", [64, 8]),
        ("L1_attn_gru_b", [8]), ("L1_attn_gru_const", [1, 16, 1, 1]),
        ("L1_ffn_Wr", [D, 8]), ("L1_ffn_Wg", [8, D, 512]),
        ("L1_ffn_Wu", [8, D, 512]), ("L1_ffn_Wd", [8, 512, D]),
        ("L1_ffn_Sg", [D, D]), ("L1_ffn_Su", [D, D]), ("L1_ffn_Sd", [D, D]),
        ("L1_ffn_expert_bias", [8]),
    ]
    for nm, shp in WNAMES:
        dr[nm] = nc.dram_tensor(nm, shp, F32, kind="ExternalInput")
    dr["h_pb"] = nc.dram_tensor("h_pb", [16, 4095], BF16, kind="Internal")
    dr["h_PT"] = nc.dram_tensor("h_PT", [D, D], BF16, kind="Internal")
    dr["h_QT"] = nc.dram_tensor("h_QT", [D, D], BF16, kind="Internal")
    dr["h_M"] = nc.dram_tensor("h_M", [D, D], BF16, kind="Internal")
    dr["h_x0"] = nc.dram_tensor("h_x0", [NT, D], F32, kind="Internal")
    dr["h_x1"] = nc.dram_tensor("h_x1", [NT, D], F32, kind="Internal")
    dr["h_qt"] = nc.dram_tensor("h_qt", [D, NT], BF16, kind="Internal")
    dr["h_avt"] = nc.dram_tensor("h_avt", [D, NT], BF16, kind="Internal")
    dr["h_v"] = nc.dram_tensor("h_v", [NT, H * 65], BF16, kind="Internal")
    dr["h_gr"] = nc.dram_tensor("h_gr", [48, NT], BF16, kind="Internal")

    import contextlib
    ctx = contextlib.ExitStack()

    def sb(name, shape, dt):
        return ctx.enter_context(nc.sbuf_tensor(name, shape, dt))

    def psum(name, shape, dt=F32):
        return ctx.enter_context(nc.psum_tensor(name, shape, dt))

    RES = sb("RES", [128, NTC * D], F32)
    RESB = RES.bitcast(BF16)
    XNB = sb("XNB", [128, 16640], BF16)
    POOL = sb("POOL", [128, 35072], BF16)
    SMALL = sb("SMALL", [128, 3072], F32)
    STAT = sb("STAT", [128, 64], F32)
    RW = sb("RW", [128, 256], F32)
    BROW = sb("BROW", [128, 2048], BF16)
    ONES = sb("ONES", [128, 128], BF16)
    IDBF = sb("IDBF", [128, 128], BF16)
    KVS = sb("KVS", [128, 520], BF16)
    KVACC = sb("KVACC", [128, 1040], F32)
    BIASC = sb("BIASC", [128, 16], F32)
    GW2B = sb("GW2B", [128, 2], BF16)
    GB2R = sb("GB2R", [128, 4], BF16)
    BD = sb("BD", [128, 4], BF16)
    CREP = sb("CREP", [128, 16], F32)
    GATE = sb("GATE", [128, 256], BF16)
    GREP = sb("GREP", [128, 4096], BF16)
    VH = sb("VH", [128, 2080], BF16)
    AVST = sb("AVST", [128, 1024], BF16)
    RECB = sb("RECB", [128, 512], BF16)
    REPS = sb("REPS", [128, 512], BF16)
    GRUW = sb("GRUW", [128, 8], F32)
    GRUB = sb("GRUB", [128, 8], F32)
    C16 = sb("C16", [128, 16], F32)

    P = [psum(f"P{i}", [128, 512]) for i in range(6)]
    T0 = psum("T0", [128, 128], BF16)
    T1 = psum("T1", [128, 128], BF16)
    TPS = [T0, T1]

    S = {n: b.sem(n) for n in ["dw", "da", "pe", "ve", "ac", "gp"]}
    dw, da, pe, ve, ac = S["dw"], S["da"], S["pe"], S["ve"], S["ac"]

    st = dict(nc=nc, b=b, dr=dr, S=S, RES=RES, RESB=RESB, XNB=XNB, POOL=POOL,
              SMALL=SMALL, STAT=STAT, RW=RW, BROW=BROW, ONES=ONES, IDBF=IDBF,
              KVS=KVS, KVACC=KVACC, BIASC=BIASC, GW2B=GW2B, GB2R=GB2R, BD=BD,
              CREP=CREP, GATE=GATE, GREP=GREP, VH=VH, AVST=AVST,
              RECB=RECB, REPS=REPS, GRUW=GRUW, GRUB=GRUB, C16=C16,
              P=P, TPS=TPS)

    if dbg:
        dr["o_xnb"] = nc.dram_tensor("o_xnb", [128, 16384], BF16, kind="ExternalOutput")
        dr["o_kvs"] = nc.dram_tensor("o_kvs", [128, 520], BF16, kind="ExternalOutput")
        dr["o_kvacc"] = nc.dram_tensor("o_kvacc", [128, 1040], F32, kind="ExternalOutput")
        dr["o_rw"] = nc.dram_tensor("o_rw", [128, 256], F32, kind="ExternalOutput")
        dr["o_out"] = nc.dram_tensor("o_out", [NT, D], F32, kind="ExternalOutput")
        if dbg == 2:
            dr["o_vb"] = nc.dram_tensor("o_vb", [128, 16640], BF16, kind="ExternalOutput")
            dr["o_kq"] = nc.dram_tensor("o_kq", [128, 1024], BF16, kind="ExternalOutput")
            dr["o_brow"] = nc.dram_tensor("o_brow", [128, 2048], BF16, kind="ExternalOutput")
            dr["o_w1"] = nc.dram_tensor("o_w1", [128, 8192], BF16, kind="ExternalOutput")
    st["dbg"] = dbg
    st["n_sub"] = n_sub
    st["cut"] = cut

    prog(st, n_sub)

    if dbg:
        vv, aa, pp = b.cnt[S["ve"]], b.cnt[S["ac"]], b.cnt[S["pe"]]
        b.wait("gpsimd", S["ve"], vv)
        b.wait("gpsimd", S["ac"], aa)
        b.wait("gpsimd", S["pe"], pp)
        dd = b.op("gpsimd", "dma_start", dr["o_xnb"].ap(), XNB[:, 0:16384],
                  inc=S["dw"], inc_by=16)
        dd = b.op("gpsimd", "dma_start", dr["o_kvs"].ap(), KVS[:, :], inc=S["dw"], inc_by=16)
        dd = b.op("gpsimd", "dma_start", dr["o_kvacc"].ap(), st["KVACC"][:, :], inc=S["dw"], inc_by=16)
        dd = b.op("gpsimd", "dma_start", dr["o_rw"].ap(), st["RW"][:, :], inc=S["dw"], inc_by=16)
        for i in range(4):
            dd = b.op("gpsimd", "dma_start",
                      bass.AP(dr["o_out"], 4096 * 128 * i if False else 4 * 128 * 1024 * i,
                              [[1024, 128], [128 * 1024, 4], [1, 1024]]),
                      bass.AP(RES, 4096 * i, [[RES.shape[1], 128], [1024, 4], [1, 1024]]),
                      inc=S["dw"], inc_by=16)
        if dbg == 2:
            dd = b.op("gpsimd", "dma_start", dr["o_vb"].ap(),
                      bass.AP(POOL, 0, [[POOL.shape[1], 128], [1, 16640]]),
                      inc=S["dw"], inc_by=16)
            dd = b.op("gpsimd", "dma_start", dr["o_kq"].ap(),
                      bass.AP(POOL, 33024, [[POOL.shape[1], 128], [1, 1024]]),
                      inc=S["dw"], inc_by=16)
            dd = b.op("gpsimd", "dma_start", dr["o_brow"].ap(), st["BROW"][:, :],
                      inc=S["dw"], inc_by=16)
            dd = b.op("gpsimd", "dma_start", dr["o_w1"].ap(),
                      bass.AP(POOL, 16640, [[POOL.shape[1], 128], [1, 8192]]),
                      inc=S["dw"], inc_by=16)
        b.wait("gpsimd", S["dw"], dd)

    b.render()
    ctx.close()
    b.close_sems()
    return nc



def ln_stats(b, S, STAT, src_ap_halves, waits_v=()):
    ve, ac = S["ve"], S["ac"]
    b.op("vector", "bn_stats", STAT[:, 4:10], src_ap_halves[0], waits=waits_v)
    b.op("vector", "bn_stats", STAT[:, 10:16], src_ap_halves[1])
    b.op("vector", "drain")
    b.op("vector", "bn_aggr", STAT[:, 0:2], STAT[:, 4:16])
    b.op("vector", "drain")
    b.op("vector", "tensor_scalar_add", STAT[:, 2:3], STAT[:, 1:2], EPS)
    b.op("vector", "drain")
    b.op("vector", "reciprocal", STAT[:, 2:3], STAT[:, 2:3])
    v1 = b.op("vector", "drain", inc=ve)
    a1 = b.op("scalar", "activation", STAT[:, 2:3], STAT[:, 2:3], AF.Sqrt,
              waits=[(ve, v1)], inc=ac)
    b.op("vector", "scalar_tensor_tensor", STAT[:, 3:4], STAT[:, 0:2][:, 0:1], -1.0,
         STAT[:, 2:3], ALU.mult, ALU.mult, waits=[(ac, a1)])
    v2 = b.op("vector", "drain", inc=ve)
    return v2


def preln_batch(b, st, x_src, tile_ap, xnb_dst_fn):
    S, STAT, RW, XNB = st["S"], st["STAT"], st["RW"], st["XNB"]
    ve, ac, pe = S["ve"], S["ac"], S["pe"]
    for t in range(NTC):
        b.op("vector", "bn_stats", RW[:, 12 * t:12 * t + 6],
             x_src[:, 1024 * t:1024 * t + 512])
        b.op("vector", "bn_stats", RW[:, 12 * t + 6:12 * t + 12],
             x_src[:, 1024 * t + 512:1024 * t + 1024])
    b.op("vector", "drain")
    for t in range(NTC):
        b.op("vector", "bn_aggr", STAT[:, 2 * t:2 * t + 2], RW[:, 12 * t:12 * t + 12])
    b.op("vector", "drain")
    varv = bass.AP(STAT, 1, [[STAT.shape[1], 128], [2, 16]])
    meanv = bass.AP(STAT, 0, [[STAT.shape[1], 128], [2, 16]])
    b.op("vector", "tensor_scalar_add", STAT[:, 32:48], varv, EPS)
    b.op("vector", "drain")
    b.op("vector", "reciprocal", STAT[:, 32:48], STAT[:, 32:48])
    v1 = b.op("vector", "drain", inc=ve)
    a1 = b.op("scalar", "activation", STAT[:, 32:48], STAT[:, 32:48], AF.Sqrt,
              waits=[(ve, v1)], inc=ac)
    b.op("vector", "scalar_tensor_tensor", STAT[:, 48:64], meanv, -1.0,
         STAT[:, 32:48], ALU.mult, ALU.mult, waits=[(ac, a1)])
    b.op("vector", "drain")
    done_ac = {}
    for t in range(NTC):
        v3 = b.op("vector", "tensor_scalar", tile_ap[:, 0:1024],
                  x_src[:, 1024 * t:1024 * t + 1024],
                  STAT[:, 32 + t:33 + t], STAT[:, 48 + t:49 + t], ALU.mult, ALU.add,
                  waits=[(pe, st.get("_tp_pe", 0))], inc=ve)
        for f in range(NFC):
            wl = [(ve, v3)]
            if f >= 2:
                wl.append((ac, st["_tp_ac"][f % 2]))
            a = transpose_tile(b, st, tile_ap[:, 128 * f:128 * f + 128],
                               xnb_dst_fn(t, f), f, waits_t=wl)
            st["_tp_ac"][f % 2] = a
        done_ac[t] = max(st["_tp_ac"][0], st["_tp_ac"][1])
    return done_ac


def transpose_tile(b, st, src_bf16_ap, dst_bf16_ap, tp_idx, waits_t=(), m=128):
    S, TPS, IDBF = st["S"], st["TPS"], st["IDBF"]
    pe, ac = S["pe"], S["ac"]
    tb = TPS[tp_idx % 2]
    p1 = b.op("tensor", "transpose", tb[0:m, 0:128], src_bf16_ap, IDBF[:, :],
              waits=waits_t, inc=pe)
    a1 = b.op("scalar", "copy", dst_bf16_ap, tb[0:m, 0:128], waits=[(pe, p1)], inc=ac)
    st["_tp_pe"] = p1
    return a1


def preln_chunk(b, st, t, x_src, xn_dst_tile, emit_transposes=True):
    S, STAT, XNB = st["S"], st["STAT"], st["XNB"]
    ve, ac = S["ve"], S["ac"]
    v2 = ln_stats(b, S, STAT,
                  (x_src[:, 1024 * t:1024 * t + 512], x_src[:, 1024 * t + 512:1024 * t + 1024]))
    v3 = b.op("vector", "tensor_scalar", xn_dst_tile[:, 0:1024],
              x_src[:, 1024 * t:1024 * t + 1024],
              STAT[:, 2:3], STAT[:, 3:4], ALU.mult, ALU.add,
              waits=[(S["pe"], st.get("_tp_pe", 0))], inc=ve)
    if emit_transposes:
        last = None
        for f in range(NFC):
            waits_t = [(ve, v3)]
            if f >= 2:
                waits_t.append((S["ac"], st["_tp_ac"][f % 2]))
            a = transpose_tile(b, st, xn_dst_tile[:, 128 * f:128 * f + 128],
                               XNB[:, 2048 * f + 128 * t:2048 * f + 128 * t + 128],
                               f, waits_t=waits_t)
            st["_tp_ac"][f % 2] = a
    return v3


def postln_residual_chunk(b, st, t, psum_pair, waits_v=()):
    S, STAT, SMALL, RES = st["S"], st["STAT"], st["SMALL"], st["RES"]
    ve = S["ve"]
    Pa, Pb = psum_pair
    v2 = ln_stats(b, S, STAT, (Pa[:, 0:512], Pb[:, 0:512]), waits_v=waits_v)
    b.op("vector", "tensor_scalar", SMALL[:, 0:512], Pa[:, 0:512],
         STAT[:, 2:3], STAT[:, 3:4], ALU.mult, ALU.add)
    v3 = b.op("vector", "tensor_scalar", SMALL[:, 512:1024], Pb[:, 0:512],
              STAT[:, 2:3], STAT[:, 3:4], ALU.mult, ALU.add, inc=ve)
    v4 = b.op("vector", "tensor_tensor", RES[:, 1024 * t:1024 * t + 1024],
              RES[:, 1024 * t:1024 * t + 1024], SMALL[:, 0:1024], ALU.add, inc=ve)
    return v3, v4



def prog(st, n_sub):
    b, S, dr = st["b"], st["S"], st["dr"]
    nc = st["nc"]
    P, TPS = st["P"], st["TPS"]
    RES, RESB, XNB, POOL, SMALL, STAT = (st["RES"], st["RESB"], st["XNB"],
                                         st["POOL"], st["SMALL"], st["STAT"])
    ONES, IDBF, BROW = st["ONES"], st["IDBF"], st["BROW"]
    dw, da, pe, ve, ac = S["dw"], S["da"], S["pe"], S["ve"], S["ac"]
    st["_tp_ac"] = [0, 0]

    WA, WB, WC = 0, 8192, 16384
    REST = 24576

    xin = 0
    for i in range(4):
        dst = bass.AP(RES, 4096 * i, [[RES.shape[1], 128], [1024, 4], [1, 1024]])
        src = bass.AP(dr["x"], 4 * 128 * 1024 * i, [[1024, 128], [128 * 1024, 4], [1, 1024]])
        xin = b.op("sync", "dma_start", dst, src, inc=da, inc_by=16)
    b.op("sync", "dma_start", IDBF[:, :], dr["c_id"].ap(), inc=da, inc_by=16)
    cosin = 0
    for i in range(2):
        dst = bass.AP(POOL, WA + 4096 * i, [[POOL.shape[1], 128], [1024, 4], [1, 1024]])
        src = bass.AP(dr["c_cos"], 4 * 128 * 1024 * i, [[1024, 128], [128 * 1024, 4], [1, 1024]])
        b.op("sync", "dma_start", dst, src, inc=da, inc_by=16)
        dst = bass.AP(POOL, WB + 4096 * i, [[POOL.shape[1], 128], [1024, 4], [1, 1024]])
        src = bass.AP(dr["c_sin"], 4 * 128 * 1024 * i, [[1024, 128], [128 * 1024, 4], [1, 1024]])
        cosin = b.op("sync", "dma_start", dst, src, inc=da, inc_by=16)
    ohc = b.op("sync", "dma_start",
               bass.AP(POOL, REST, [[POOL.shape[1], 32], [1, 4095]]),
               dr["c_oh"].ap(), inc=da, inc_by=16)
    relc = b.op("gpsimd", "dma_start",
                bass.AP(POOL, REST + 4096, [[POOL.shape[1], 32], [1, 16]]),
                dr["L1_attn_rel_embed"].ap(), inc=dw, inc_by=16)
    w_wr = 0
    w_wr += w_chunk_dma(b, (POOL, WC), dr["L0_ffn_w_freq_real"], D, D, dw)
    w_wi_base = REST + 8192
    w_wr += w_chunk_dma(b, (XNB, 0), dr["L0_ffn_w_freq_imag"], D, D, dw)
    wr_done = b.cnt[dw]
    vones = b.op("vector", "memset", ONES[:, :], 1.0, inc=ve)

    PBB0 = REST + 5120
    pbecnt = []
    for i in range(8):
        nwide = 512 if i < 7 else 4095 - 7 * 512
        bank = P[i % 2]
        waits = [(dw, relc), (da, ohc)]
        if i >= 2:
            waits.append((ac, pbecnt[i - 2]))
        pcur = b.op("tensor", "matmul", bank[0:16, 0:nwide],
                    bass.AP(POOL, REST + 4096, [[POOL.shape[1], 32], [1, 16]]),
                    bass.AP(POOL, REST + 512 * i, [[POOL.shape[1], 32], [1, nwide]]),
                    start=True, stop=True, waits=waits, inc=pe)
        a = b.op("scalar", "copy",
                 bass.AP(POOL, PBB0 + 512 * i, [[POOL.shape[1], 16], [1, nwide]]),
                 bank[0:16, 0:nwide], waits=[(pe, pcur)], inc=ac)
        pbecnt.append(a)
    pbh = b.op("sync", "dma_start", dr["h_pb"].ap(),
               bass.AP(POOL, PBB0, [[POOL.shape[1], 16], [1, 4095]]),
               waits=[(ac, pbecnt[-1])], inc=da, inc_by=16)
    st["pbh"] = pbh

    g1 = b.op("gpsimd", "dma_start", st["GRUW"][0:64, 0:8], dr["L1_attn_gru_w"].ap(),
              inc=dw, inc_by=16)
    g2 = b.op("gpsimd", "dma_start", st["GRUW"][64:128, 0:8], dr["L1_attn_gru_w"].ap(),
              inc=dw, inc_by=16)
    g3 = b.op("gpsimd", "dma_start", st["GRUB"][0:1, 0:8],
              bass.AP(dr["L1_attn_gru_b"], 0, [[8, 1], [1, 8]]), inc=dw, inc_by=16)
    g4 = b.op("gpsimd", "dma_start", st["C16"][0:1, 0:16],
              bass.AP(dr["L1_attn_gru_const"], 0, [[16, 1], [1, 16]]), inc=dw, inc_by=16)
    vg = b.op("vector", "tensor_reduce", st["GW2B"][:, 0:2],
              bass.AP(st["GRUW"], 0, [[st["GRUW"].shape[1], 128], [4, 2], [1, 4]]),
              axis=AX.X, op=ALU.add, waits=[(dw, g2)], lp="pair sums to bf16")
    b.op("vector", "drain")
    vg2 = b.op("vector", "tensor_reduce", st["STAT"][0:1, 32:34],
               bass.AP(st["GRUB"], 0, [[GRUBROW := st["GRUB"].shape[1], 1], [4, 2], [1, 4]]),
               axis=AX.X, op=ALU.add, waits=[(dw, g3)])
    vg3 = b.op("vector", "drain", inc=ve)
    a_g1 = b.op("scalar", "copy", st["GB2R"][0:1, 0:2], st["STAT"][0:1, 32:34],
                waits=[(ve, vg3)], lp="gb2 bf16")
    a_g2 = b.op("scalar", "copy", st["GB2R"][0:1, 2:4], st["STAT"][0:1, 32:34],
                lp="gb2 bf16")
    vb0 = b.op("vector", "memset", st["BD"][:, :], 0.0, inc=ve)
    b.op("vector", "drain")
    a_bd1 = b.op("scalar", "copy", st["BD"][0:64, 0:2], st["GW2B"][0:64, 0:2],
                 waits=[(ve, b.cnt[ve])])
    a_bd = b.op("scalar", "copy", st["BD"][64:128, 2:4], st["GW2B"][64:128, 0:2], inc=ac)
    st["a_bd"] = a_bd
    a_c16 = b.op("scalar", "copy", st["GREP"][0:1, 0:16], st["C16"][0:1, 0:16],
                 waits=[(dw, g4)], inc=ac, lp="c16 bf16")
    pc = b.op("tensor", "matmul", P[2][0:128, 0:16], ONES[0:1, 0:128],
              st["GREP"][0:1, 0:16], start=True, stop=True,
              waits=[(ac, a_c16), (ve, vones)], inc=pe)
    a_crep = b.op("scalar", "copy", st["CREP"][:, 0:16], P[2][0:128, 0:16],
                  waits=[(pe, pc)], inc=ac)
    st["a_crep"] = a_crep

    MST = REST + 4096
    def mb_stage(src_banks, dst_hbm, m, wait_extra=()):
        a1 = b.op("scalar", "copy",
                  bass.AP(POOL, REST, [[POOL.shape[1], 128], [1, 512]]),
                  src_banks[0][:, 0:512], waits=list(wait_extra), inc=ac)
        a2 = b.op("scalar", "copy",
                  bass.AP(POOL, REST + 512, [[POOL.shape[1], 128], [1, 512]]),
                  src_banks[1][:, 0:512], inc=ac)
        d = b.op("sync", "dma_start", dst_hbm[128 * m:128 * m + 128, :],
                 bass.AP(POOL, REST, [[POOL.shape[1], 128], [1, 1024]]),
                 waits=[(ac, a2)], inc=da, inc_by=16)
        return a2, d

    if n_sub >= 2:
        last_stage_d = 0
        pt_evac = {}
        for m in range(8):
            for half in range(2):
                bank = P[half]
                for f in range(8):
                    wl = []
                    if f == 0 and half == 0:
                        wl = [(dw, wr_done), (da, cosin)]
                        if m == 0:
                            wl.append((ac, pbecnt[-1]))
                        if m >= 1:
                            wl.append((ac, pt_evac[m - 1]))
                    b.op("tensor", "matmul", bank[:, 0:512],
                         bass.AP(POOL, WC + 1024 * f + 128 * m,
                                 [[POOL.shape[1], 128], [1, 128]]),
                         bass.AP(POOL, WA + 1024 * f + 512 * half,
                                 [[POOL.shape[1], 128], [1, 512]]),
                         start=(f == 0), stop=False, waits=wl)
                    pcur = b.op("tensor", "matmul", bank[:, 0:512],
                                bass.AP(XNB, 1024 * f + 128 * m,
                                        [[XNB.shape[1], 128], [1, 128]]),
                                bass.AP(POOL, WB + 1024 * f + 512 * half,
                                        [[POOL.shape[1], 128], [1, 512]]),
                                start=False, stop=(f == 7), inc=(pe if f == 7 else None))
                b.wait("scalar", pe, pcur)
            a2, last_stage_d = mb_stage([P[0], P[1]], dr["h_PT"], m,
                                        wait_extra=([(da, last_stage_d)] if m >= 1 else []))
            pt_evac[m] = a2
        qt_v = {}
        for m in range(8):
            for half in range(2):
                for f in range(8):
                    wl = []
                    if f == 0 and half == 0 and m >= 1:
                        wl.append((ve, qt_v[m - 1]))
                    b.op("tensor", "matmul", P[2 + half][:, 0:512],
                         bass.AP(XNB, 1024 * f + 128 * m, [[XNB.shape[1], 128], [1, 128]]),
                         bass.AP(POOL, WA + 1024 * f + 512 * half,
                                 [[POOL.shape[1], 128], [1, 512]]),
                         start=(f == 0), stop=(f == 7), waits=wl)
                    pcur = b.op("tensor", "matmul", P[4 + half][:, 0:512],
                                bass.AP(POOL, WC + 1024 * f + 128 * m,
                                        [[POOL.shape[1], 128], [1, 128]]),
                                bass.AP(POOL, WB + 1024 * f + 512 * half,
                                        [[POOL.shape[1], 128], [1, 512]]),
                                start=(f == 0), stop=(f == 7),
                                inc=(pe if f == 7 else None))
            b.wait("vector", pe, pcur)
            b.wait("vector", da, last_stage_d)
            a1 = b.op("scalar", "copy", SMALL[:, 0:512], P[2][:, 0:512],
                      waits=[(pe, pcur)], inc=ac)
            a2 = b.op("scalar", "copy", SMALL[:, 512:1024], P[3][:, 0:512], inc=ac)
            b.wait("vector", ac, a2)
            b.op("vector", "scalar_tensor_tensor",
                 bass.AP(POOL, REST, [[POOL.shape[1], 128], [1, 512]]),
                 P[4][:, 0:512], -1.0, SMALL[:, 0:512], ALU.mult, ALU.add)
            v = b.op("vector", "scalar_tensor_tensor",
                     bass.AP(POOL, REST + 512, [[POOL.shape[1], 128], [1, 512]]),
                     P[5][:, 0:512], -1.0, SMALL[:, 512:1024], ALU.mult, ALU.add, inc=ve)
            qt_v[m] = v
            last_stage_d = b.op("sync", "dma_start", dr["h_QT"][128 * m:128 * m + 128, :],
                                bass.AP(POOL, REST, [[POOL.shape[1], 128], [1, 1024]]),
                                waits=[(ve, v)], inc=da, inc_by=16)
        for m in range(8):
            dl = []
            for which, hbm in ((0, dr["h_PT"]), (1, dr["h_QT"])):
                for g in range(2):
                    dd = b.op("sync", "dma_start",
                              bass.AP(POOL, REST + 1024 + 1024 * which + 512 * g,
                                      [[POOL.shape[1], 128], [128, 4], [1, 128]]),
                              bass.AP(hbm, 128 * m + 4 * g * 128 * 1024,
                                      [[1024, 128], [128 * 1024, 4], [1, 128]]),
                              waits=[(da, last_stage_d)] if (which == 0 and g == 0) else [],
                              inc=da, inc_by=16)
                    dl.append(dd)
            for half in range(2):
                for f in range(8):
                    wl = [(da, dl[-1])] if f == 0 and half == 0 else []
                    b.op("tensor", "matmul", P[0 + half][:, 0:512],
                         bass.AP(POOL, REST + 1024 + 128 * f,
                                 [[POOL.shape[1], 128], [1, 128]]),
                         bass.AP(POOL, WA + 1024 * f + 512 * half,
                                 [[POOL.shape[1], 128], [1, 512]]),
                         start=(f == 0), stop=(f == 7), waits=wl)
                    pcur = b.op("tensor", "matmul", P[2 + half][:, 0:512],
                                bass.AP(POOL, REST + 2048 + 128 * f,
                                        [[POOL.shape[1], 128], [1, 128]]),
                                bass.AP(POOL, WB + 1024 * f + 512 * half,
                                        [[POOL.shape[1], 128], [1, 512]]),
                                start=(f == 0), stop=(f == 7),
                                inc=(pe if f == 7 else None))
            b.wait("scalar", pe, pcur)
            a1 = b.op("scalar", "activation", SMALL[:, 0:512], P[0][:, 0:512],
                      AF.Copy, bias=0.0, scale=1.0 / D, waits=[(pe, pcur)], inc=ac)
            a2 = b.op("scalar", "activation", SMALL[:, 512:1024], P[1][:, 0:512],
                      AF.Copy, bias=0.0, scale=1.0 / D, inc=ac)
            b.wait("vector", ac, a2)
            b.op("vector", "scalar_tensor_tensor",
                 bass.AP(POOL, REST, [[POOL.shape[1], 128], [1, 512]]),
                 P[2][:, 0:512], -1.0 / D, SMALL[:, 0:512], ALU.mult, ALU.add)
            v = b.op("vector", "scalar_tensor_tensor",
                     bass.AP(POOL, REST + 512, [[POOL.shape[1], 128], [1, 512]]),
                     P[3][:, 0:512], -1.0 / D, SMALL[:, 512:1024], ALU.mult, ALU.add,
                     inc=ve)
            last_stage_d = b.op("sync", "dma_start", dr["h_M"][128 * m:128 * m + 128, :],
                                bass.AP(POOL, REST, [[POOL.shape[1], 128], [1, 1024]]),
                                waits=[(ve, v)], inc=da, inc_by=16)
        st["m_done_da"] = last_stage_d
    else:
        st["m_done_da"] = 0
    mb_pe = b.cnt[pe]

    b.wait("vector", da, xin)
    if n_sub >= 2:
        b.wait("vector", pe, mb_pe)
        b.wait("scalar", pe, mb_pe)
        b.wait("gpsimd", pe, mb_pe)
    XT1 = 34048
    preln_batch(b, st, RES, bass.AP(POOL, XT1, [[POOL.shape[1], 128], [1, 1024]]),
                lambda t, f: XNB[:, 2048 * f + 128 * t:2048 * f + 128 * t + 128])
    st["preln0_ac"] = b.cnt[ac]

    l0_attn(st, WA, WB, WC, REST)
    if n_sub >= 2:
        l0_ffn(st, WA, WB, WC, REST)
    if n_sub >= 3:
        l1_attn(st, WA, WB, WC, REST)
    if n_sub >= 4:
        l1_ffn(st, WA, WB, WC, REST)

    if n_sub >= 3:
        pass
    else:
        vv = b.cnt[ve]
        dy = 0
        for i in range(4):
            src = bass.AP(RES, 4096 * i, [[RES.shape[1], 128], [1024, 4], [1, 1024]])
            dst = bass.AP(dr["y"], 4 * 128 * 1024 * i, [[1024, 128], [128 * 1024, 4], [1, 1024]])
            dy = b.op("sync", "dma_start", dst, src, waits=[(ve, vv)] if i == 0 else [],
                      inc=da, inc_by=16)
        b.wait("sync", da, dy)



def l0_attn(st, WA, WB, WC, REST):
    b, S, dr, P = st["b"], st["S"], st["dr"], st["P"]
    POOL, XNB, RES, SMALL, STAT = st["POOL"], st["XNB"], st["RES"], st["SMALL"], st["STAT"]
    ONES, BROW, KVS, KVACC = st["ONES"], st["BROW"], st["KVS"], st["KVACC"]
    dw, da, pe, ve, ac = S["dw"], S["da"], S["pe"], S["ve"], S["ac"]
    nc = st["nc"]

    VB = REST
    TIL = REST + 16640
    T_KQ = TIL
    T_QT = TIL + 1024
    T_SO = TIL + 2048
    T_G = TIL + 3072
    T_GS = TIL + 4096
    T_GT = TIL + 5120
    VB = 0
    W1 = 16640
    W2 = 24832
    TIL = 33024
    T_KQ = TIL
    T_QT = TIL + 1024
    T_G = VB
    T_GS = VB + 1024
    T_GT = VB + 2048
    T_SO = VB + 3072

    def pool_ap(col, width, parts=128, p0=0):
        return bass.AP(POOL, p0 * POOL.shape[1] + col,
                       [[POOL.shape[1], parts], [1, width]])

    b.wait("vector", da, st["pbh"])
    b.wait("scalar", da, st["pbh"])
    b.wait("gpsimd", da, st["pbh"])
    b.wait("gpsimd", da, st["m_done_da"])

    wk_d = w_chunk_dma(b, (POOL, W1), dr["L0_attn_wk_w"], D, D, dw)
    wk_done = b.cnt[dw]
    bk_d = b.op("gpsimd", "dma_start", BROW[0:1, 0:1024],
                bass.AP(dr["L0_attn_wk_b"], 0, [[D, 1], [1, D]]), inc=dw, inc_by=16)
    w_chunk_dma(b, (POOL, W2), dr["L0_attn_wv_w"], D, D, dw)
    wv_done = b.cnt[dw]
    bv_d = b.op("gpsimd", "dma_start", BROW[0:1, 1024:2048],
                bass.AP(dr["L0_attn_wv_b"], 0, [[D, 1], [1, D]]), inc=dw, inc_by=16)

    vvb = b.op("vector", "memset",
               bass.AP(POOL, VB + 64, [[POOL.shape[1], 128], [1040, 16], [65, 16], [1, 1]]),
               1.0, inc=ve)

    kvacc_init = [False] * 8
    evac_k_ve = 0
    evac_v_ac = 0
    kv_dve = {}
    for t in range(NTC):
        for half in range(2):
            for f in range(NFC):
                wl = []
                if t == 0 and half == 0 and f == 0:
                    wl = [(dw, wk_done + 16), (ac, st["preln0_ac"])]
                if t > 0 and half == 0 and f == 0:
                    wl = [(ve, kv_dve.get(("kdone", t - 1), 0))]
                b.op("tensor", "matmul", P[half][:, 0:512],
                     bass.AP(XNB, 2048 * f + 128 * t, [[XNB.shape[1], 128], [1, 128]]),
                     bass.AP(POOL, W1 + 1024 * f + 512 * half,
                             [[POOL.shape[1], 128], [1, 512]]),
                     start=(f == 0), stop=False, waits=wl)
            pk = b.op("tensor", "matmul", P[half][:, 0:512], ONES[0:1, 0:128],
                      BROW[0:1, 512 * half:512 * half + 512],
                      start=False, stop=True, inc=pe)
        b.wait("vector", pe, pk)
        b.op("vector", "tensor_scalar_min", SMALL[:, 0:512], P[0][:, 0:512], 0.0)
        v1 = b.op("vector", "tensor_scalar_min", SMALL[:, 512:1024], P[1][:, 0:512],
                  0.0, inc=ve)
        a1 = b.op("scalar", "activation", SMALL[:, 1024:1536], SMALL[:, 0:512],
                  AF.Exp, waits=[(ve, v1)])
        a2 = b.op("scalar", "activation", SMALL[:, 1536:2048], SMALL[:, 512:1024],
                  AF.Exp, inc=ac)
        b.wait("vector", ac, a2)
        b.op("vector", "scalar_tensor_tensor", pool_ap(T_KQ, 512),
             P[0][:, 0:512], 0.0, SMALL[:, 1024:1536], ALU.max, ALU.add,
             lp="bf16 matmul operand")
        v2 = b.op("vector", "scalar_tensor_tensor", pool_ap(T_KQ + 512, 512),
                  P[1][:, 0:512], 0.0, SMALL[:, 1536:2048], ALU.max, ALU.add,
                  inc=ve, lp="bf16 matmul operand")
        kv_dve[("kphi", t)] = v2
        kv_dve[("kdone", t)] = v2
        for half in range(2):
            for f in range(NFC):
                wl = []
                if t == 0 and half == 0 and f == 0:
                    wl = [(dw, wv_done + 16)]
                if t > 0 and half == 0 and f == 0:
                    wl = [(ac, kv_dve.get(("vdone", t - 1), 0))]
                b.op("tensor", "matmul", P[2 + half][:, 0:512],
                     bass.AP(XNB, 2048 * f + 128 * t, [[XNB.shape[1], 128], [1, 128]]),
                     bass.AP(POOL, W2 + 1024 * f + 512 * half,
                             [[POOL.shape[1], 128], [1, 512]]),
                     start=(f == 0), stop=False, waits=wl)
            pv = b.op("tensor", "matmul", P[2 + half][:, 0:512], ONES[0:1, 0:128],
                      BROW[0:1, 1024 + 512 * half:1536 + 512 * half],
                      start=False, stop=True, inc=pe)
        wl = [(pe, pv)]
        if t == 0:
            wl.append((ve, vvb))
        b.op("scalar", "copy",
             bass.AP(POOL, VB + 1040 * t, [[POOL.shape[1], 128], [65, 8], [1, 64]]),
             P[2][:, 0:512], waits=wl, lp="bf16 v")
        av = b.op("scalar", "copy",
                  bass.AP(POOL, VB + 1040 * t + 520, [[POOL.shape[1], 128], [65, 8], [1, 64]]),
                  P[3][:, 0:512], inc=ac, lp="bf16 v")
        kv_dve[("vdone", t)] = av
        for j in range(8):
            bank = P[4 + (j % 2)]
            wl = [(ve, v2), (ac, av)]
            if (t * 8 + j) >= 2:
                wl.append((ve, kv_dve[("kvadd", t * 8 + j - 2)]))
            pj = b.op("tensor", "matmul", bank[:, 0:130],
                      pool_ap(T_KQ + 128 * j, 128),
                      bass.AP(POOL, VB + 1040 * t + 130 * j,
                              [[POOL.shape[1], 128], [1, 130]]),
                      start=True, stop=True, waits=wl, inc=pe)
            if not kvacc_init[j]:
                vv = b.op("vector", "tensor_copy", KVACC[:, 130 * j:130 * j + 130],
                          bank[:, 0:130], waits=[(pe, pj)], inc=ve)
                kvacc_init[j] = True
            else:
                vv = b.op("vector", "tensor_tensor", KVACC[:, 130 * j:130 * j + 130],
                          KVACC[:, 130 * j:130 * j + 130], bank[:, 0:130], ALU.add,
                          waits=[(pe, pj)], inc=ve)
            kv_dve[("kvadd", t * 8 + j)] = vv
    vlast = kv_dve[("kvadd", NTC * 8 - 1)]
    b.wait("scalar", ve, vlast)
    for j in range(8):
        b.op("scalar", "copy", KVS[0:64, 65 * j:65 * j + 65],
             KVACC[0:64, 130 * j:130 * j + 65], lp="bf16 kv")
        akv = b.op("scalar", "copy", KVS[64:128, 65 * j:65 * j + 65],
                   KVACC[64:128, 130 * j + 65:130 * j + 130], inc=ac, lp="bf16 kv")

    if st.get("dbg") == 2:
        st["l0a_done_ve"] = b.cnt[ve]
        return
    last_pe_pass1 = b.cnt[pe]
    b.wait("gpsimd", pe, last_pe_pass1)
    w_chunk_dma(b, (POOL, W1), dr["L0_attn_wq_w"], D, D, dw)
    wq_done = b.cnt[dw]
    b.op("gpsimd", "dma_start", BROW[0:1, 0:1024],
         bass.AP(dr["L0_attn_wq_b"], 0, [[D, 1], [1, D]]), inc=dw, inc_by=16)
    w_chunk_dma(b, (POOL, W2), dr["L0_attn_wg_w"], D, D, dw)
    wg_done = b.cnt[dw]
    b.op("gpsimd", "dma_start", BROW[0:1, 1024:2048],
         bass.AP(dr["L0_attn_wg_b"], 0, [[D, 1], [1, D]]), inc=dw, inc_by=16)
    OW = VB + 4096
    b.wait("gpsimd", ac, akv)
    w_chunk_dma(b, (POOL, OW), dr["L0_attn_out_w"], D, D, dw)
    ow_done = b.cnt[dw]
    ob_done = b.op("gpsimd", "dma_start", st["GREP"][0:1, 2048:3072],
                   bass.AP(dr["L0_attn_out_b"], 0, [[D, 1], [1, D]]), inc=dw, inc_by=16)

    p2state = {}
    for t in range(NTC):
        for half in range(2):
            for f in range(NFC):
                wl = []
                if t == 0 and half == 0 and f == 0:
                    wl = [(dw, wq_done + 16), (ve, vlast)]
                if t > 0 and half == 0 and f == 0:
                    wl = [(ac, p2state.get(("qT", t - 1), 0))]
                b.op("tensor", "matmul", P[half][:, 0:512],
                     bass.AP(XNB, 2048 * f + 128 * t, [[XNB.shape[1], 128], [1, 128]]),
                     bass.AP(POOL, W1 + 1024 * f + 512 * half,
                             [[POOL.shape[1], 128], [1, 512]]),
                     start=(f == 0), stop=False, waits=wl)
            pq = b.op("tensor", "matmul", P[half][:, 0:512], ONES[0:1, 0:128],
                      BROW[0:1, 512 * half:512 * half + 512],
                      start=False, stop=True, inc=pe)
        b.wait("vector", pe, pq)
        b.op("vector", "tensor_scalar_min", SMALL[:, 0:512], P[0][:, 0:512], 0.0)
        v1 = b.op("vector", "tensor_scalar_min", SMALL[:, 512:1024], P[1][:, 0:512],
                  0.0, inc=ve)
        a1 = b.op("scalar", "activation", SMALL[:, 1024:1536], SMALL[:, 0:512],
                  AF.Exp, waits=[(ve, v1)])
        a2 = b.op("scalar", "activation", SMALL[:, 1536:2048], SMALL[:, 512:1024],
                  AF.Exp, inc=ac)
        b.wait("vector", ac, a2)
        b.op("vector", "scalar_tensor_tensor", pool_ap(T_KQ, 512),
             P[0][:, 0:512], 0.0, SMALL[:, 1024:1536], ALU.max, ALU.add,
             lp="bf16 matmul operand")
        v2 = b.op("vector", "scalar_tensor_tensor", pool_ap(T_KQ + 512, 512),
                  P[1][:, 0:512], 0.0, SMALL[:, 1536:2048], ALU.max, ALU.add,
                  inc=ve, lp="bf16 matmul operand")
        for f in range(NFC):
            wl = [(ve, v2)]
            if f >= 2:
                wl.append((ac, st["_tp_ac"][f % 2]))
            a = transpose_tile(b, st, pool_ap(T_KQ + 128 * f, 128),
                               pool_ap(T_QT + 128 * f, 128), f, waits_t=wl)
            st["_tp_ac"][f % 2] = a
        aqT = st["_tp_ac"][(NFC - 1) % 2]
        aqT2 = st["_tp_ac"][(NFC - 2) % 2]
        p2state[("qT", t)] = max(aqT, aqT2)
        for half in range(2):
            for f in range(NFC):
                wl = []
                if t == 0 and half == 0 and f == 0:
                    wl = [(dw, wg_done + 16)]
                if t > 0 and half == 0 and f == 0:
                    wl = [(ve, p2state.get(("gs", t - 1), 0))]
                b.op("tensor", "matmul", P[2 + half][:, 0:512],
                     bass.AP(XNB, 2048 * f + 128 * t, [[XNB.shape[1], 128], [1, 128]]),
                     bass.AP(POOL, W2 + 1024 * f + 512 * half,
                             [[POOL.shape[1], 128], [1, 512]]),
                     start=(f == 0), stop=False, waits=wl)
            pg = b.op("tensor", "matmul", P[2 + half][:, 0:512], ONES[0:1, 0:128],
                      BROW[0:1, 1024 + 512 * half:1536 + 512 * half],
                      start=False, stop=True, inc=pe)
        b.wait("scalar", pe, pg)
        b.op("scalar", "activation", pool_ap(T_G, 512), P[2][:, 0:512], AF.Sigmoid,
             lp="bf16 gate")
        ag = b.op("scalar", "activation", pool_ap(T_G + 512, 512), P[3][:, 0:512],
                  AF.Sigmoid, inc=ac, lp="bf16 gate")
        sc_evac = {}
        for h in range(16):
            bank = P[2 + (h % 2)]
            p0 = 64 * (h % 2)
            hh = h // 2
            wl = [(ac, ag)]
            if h >= 2:
                wl.append((ac, sc_evac[h - 2]))
            psc = b.op("tensor", "matmul", bank[:, 0:65],
                       pool_ap(T_QT + 128 * hh, 128, parts=64, p0=p0),
                       bass.AP(KVS, p0 * KVS.shape[1] + 65 * hh,
                               [[KVS.shape[1], 64], [1, 65]]),
                       start=True, stop=True, waits=wl, inc=pe)
            sc_evac[h] = b.op("scalar", "copy", SMALL[:, 65 * h:65 * h + 65],
                              bank[:, 0:65], waits=[(pe, psc)], inc=ac)
        b.wait("vector", ac, sc_evac[15])
        densv = bass.AP(SMALL, 64, [[SMALL.shape[1], 128], [65, 16]])
        b.op("vector", "tensor_scalar_add", STAT[:, 20:36], densv, 1e-6)
        b.op("vector", "drain")
        b.op("vector", "reciprocal", STAT[:, 20:36], STAT[:, 20:36])
        b.op("vector", "drain")
        for h in range(16):
            vs = b.op("vector", "tensor_scalar", pool_ap(T_SO + 64 * h, 64),
                      SMALL[:, 65 * h:65 * h + 64], STAT[:, 20 + h:21 + h], None,
                      ALU.mult, inc=(ve if h == 15 else None), lp="bf16 so")
        vgs = b.op("vector", "tensor_tensor", pool_ap(T_GS, 1024), pool_ap(T_G, 1024),
                   pool_ap(T_SO, 1024), ALU.mult, inc=ve, lp="bf16 gs")
        p2state[("gs", t)] = vgs
        for f in range(NFC):
            wl = [(ve, vgs)]
            if f >= 2:
                wl.append((ac, st["_tp_ac"][f % 2]))
            a = transpose_tile(b, st, pool_ap(T_GS + 128 * f, 128),
                               pool_ap(T_GT + 128 * f, 128), f, waits_t=wl)
            st["_tp_ac"][f % 2] = a
        agT = max(st["_tp_ac"][0], st["_tp_ac"][1])
        for half in range(2):
            for f in range(NFC):
                wl = [(ac, agT), (dw, ow_done + 16)] if f == 0 else []
                if f == 0 and t > 0:
                    wl.append((ve, p2state.get(("postv3", t - 1), 0)))
                b.op("tensor", "matmul", P[4 + half][:, 0:512],
                     pool_ap(T_GT + 128 * f, 128),
                     bass.AP(POOL, OW + 1024 * f + 512 * half,
                             [[POOL.shape[1], 128], [1, 512]]),
                     start=(f == 0), stop=False, waits=wl)
            po = b.op("tensor", "matmul", P[4 + half][:, 0:512], ONES[0:1, 0:128],
                      st["GREP"][0:1, 2048 + 512 * half:2560 + 512 * half],
                      start=False, stop=True, inc=pe)
        b.wait("vector", pe, po)
        v3, v4 = postln_residual_chunk(b, st, t, (P[4], P[5]))
        p2state[("postv3", t)] = v3
        p2state[("postln", t)] = v4
    st["l0a_done_ve"] = b.cnt[ve]



def l0_ffn(st, WA, WB, WC, REST):
    b, S, dr, P = st["b"], st["S"], st["dr"], st["P"]
    POOL, XNB, RES, SMALL = st["POOL"], st["XNB"], st["RES"], st["SMALL"]
    ONES, BROW = st["ONES"], st["BROW"]
    dw, da, pe, ve, ac = S["dw"], S["da"], S["pe"], S["ve"], S["ac"]

    b.wait("sync", ve, st["l0a_done_ve"])
    b.wait("sync", da, st["m_done_da"])
    md = 0
    for i in range(2):
        dst = bass.AP(POOL, 4096 * i, [[POOL.shape[1], 128], [1024, 4], [1, 1024]])
        src = bass.AP(dr["h_M"], 4 * 128 * 1024 * i, [[1024, 128], [128 * 1024, 4], [1, 1024]])
        md = b.op("sync", "dma_start", dst, src, inc=da, inc_by=16)
    fb = b.op("gpsimd", "dma_start", BROW[0:1, 0:1024],
              bass.AP(dr["L0_ffn_bias"], 0, [[D, 1], [1, D]]),
              waits=[(ve, st["l0a_done_ve"])], inc=dw, inc_by=16)

    XT1 = 8192
    XNBl = st["XNB"]
    pl_ac = preln_batch(b, st, RES,
                        bass.AP(POOL, XT1, [[POOL.shape[1], 128], [1, 1024]]),
                        lambda tt, ff: XNBl[:, 2048 * ff + 128 * tt:2048 * ff + 128 * tt + 128])
    post = {}
    for t in range(NTC):
        pair = (P[0], P[1]) if t % 2 == 0 else (P[2], P[3])
        for half in range(2):
            for f in range(NFC):
                wl = []
                if f == 0:
                    wl = [(ac, pl_ac[t])]
                    if t == 0 and half == 0:
                        wl += [(da, md), (dw, fb), (ve, st["l0a_done_ve"])]
                    if t == 1 and half == 0:
                        wl.append((ve, st["l0a_done_ve"]))
                    if t >= 2 and half == 0:
                        wl.append((ve, post[t - 2]))
                b.op("tensor", "matmul", pair[half][:, 0:512],
                     bass.AP(XNB, 2048 * f + 128 * t, [[XNB.shape[1], 128], [1, 128]]),
                     bass.AP(POOL, 1024 * f + 512 * half, [[POOL.shape[1], 128], [1, 512]]),
                     start=(f == 0), stop=False, waits=wl)
            pm = b.op("tensor", "matmul", pair[half][:, 0:512], ONES[0:1, 0:128],
                      BROW[0:1, 512 * half:512 * half + 512],
                      start=False, stop=True, inc=pe)
        b.wait("vector", pe, pm)
        v3, v4 = postln_residual_chunk(b, st, t, pair)
        post[t] = v4
    st["l0f_done_ve"] = b.cnt[ve]




def l1_attn(st, WA, WB, WC, REST):
    b, S, dr, P = st["b"], st["S"], st["dr"], st["P"]
    POOL, XNB, RES, RESB = st["POOL"], st["XNB"], st["RES"], st["RESB"]
    SMALL, STAT, RW = st["SMALL"], st["STAT"], st["RW"]
    ONES, BROW, GREP, VH, AVST = st["ONES"], st["BROW"], st["GREP"], st["VH"], st["AVST"]
    RECB, REPS, BD, GB2R, CREP, GATE = (st["RECB"], st["REPS"], st["BD"], st["GB2R"],
                                        st["CREP"], st["GATE"])
    dw, da, pe, ve, ac = S["dw"], S["da"], S["pe"], S["ve"], S["ac"]
    n_sub = st["n_sub"]
    SMALLB = SMALL.bitcast(BF16)

    def pool_ap(col, width, parts=128, p0=0):
        return bass.AP(POOL, p0 * POOL.shape[1] + col,
                       [[POOL.shape[1], parts], [1, width]])

    A0, A1 = 0, 8192
    ROLL = [16384, 20352]
    WSLOT = 24576
    XT1 = 32768
    ABUF = [A0, A1]

    start_ve = st["l0f_done_ve"]
    b.wait("vector", ve, start_ve)

    sp = 0
    for i in range(4):
        src_ap = bass.AP(RES, 4096 * i, [[RES.shape[1], 128], [1024, 4], [1, 1024]])
        dst = bass.AP(dr["h_x0"], 4 * 128 * 1024 * i,
                      [[1024, 128], [128 * 1024, 4], [1, 1024]])
        sp = b.op("sync", "dma_start", dst, src_ap,
                  waits=[(ve, start_ve)] if i == 0 else [], inc=da, inc_by=16)
    st["x0_spill_da"] = sp

    preln_batch(b, st, RES, bass.AP(POOL, XT1, [[POOL.shape[1], 128], [1, 1024]]),
                lambda t, f: XNB[:, 2048 * f + 128 * t:2048 * f + 128 * t + 128])
    preln2_ac = b.cnt[ac]

    bqc = b.op("sync", "dma_start", st["BIASC"][:, 0:8],
               bass.AP(dr["L1_attn_bq"], 0, [[1, 128], [128, 8]]), inc=da, inc_by=16,
               allow_slow_non_contiguous=True)
    bkc = b.op("sync", "dma_start", st["BIASC"][:, 8:16],
               bass.AP(dr["L1_attn_bk"], 0, [[1, 128], [128, 8]]), inc=da, inc_by=16,
               allow_slow_non_contiguous=True)
    wq_d0 = b.cnt[dw]
    w_chunk_dma(b, (POOL, WSLOT), dr["L1_attn_Wq"], D, D, dw)
    wq_done = b.cnt[dw]

    cn_d = b.op("gpsimd", "dma_start", st["KVACC"][0:16, 0:1],
                bass.AP(dr["h_pb"], 3000, [[4095, 16], [1, 1]]),
                waits=[(da, st["pbh"])], inc=dw, inc_by=16,
                allow_slow_non_contiguous=True)
    cn_d = b.op("gpsimd", "dma_start", st["KVACC"][0:16, 1:2],
                bass.AP(dr["h_pb"], 1000, [[4095, 16], [1, 1]]),
                inc=dw, inc_by=16, allow_slow_non_contiguous=True)
    st["cn_dw"] = cn_d

    gate_ac = {}
    sg_evac = {}
    for t in range(NTC):
        for f in range(NFC):
            bank = P[f % 6] if f < 6 else P[f - 6]
            wl = []
            if t == 0 and f == 0:
                wl = [(ac, st["a_bd"]), (ac, st["a_crep"]),
                      (ve, st["l0f_done_ve"])]
            if f >= 6:
                wl.append((ac, sg_evac[(t, f - 6)]))
            elif t > 0 and f == 0:
                wl.append((ac, sg_evac[(t - 1, 7)]))
            b.op("tensor", "matmul", bank[:, 0:4],
                 bass.AP(XNB, 2048 * f + 128 * t, [[XNB.shape[1], 128], [1, 128]]),
                 BD[:, 0:4], start=True, stop=False, waits=wl)
            pg = b.op("tensor", "matmul", bank[:, 0:4], ONES[0:1, 0:128],
                      GB2R[0:1, 0:4], start=False, stop=True, inc=pe)
            a = b.op("scalar", "activation", RW[:, f * 4:f * 4 + 4], bank[:, 0:4],
                     AF.Sigmoid, waits=[(pe, pg)], inc=ac)
            sg_evac[(t, f)] = a
        b.wait("vector", ac, sg_evac[(t, 7)])
        sa = bass.AP(RW, 0, [[RW.shape[1], 128], [2, 16]])
        sb = bass.AP(RW, 1, [[RW.shape[1], 128], [2, 16]])
        b.op("vector", "tensor_tensor", STAT[:, 32:48], sa, sb, ALU.mult)
        b.op("vector", "drain")
        b.op("vector", "tensor_tensor", STAT[:, 32:48], STAT[:, 32:48], CREP[:, 0:16],
             ALU.mult)
        b.op("vector", "drain")
        b.op("vector", "tensor_tensor", STAT[:, 32:48], STAT[:, 32:48], sa,
             ALU.subtract)
        b.op("vector", "drain")
        vgt = b.op("vector", "tensor_scalar_add", GATE[:, 16 * t:16 * t + 16],
                   STAT[:, 32:48], 2.0, inc=ve, lp="bf16 gate")
        wl = [(ve, vgt)]
        if t >= 2:
            wl.append((ac, st["_tp_ac"][t % 2]))
        a = transpose_tile(b, st, GATE[:, 16 * t:16 * t + 16],
                           bass.AP(GREP, 128 * t, [[GREP.shape[1], 16], [1, 128]]),
                           t, waits_t=wl, m=16)
        st["_tp_ac"][t % 2] = a
        gate_ac[t] = a
    if st["cut"] == "gates":
        st["l1a_done_ve"] = b.cnt[ve]
        st["l1a_preln3_ac"] = b.cnt[ac]
        return
    b.op("vector", "tensor_scalar", GREP[0:16, 2048:4096],
         bass.AP(GREP, 0, [[GREP.shape[1], 16], [1, 2048]]),
         st["KVACC"][0:16, 0:1], None, ALU.mult,
         waits=[(ac, gate_ac[NTC - 1]), (dw, st["cn_dw"])], lp="bf16 gcn")
    hgr0 = b.op("sync", "dma_start", dr["h_gr"][0:16, :],
                bass.AP(GREP, 0, [[GREP.shape[1], 16], [1, 2048]]),
                waits=[(ac, gate_ac[NTC - 1])], inc=da, inc_by=16)
    vgcp = b.op("vector", "tensor_scalar", GREP[0:16, 0:2048],
                bass.AP(GREP, 0, [[GREP.shape[1], 16], [1, 2048]]),
                st["KVACC"][0:16, 1:2], None, ALU.mult,
                waits=[(da, hgr0)], inc=ve, lp="bf16 gcp")
    hgr = b.op("sync", "dma_start", dr["h_gr"][16:32, :],
               bass.AP(GREP, 2048, [[GREP.shape[1], 16], [1, 2048]]),
               waits=[(ve, vgcp)], inc=da, inc_by=16)
    hgr = b.op("sync", "dma_start", dr["h_gr"][32:48, :],
               bass.AP(GREP, 0, [[GREP.shape[1], 16], [1, 2048]]),
               waits=[(ve, vgcp)], inc=da, inc_by=16)

    proj_ac = 0
    for which, wname, bias_c0, dst0, wdone_pre in (
            (0, "L1_attn_Wq", 0, 0, wq_done),
            (1, "L1_attn_Wk", 8, 16384, None)):
        if which == 1:
            b.wait("gpsimd", pe, st["_l1_lastq_pe"])
            w_chunk_dma(b, (POOL, WSLOT), dr["L1_attn_Wk"], D, D, dw)
            wdone_pre = b.cnt[dw]
        for m in range(NFC):
            for g in range(4):
                bank = P[(m * 4 + g) % 4]
                for f in range(NFC):
                    wl = []
                    if f == 0:
                        idx = m * 4 + g
                        if idx == 0:
                            wl = [(dw, wdone_pre), (ac, preln2_ac),
                                  (da, st["x0_spill_da"]), (da, bqc), (da, bkc),
                                  (ac, gate_ac[NTC - 1])]
                        if idx >= 4:
                            wl.append((ac, st[f"_l1_proj_ac{(idx - 4) % 4}"]))
                    b.op("tensor", "matmul", bank[:, 0:512],
                         bass.AP(POOL, WSLOT + 1024 * f + 128 * m,
                                 [[POOL.shape[1], 128], [1, 128]]),
                         bass.AP(XNB, 2048 * f + 512 * g, [[XNB.shape[1], 128], [1, 512]]),
                         start=(f == 0), stop=(f == 7),
                         inc=(pe if f == 7 else None), waits=wl)
                pq = b.cnt[pe]
                a = b.op("vector", "tensor_scalar",
                         bass.AP(RESB, dst0 + 2048 * m + 512 * g,
                                 [[RESB.shape[1], 128], [1, 512]]),
                         bank[:, 0:512],
                         st["BIASC"][:, bias_c0 + m:bias_c0 + m + 1], None, ALU.add,
                         waits=[(pe, pq)], inc=ac, lp="bf16 qkT")
                st[f"_l1_proj_ac{(m * 4 + g) % 4}"] = a
        st["_l1_lastq_pe"] = b.cnt[pe]
    proj_ac = b.cnt[ac]
    b.wait("gpsimd", pe, st["_l1_lastq_pe"])
    w_chunk_dma(b, (POOL, WSLOT), dr["L1_attn_Wv"], D, D, dw)
    wv_done = b.cnt[dw]
    bv_d = b.op("gpsimd", "dma_start", BROW[0:1, 0:1024],
                bass.AP(dr["L1_attn_bv"], 0, [[D, 1], [1, D]]), inc=dw, inc_by=16)
    vmem = b.op("vector", "memset",
                bass.AP(POOL, A0 + 64, [[POOL.shape[1], 128], [1040, 2], [65, 16], [1, 1]]),
                1.0, inc=ve)
    vdma = {}
    for t in range(NTC):
        stg = A0 + 1040 * (t % 2)
        pair = (P[4], P[5])
        for half in range(2):
            for f in range(NFC):
                wl = []
                if half == 0 and f == 0:
                    wl = [(dw, wv_done + 16)] if t == 0 else                          [(ac, st[f"_l1_v_ac{(t - 1) % 2}"])] if t >= 1 else []
                b.op("tensor", "matmul", pair[half][:, 0:512],
                     bass.AP(XNB, 2048 * f + 128 * t, [[XNB.shape[1], 128], [1, 128]]),
                     bass.AP(POOL, WSLOT + 1024 * f + 512 * half,
                             [[POOL.shape[1], 128], [1, 512]]),
                     start=(f == 0), stop=False, waits=wl)
            pv = b.op("tensor", "matmul", pair[half][:, 0:512], ONES[0:1, 0:128],
                      BROW[0:1, 512 * half:512 * half + 512],
                      start=False, stop=True, inc=pe)
        wl = [(pe, pv)]
        if t == 0:
            wl.append((ve, vmem))
        if t >= 2:
            wl.append((da, vdma[t - 2]))
        b.op("scalar", "copy",
             bass.AP(POOL, stg, [[POOL.shape[1], 128], [65, 8], [1, 64]]),
             P[4][:, 0:512], waits=wl, lp="bf16 v")
        a = b.op("scalar", "copy",
                 bass.AP(POOL, stg + 520, [[POOL.shape[1], 128], [65, 8], [1, 64]]),
                 P[5][:, 0:512], inc=ac, lp="bf16 v")
        st[f"_l1_v_ac{t % 2}"] = a
        vdma[t] = b.op("sync", "dma_start", dr["h_v"][128 * t:128 * t + 128, :],
                       bass.AP(POOL, stg, [[POOL.shape[1], 128], [1, 1040]]),
                       waits=[(ac, a)], inc=da, inc_by=16)
    if st["cut"] == "proj":
        st["l1a_done_ve"] = b.cnt[ve]
        st["l1a_preln3_ac"] = b.cnt[ac]
        return
    bo_d = b.op("gpsimd", "dma_start", BROW[0:1, 1024:2048],
                bass.AP(dr["L1_attn_bo"], 0, [[D, 1], [1, D]]), inc=dw, inc_by=16)

    SC = 1.0 / 8.0
    TB = 0
    SCBC = [1024, 1536, 2048, 2560]
    roll_d = {}
    grep_d = {}
    vh_d = {}

    def prefetch(h):
        if h >= 16:
            return
        wl_r = [(da, st["pbh"])]
        wl_g = [(da, hgr)]
        wl_v = [(da, vdma[NTC - 1])]
        if h >= 2:
            wl_r.append((ve, st.get(("tt1last", h - 2), 0)))
            wl_g.append((ve, st.get(("tt1last", h - 2), 0)))
            wl_v.append((pe, st.get(("avlast", "hq", (h - 2) * 4 + 3), 0)))
        roll_d[h] = b.op("sync", "dma_start",
                         pool_ap(ROLL[h % 2], 3968),
                         bass.AP(dr["h_pb"], 4095 * h, [[1, 128], [1, 3968]]),
                         waits=wl_r, inc=da, inc_by=16)
        grep_d[h] = b.op("sync", "dma_start", GREP[:, 2048 * (h % 2):2048 * (h % 2) + 2048],
                         bass.AP(dr["h_gr"], 2048 * h, [[0, 128], [1, 2048]]),
                         waits=wl_g, inc=da, inc_by=16)
        b.op("sync", "dma_start",
             bass.AP(POOL, WSLOT + 4096 * (h % 2), [[POOL.shape[1], 128], [1, 2048]]),
             bass.AP(dr["h_gr"], 2048 * (16 + h), [[0, 128], [1, 2048]]),
             waits=wl_g, inc=da, inc_by=16)
        grep_d[h] = b.op("sync", "dma_start",
                         bass.AP(POOL, WSLOT + 4096 * (h % 2) + 2048,
                                 [[POOL.shape[1], 128], [1, 2048]]),
                         bass.AP(dr["h_gr"], 2048 * (32 + h), [[0, 128], [1, 2048]]),
                         waits=wl_g, inc=da, inc_by=16)
        vh_d[h] = b.op("sync", "dma_start",
                       bass.AP(VH, 1040 * (h % 2), [[VH.shape[1], 128], [65, 16], [1, 65]]),
                       bass.AP(dr["h_v"], 65 * h, [[1040, 128], [128 * 1040, 16], [1, 65]]),
                       waits=wl_v, inc=da, inc_by=16)

    prefetch(0)
    prefetch(1)
    avst_d = {}
    exp_cnt = {}
    tt2_cnt = {}
    for h in range(16):
        p0 = 64 * (h % 2)
        hh = h // 2
        gslot = 2048 * (h % 2)
        b.wait("vector", da, roll_d[h])
        b.wait("vector", da, grep_d[h])
        def av_tail(hqx):
            hx = hqx // 4
            qgx = hqx % 4
            Abx = ABUF[hqx % 2]
            avb = P[4 + hqx % 2]
            repb = P[4 + (hqx + 1) % 2]
            for ktav in range(NTC):
                wl_av = []
                if ktav == 0:
                    wl_av = [(da, vh_d[hx]), (ac, exp_cnt[(hqx, NTC - 1)])]
                    if hqx >= 1:
                        wl_av.append((ac, st.get(("reps_ac", hqx - 1), 0)))
                pav = b.op("tensor", "matmul", avb[0:65, 0:512],
                           bass.AP(VH, 1040 * (hx % 2) + 65 * ktav,
                                   [[VH.shape[1], 128], [1, 65]]),
                           pool_ap(Abx + 512 * ktav, 512),
                           start=(ktav == 0), stop=(ktav == NTC - 1),
                           waits=wl_av, inc=(pe if ktav == NTC - 1 else None))
            st[("avlast", "hq", hqx)] = pav
            if qgx == 3:
                st[("avlast", hx)] = pav
            b.op("vector", "reciprocal", RECB[64:65, 0:512], avb[64:65, 0:512],
                 waits=[(pe, pav)], lp="bf16 recip")
            vr = b.op("vector", "drain", inc=ve)
            prep = b.op("tensor", "matmul", repb[0:64, 0:512], ONES[64:65, 0:64],
                        RECB[64:65, 0:512], start=True, stop=True,
                        waits=[(ve, vr)], inc=pe)
            arep = b.op("scalar", "copy", REPS[0:64, 0:512], repb[0:64, 0:512],
                        waits=[(pe, prep)], inc=ac, lp="bf16 reps")
            st[("reps_ac", hqx)] = arep
            aslot = 512 * (hqx % 2)
            wl_avt = [(ac, arep)]
            if hqx >= 2:
                wl_avt.append((da, avst_d[hqx - 2]))
            vavt = b.op("vector", "tensor_tensor", AVST[0:64, aslot:aslot + 512],
                        avb[0:64, 0:512], REPS[0:64, 0:512], ALU.mult,
                        waits=wl_avt, inc=ve, lp="bf16 avt")
            st[("avt_tt", hqx)] = vavt
            avst_d[hqx] = b.op("sync", "dma_start",
                               bass.AP(dr["h_avt"], 2048 * 64 * hx + 512 * qgx,
                                       [[2048, 64], [1, 512]]),
                               AVST[0:64, aslot:aslot + 512],
                               waits=[(ve, vavt)], inc=da, inc_by=16)

        for qg in range(4):
            hq = h * 4 + qg
            Ab = ABUF[hq % 2]
            for kt in range(NTC):
                bank = P[kt % 4]
                wl = []
                if kt >= 4:
                    wl.append((ve, tt2_cnt[(hq, kt - 4)]))
                elif hq >= 1:
                    wl.append((ve, tt2_cnt[(hq - 1, kt + 12)]))
                else:
                    wl.append((ac, proj_ac))
                if kt == 0 and hq >= 2:
                    wl.append((pe, st.get(("avlast", "hq", hq - 2), 0)))
                pqk = b.op("tensor", "matmul", bank[:, 0:512],
                           bass.AP(RESB, 16384 + 2048 * hh + 128 * kt + p0 * RESB.shape[1],
                                   [[RESB.shape[1], 64], [1, 128]]),
                           bass.AP(RESB, 2048 * hh + 512 * qg + p0 * RESB.shape[1],
                                   [[RESB.shape[1], 64], [1, 512]]),
                           start=True, stop=True, waits=wl, inc=pe)
                is_left = (128 * kt + 127) <= (512 * qg - 91)
                is_right = (128 * kt) >= (512 * qg + 511 + 91)
                if is_left or is_right:
                    coff = WSLOT + 4096 * (h % 2) + (0 if is_left else 2048)
                    bias_ap = bass.AP(POOL, coff + 512 * qg,
                                      [[POOL.shape[1], 128], [1, 512]])
                else:
                    rv = bass.AP(POOL, ROLL[h % 2] + 2047 + 128 * kt - 512 * qg,
                                 [[POOL.shape[1], 128], [-1, 512]])
                    wl_t = []
                    if hq >= 2 and kt == 0:
                        wl_t.append((ac, exp_cnt.get((hq - 2, NTC - 1), 0)))
                    b.op("vector", "tensor_tensor",
                         bass.AP(SMALLB, TB, [[SMALLB.shape[1], 128], [1, 512]]),
                         GREP[:, gslot + 512 * qg:gslot + 512 * qg + 512], rv, ALU.mult,
                         waits=wl_t, lp="bf16 bias")
                    bias_ap = bass.AP(SMALLB, TB, [[SMALLB.shape[1], 128], [1, 512]])
                wl2 = [(pe, pqk)]
                if kt >= 4:
                    wl2.append((ac, exp_cnt[(hq, kt - 4)]))
                elif hq >= 1:
                    wl2.append((ac, exp_cnt[(hq - 1, 12 + kt)]))
                v2 = b.op("vector", "scalar_tensor_tensor",
                          bass.AP(SMALLB, SCBC[kt % 4], [[SMALLB.shape[1], 128], [1, 512]]),
                          bank[:, 0:512], SC, bias_ap,
                          ALU.mult, ALU.add, inc=ve, lp="bf16 scores")
                tt2_cnt[(hq, kt)] = v2
                a = b.op("scalar", "activation",
                         pool_ap(Ab + 512 * kt, 512),
                         bass.AP(SMALLB, SCBC[kt % 4], [[SMALLB.shape[1], 128], [1, 512]]),
                         AF.Exp, waits=[(ve, v2)], inc=ac, lp="bf16 softmax")
                exp_cnt[(hq, kt)] = a
                if st["cut"] == "qkonly" and kt == NTC - 1:
                    st["l1a_done_ve"] = b.cnt[ve]
                    st["l1a_preln3_ac"] = b.cnt[ac]
                    return
            if qg == 3:
                st[("tt1last", h)] = tt2_cnt[(hq, NTC - 1)]
            if hq >= 1:
                av_tail(hq - 1)
            if qg == 0 and h >= 1:
                prefetch(h + 1)

    av_tail(63)
    b.wait("gpsimd", ve, b.cnt[ve])
    w_chunk_dma(b, (POOL, WSLOT), dr["L1_attn_Wo"], D, D, dw)
    wo_done = b.cnt[dw]
    if st["cut"] == "mainloop":
        st["l1a_done_ve"] = b.cnt[ve]
        st["l1a_preln3_ac"] = b.cnt[ac]
        return
    AVL = 0
    post = {}
    xstream = {}
    for t in range(NTC):
        wl = [(da, avst_d[63])] if t == 0 else []
        dls = []
        for g2 in range(2):
            dd = b.op("sync", "dma_start",
                      bass.AP(POOL, AVL + 1024 * (t % 2) + 512 * g2,
                              [[POOL.shape[1], 128], [128, 4], [1, 128]]),
                      bass.AP(dr["h_avt"], 128 * t + 4 * g2 * 128 * 2048,
                              [[2048, 128], [128 * 2048, 4], [1, 128]]),
                      waits=wl + ([(ve, post.get(t - 2, 0))] if t >= 2 else []),
                      inc=da, inc_by=16)
            dls.append(dd)
        dx = b.op("sync", "dma_start",
                  bass.AP(SMALL, 1024, [[SMALL.shape[1], 128], [1, 1024]]),
                  bass.AP(dr["h_x0"], 128 * 1024 * t, [[1024, 128], [1, 1024]]),
                  waits=[(ve, xstream.get(t - 1, 0))], inc=da, inc_by=16)
        pair = (P[0], P[1]) if t % 2 == 0 else (P[2], P[3])
        for half in range(2):
            for f in range(NFC):
                wl = []
                if f == 0:
                    wl = [(da, dls[-1])]
                    if t == 0 and half == 0:
                        wl += [(dw, wo_done)]
                    if t < 2 and half == 0:
                        wl.append((ve, st[("avt_tt", 63)]))
                    if t >= 2 and half == 0:
                        wl.append((ve, post.get(t - 2, 0)))
                b.op("tensor", "matmul", pair[half][:, 0:512],
                     bass.AP(POOL, AVL + 1024 * (t % 2) + 128 * f,
                             [[POOL.shape[1], 128], [1, 128]]),
                     bass.AP(POOL, WSLOT + 1024 * f + 512 * half,
                             [[POOL.shape[1], 128], [1, 512]]),
                     start=(f == 0), stop=False, waits=wl)
            po = b.op("tensor", "matmul", pair[half][:, 0:512], ONES[0:1, 0:128],
                      BROW[0:1, 1024 + 512 * half:1536 + 512 * half],
                      start=False, stop=True, inc=pe)
        b.wait("vector", pe, po)
        v2 = ln_stats(b, S, STAT, (pair[0][:, 0:512], pair[1][:, 0:512]))
        b.op("vector", "tensor_scalar", SMALL[:, 0:512], pair[0][:, 0:512],
             STAT[:, 2:3], STAT[:, 3:4], ALU.mult, ALU.add)
        vln = b.op("vector", "tensor_scalar", SMALL[:, 512:1024], pair[1][:, 0:512],
                   STAT[:, 2:3], STAT[:, 3:4], ALU.mult, ALU.add, inc=ve)
        b.wait("vector", da, dx)
        vxn = b.op("vector", "tensor_tensor", SMALL[:, 2048:3072], SMALL[:, 1024:2048],
                   SMALL[:, 0:1024], ALU.add, inc=ve)
        post[t] = vxn
        dst = dr["y"] if n_sub == 3 else dr["h_x1"]
        dxo = b.op("sync", "dma_start",
                   bass.AP(dst, 128 * 1024 * t, [[1024, 128], [1, 1024]]),
                   bass.AP(SMALL, 2048, [[SMALL.shape[1], 128], [1, 1024]]),
                   waits=[(ve, vxn)], inc=da, inc_by=16)
        xstream[t] = vxn
        st[("x1_dma", t)] = dxo
        if n_sub >= 4:
            v2b = ln_stats(b, S, STAT, (SMALL[:, 2048:2560], SMALL[:, 2560:3072]))
            v3 = b.op("vector", "tensor_scalar",
                      bass.AP(POOL, XT1, [[POOL.shape[1], 128], [1, 1024]]),
                      SMALL[:, 2048:3072], STAT[:, 2:3], STAT[:, 3:4],
                      ALU.mult, ALU.add,
                      waits=[(pe, st.get("_tp_pe", 0))], inc=ve)
            xstream[t] = v3
            for f in range(NFC):
                wl = [(ve, v3)]
                if f >= 2:
                    wl.append((ac, st["_tp_ac"][f % 2]))
                a = transpose_tile(b, st, bass.AP(POOL, XT1 + 128 * f,
                                                  [[POOL.shape[1], 128], [1, 128]]),
                                   XNB[:, 2048 * f + 128 * t:2048 * f + 128 * t + 128],
                                   f, waits_t=wl)
                st["_tp_ac"][f % 2] = a
    if n_sub == 3 and st["cut"] is None:
        last = st[("x1_dma", NTC - 1)]
        b.wait("sync", da, last)
    st["l1a_done_ve"] = b.cnt[ve]
    st["l1a_preln3_ac"] = b.cnt[ac]


def l1_ffn(st, WA, WB, WC, REST):
    b, S, dr, P = st["b"], st["S"], st["dr"], st["P"]
    POOL, XNB, RES, SMALL, STAT, RW = (st["POOL"], st["XNB"], st["RES"], st["SMALL"],
                                       st["STAT"], st["RW"])
    ONES, BROW = st["ONES"], st["BROW"]
    dw, da, pe, ve, ac = S["dw"], S["da"], S["pe"], S["ve"], S["ac"]
    OUT = RES

    WS1, WS2, HT, WRC = 0, 8192, 16384, 32768

    start_ve = st["l1a_done_ve"]
    wr_d = b.op("gpsimd", "dma_start",
                bass.AP(POOL, WRC, [[POOL.shape[1], 128], [8, 8], [1, 8]]),
                bass.AP(dr["L1_ffn_Wr"], 0, [[8, 128], [128 * 8, 8], [1, 8]]),
                waits=[(ve, start_ve)], inc=dw, inc_by=16)
    eb_d = b.op("gpsimd", "dma_start", BROW[0:1, 0:8],
                bass.AP(dr["L1_ffn_expert_bias"], 0, [[8, 1], [1, 8]]),
                inc=dw, inc_by=16)
    w_chunk_dma(b, (POOL, WS1), dr["L1_ffn_Sg"], D, D, dw)
    sg_done = b.cnt[dw]
    w_chunk_dma(b, (POOL, WS2), dr["L1_ffn_Su"], D, D, dw)
    su_done = b.cnt[dw]

    rt_evac = {}
    for t in range(NTC):
        bank = P[4 + t % 2]
        wl = [(dw, eb_d), (ac, st["l1a_preln3_ac"])] if t == 0 else              [(ve, rt_evac[t - 2])] if t >= 2 else []
        for f in range(NFC):
            b.op("tensor", "matmul", bank[:, 0:8],
                 bass.AP(XNB, 2048 * f + 128 * t, [[XNB.shape[1], 128], [1, 128]]),
                 bass.AP(POOL, WRC + 8 * f, [[POOL.shape[1], 128], [1, 8]]),
                 start=(f == 0), stop=False, waits=wl if f == 0 else [])
        pr = b.op("tensor", "matmul", bank[:, 0:8], ONES[0:1, 0:128],
                  BROW[0:1, 0:8], start=False, stop=True, inc=pe)
        vr = b.op("vector", "tensor_copy", RW[:, 8 * t:8 * t + 8], bank[:, 0:8],
                  waits=[(pe, pr)], inc=ve)
        rt_evac[t] = vr
    for t in range(NTC):
        L = RW[:, 8 * t:8 * t + 8]
        b.op("vector", "tensor_reduce", STAT[:, 32:33], L, axis=AX.X, op=ALU.max)
        b.op("vector", "drain")
        b.op("vector", "tensor_scalar", STAT[:, 34:42], L, STAT[:, 32:33], None,
             ALU.is_ge)
        b.op("vector", "drain")
        b.op("vector", "scalar_tensor_tensor", STAT[:, 42:50], STAT[:, 34:42], -1e30,
             L, ALU.mult, ALU.add)
        b.op("vector", "drain")
        b.op("vector", "tensor_reduce", STAT[:, 33:34], STAT[:, 42:50], axis=AX.X,
             op=ALU.max)
        b.op("vector", "drain")
        b.op("vector", "tensor_scalar", STAT[:, 34:42], L, STAT[:, 33:34], None,
             ALU.is_ge)
        b.op("vector", "tensor_scalar", STAT[:, 50:51], STAT[:, 32:33], -1.0, None,
             ALU.mult)
        v1 = b.op("vector", "drain", inc=ve)
        a1 = b.op("scalar", "activation", STAT[:, 42:50], L, AF.Exp,
                  bias=STAT[:, 50:51], scale=1.0, waits=[(ve, v1)], inc=ac)
        b.op("vector", "tensor_tensor", STAT[:, 42:50], STAT[:, 42:50],
             STAT[:, 34:42], ALU.mult, waits=[(ac, a1)])
        b.op("vector", "drain")
        b.op("vector", "tensor_reduce", STAT[:, 51:52], STAT[:, 42:50], axis=AX.X,
             op=ALU.add)
        b.op("vector", "drain")
        b.op("vector", "reciprocal", STAT[:, 51:52], STAT[:, 51:52])
        b.op("vector", "drain")
        vrw = b.op("vector", "tensor_scalar", RW[:, 128 + 8 * t:136 + 8 * t],
                   STAT[:, 42:50], STAT[:, 51:52], None, ALU.mult, inc=ve)
    rw_done = b.cnt[ve]

    h_done = {}
    for m in range(NFC):
        for g in range(4):
            pg_b = P[0 + (g % 2)]
            pu_b = P[2 + (g % 2)]
            idx = m * 4 + g
            for f in range(NFC):
                wl = []
                if f == 0:
                    if idx == 0:
                        wl = [(dw, sg_done), (ve, st["l1a_done_ve"])]
                    if idx >= 2:
                        wl.append((ve, h_done[idx - 2]))
                b.op("tensor", "matmul", pg_b[:, 0:512],
                     bass.AP(POOL, WS1 + 1024 * f + 128 * m,
                             [[POOL.shape[1], 128], [1, 128]]),
                     bass.AP(XNB, 2048 * f + 512 * g, [[XNB.shape[1], 128], [1, 512]]),
                     start=(f == 0), stop=(f == 7), waits=wl,
                     inc=(pe if f == 7 else None))
            pgc = b.cnt[pe]
            for f in range(NFC):
                wl = [(dw, su_done)] if (idx == 0 and f == 0) else []
                b.op("tensor", "matmul", pu_b[:, 0:512],
                     bass.AP(POOL, WS2 + 1024 * f + 128 * m,
                             [[POOL.shape[1], 128], [1, 128]]),
                     bass.AP(XNB, 2048 * f + 512 * g, [[XNB.shape[1], 128], [1, 512]]),
                     start=(f == 0), stop=(f == 7), waits=wl,
                     inc=(pe if f == 7 else None))
            puc = b.cnt[pe]
            sbuf0 = 512 * (idx % 2)
            wl_s = [(pe, pgc)]
            if idx >= 2:
                wl_s.append((ve, h_done[idx - 2]))
            a1 = b.op("scalar", "activation", SMALL[:, sbuf0:sbuf0 + 512],
                      pg_b[:, 0:512], AF.Silu, waits=wl_s, inc=ac)
            vh = b.op("vector", "tensor_tensor",
                      bass.AP(POOL, HT + 2048 * m + 512 * g,
                              [[POOL.shape[1], 128], [1, 512]]),
                      pu_b[:, 0:512], SMALL[:, sbuf0:sbuf0 + 512], ALU.mult,
                      waits=[(ac, a1), (pe, puc)], inc=ve, lp="bf16 h")
            h_done[idx] = vh
    sh_h_ve = b.cnt[ve]
    b.wait("gpsimd", pe, b.cnt[pe])
    w_chunk_dma(b, (POOL, WS2), dr["L1_ffn_Sd"], D, D, dw)
    sd_done = b.cnt[dw]
    dsh = {}
    for t in range(NTC):
        g, r = divmod(t, 4)
        pair = (P[4], P[5]) if t % 2 == 0 else (P[0], P[1])
        for half in range(2):
            for m in range(NFC):
                wl = []
                if m == 0:
                    if t < 2 and half == 0:
                        wl = [(dw, sd_done), (ve, sh_h_ve)]
                    if t >= 2 and half == 0:
                        wl.append((ac, dsh[t - 2]))
                b.op("tensor", "matmul", pair[half][:, 0:512],
                     bass.AP(POOL, HT + 2048 * m + 128 * t,
                             [[POOL.shape[1], 128], [1, 128]]),
                     bass.AP(POOL, WS2 + 1024 * m + 512 * half,
                             [[POOL.shape[1], 128], [1, 512]]),
                     start=(m == 0), stop=(m == 7),
                     inc=(pe if m == 7 else None), waits=wl)
        pd = b.cnt[pe]
        b.op("scalar", "copy", OUT[:, 1024 * t:1024 * t + 512], pair[0][:, 0:512],
             waits=[(pe, pd)])
        a = b.op("scalar", "copy", OUT[:, 1024 * t + 512:1024 * t + 1024],
                 pair[1][:, 0:512], inc=ac)
        dsh[t] = a
    out_init_ac = b.cnt[ac]

    ex_state = {}
    EWS = [0, 12288]
    HTE = 24576
    exw = {}

    def load_expert(e):
        if e >= 8:
            return
        base2 = EWS[e % 2]
        if e < 2:
            b.wait("gpsimd", pe, b.cnt[pe])
        else:
            b.wait("gpsimd", pe, exw[("lastpe", e - 2)])
        for nm, off in (("L1_ffn_Wg", base2), ("L1_ffn_Wu", base2 + 4096)):
            dstap = bass.AP(POOL, off, [[POOL.shape[1], 128], [512, 8], [1, 512]])
            srcap = bass.AP(dr[nm], e * D * 512, [[512, 128], [128 * 512, 8], [1, 512]])
            b.op("gpsimd", "dma_start", dstap, srcap, inc=dw, inc_by=16)
        wd_dst = bass.AP(POOL, base2 + 8192, [[POOL.shape[1], 128], [1024, 4], [1, 1024]])
        wd_src = bass.AP(dr["L1_ffn_Wd"], e * 512 * D,
                         [[1024, 128], [128 * 1024, 4], [1, 1024]])
        b.op("gpsimd", "dma_start", wd_dst, wd_src, inc=dw, inc_by=16)
        exw[("dw", e)] = b.cnt[dw]

    for e in range(8):
        base = EWS[e % 2]
        if e == 0:
            load_expert(0)
            load_expert(1)
        wg_d = exw[("dw", e)]
        wd_d = wg_d
        eh_done = {}
        for m in range(4):
            for g in range(4):
                idx = m * 4 + g
                pg_b = P[0 + (g % 2)]
                pu_b = P[2 + (g % 2)]
                for f in range(NFC):
                    wl = []
                    if f == 0:
                        if idx == 0:
                            wl = [(dw, wg_d)]
                            if e == 0:
                                wl.append((ac, out_init_ac))
                            else:
                                wl.append((ve, ex_state[e - 1]))
                        if idx >= 2:
                            wl.append((ve, eh_done[idx - 2]))
                    b.op("tensor", "matmul", pg_b[:, 0:512],
                         bass.AP(POOL, base + 512 * f + 128 * m,
                                 [[POOL.shape[1], 128], [1, 128]]),
                         bass.AP(XNB, 2048 * f + 512 * g, [[XNB.shape[1], 128], [1, 512]]),
                         start=(f == 0), stop=(f == 7), waits=wl,
                         inc=(pe if f == 7 else None))
                pgc = b.cnt[pe]
                for f in range(NFC):
                    b.op("tensor", "matmul", pu_b[:, 0:512],
                         bass.AP(POOL, base + 4096 + 512 * f + 128 * m,
                                 [[POOL.shape[1], 128], [1, 128]]),
                         bass.AP(XNB, 2048 * f + 512 * g, [[XNB.shape[1], 128], [1, 512]]),
                         start=(f == 0), stop=(f == 7),
                         inc=(pe if f == 7 else None))
                puc = b.cnt[pe]
                sbuf0 = 512 * (idx % 2)
                wl_s = [(pe, pgc)]
                if idx >= 2:
                    wl_s.append((ve, eh_done[idx - 2]))
                elif e == 0:
                    wl_s.append((ve, h_done[30]))
                else:
                    wl_s.append((ve, ex_state[e - 1]))
                a1 = b.op("scalar", "activation", SMALL[:, sbuf0:sbuf0 + 512],
                          pg_b[:, 0:512], AF.Silu, waits=wl_s, inc=ac)
                vh = b.op("vector", "tensor_tensor",
                          bass.AP(POOL, HTE + 2048 * m + 512 * g,
                                  [[POOL.shape[1], 128], [1, 512]]),
                          pu_b[:, 0:512], SMALL[:, sbuf0:sbuf0 + 512], ALU.mult,
                          waits=[(ac, a1), (pe, puc)], inc=ve, lp="bf16 h")
                eh_done[idx] = vh
        he_ve = b.cnt[ve]
        dacc = {}
        for t in range(NTC):
            pair = (P[4], P[5]) if t % 2 == 0 else (P[0], P[1])
            for half in range(2):
                for m in range(4):
                    wl = []
                    if m == 0:
                        if t < 2 and half == 0:
                            wl = [(dw, wd_d), (ve, he_ve)]
                        if t >= 2 and half == 0:
                            wl.append((ve, dacc[t - 2]))
                    b.op("tensor", "matmul", pair[half][:, 0:512],
                         bass.AP(POOL, HTE + 2048 * m + 128 * t,
                                 [[POOL.shape[1], 128], [1, 128]]),
                         bass.AP(POOL, base + 8192 + 1024 * m + 512 * half,
                                 [[POOL.shape[1], 128], [1, 512]]),
                         start=(m == 0), stop=(m == 3),
                         inc=(pe if m == 3 else None), waits=wl)
            pd = b.cnt[pe]
            wl = [(pe, pd)]
            if e == 0 and t < 2:
                wl.append((ve, rw_done))
            b.op("vector", "scalar_tensor_tensor", OUT[:, 1024 * t:1024 * t + 512],
                 pair[0][:, 0:512], RW[:, 128 + 8 * t + e:129 + 8 * t + e],
                 OUT[:, 1024 * t:1024 * t + 512], ALU.mult, ALU.add, waits=wl)
            vd = b.op("vector", "scalar_tensor_tensor",
                      OUT[:, 1024 * t + 512:1024 * t + 1024],
                      pair[1][:, 0:512], RW[:, 128 + 8 * t + e:129 + 8 * t + e],
                      OUT[:, 1024 * t + 512:1024 * t + 1024], ALU.mult, ALU.add,
                      inc=ve)
            dacc[t] = vd
        ex_state[e] = b.cnt[ve]
        exw[("lastpe", e)] = b.cnt[pe]
        load_expert(e + 2)

    ydma = {}
    for t in range(NTC):
        dx = b.op("sync", "dma_start",
                  bass.AP(SMALL, 1024, [[SMALL.shape[1], 128], [1, 1024]]),
                  bass.AP(dr["h_x1"], 128 * 1024 * t, [[1024, 128], [1, 1024]]),
                  waits=[(da, st[("x1_dma", t)]), (ve, ex_state[7])] if t == 0
                  else [(ve, ydma.get(t - 1, 0))], inc=da, inc_by=16)
        v2 = ln_stats(b, S, STAT,
                      (OUT[:, 1024 * t:1024 * t + 512],
                       OUT[:, 1024 * t + 512:1024 * t + 1024]),
                      waits_v=[(ve, ex_state[7])] if t == 0 else ())
        b.op("vector", "tensor_scalar", SMALL[:, 0:512], OUT[:, 1024 * t:1024 * t + 512],
             STAT[:, 2:3], STAT[:, 3:4], ALU.mult, ALU.add)
        b.op("vector", "tensor_scalar", SMALL[:, 512:1024],
             OUT[:, 1024 * t + 512:1024 * t + 1024],
             STAT[:, 2:3], STAT[:, 3:4], ALU.mult, ALU.add)
        b.wait("vector", da, dx)
        vy = b.op("vector", "tensor_tensor", SMALL[:, 2048:3072], SMALL[:, 1024:2048],
                  SMALL[:, 0:1024], ALU.add, inc=ve)
        dy = b.op("sync", "dma_start",
                  bass.AP(dr["y"], 128 * 1024 * t, [[1024, 128], [1, 1024]]),
                  bass.AP(SMALL, 2048, [[SMALL.shape[1], 128], [1, 1024]]),
                  waits=[(ve, vy)], inc=da, inc_by=16)
        ydma[t] = vy
        st[("y_dma", t)] = dy
    b.wait("sync", da, st[("y_dma", NTC - 1)])



def make_in_maps(inputs, n_cores=8):
    consts = host_constants()
    x = np.ascontiguousarray(inputs["L0_attn_x"], dtype=np.float32)
    base = {}
    for k, v in inputs.items():
        if k == "L0_attn_x":
            continue
        base[k] = np.ascontiguousarray(v, dtype=np.float32)
    for k, v in consts.items():
        base[k] = v
    maps = []
    for c in range(n_cores):
        m = dict(base)
        m["x"] = x[c]
        maps.append(m)
    return maps


def kernel(**inputs):
    nc = build_program(n_sub=4)
    in_maps = make_in_maps(inputs)
    res = run_bass_kernel_spmd(nc, in_maps, core_ids=list(range(8)))
    out = np.stack([np.asarray(r["y"], dtype=np.float32) for r in res.results], axis=0)
    return out


# version 14 (final)
# speedup vs baseline: 1.0317x; 1.0018x over previous
import numpy as np
import ml_dtypes
import concourse.bass as bass
import concourse.mybir as mybir
from concourse.bass_utils import run_bass_kernel_spmd

F32 = mybir.dt.float32
BF16 = mybir.dt.bfloat16
AF = mybir.ActivationFunctionType
ALU = mybir.AluOpType
AX = mybir.AxisListType

NT = 2048
D = 1024
NTC = 16
NFC = 8
H = 16
HD = 64
EPS = 1e-5

ENGINES = ("sync", "tensor", "vector", "scalar", "gpsimd")



def _bucket_1d():
    rel = np.arange(4095) - 2047
    nb = 16
    buckets = (rel > 0).astype(np.int64) * nb
    rp = np.abs(rel)
    max_exact = nb // 2
    is_small = rp < max_exact
    large = max_exact + (np.log(np.maximum(rp, 1) / max_exact) /
                         np.log(128 / max_exact) * (nb - max_exact)).astype(np.int64)
    large = np.minimum(large, nb - 1)
    return buckets + np.where(is_small, rp, large)


def host_constants():
    c = {}
    b = _bucket_1d()
    oh = np.zeros((32, 4095), np.float32)
    oh[b[::-1], np.arange(4095)] = 1.0
    c["c_oh"] = oh.astype(ml_dtypes.bfloat16)
    jk = np.outer(np.arange(D, dtype=np.float64), np.arange(D, dtype=np.float64))
    ang = 2.0 * np.pi * (jk % D) / D
    c["c_cos"] = np.cos(ang).astype(np.float32).astype(ml_dtypes.bfloat16)
    c["c_sin"] = np.sin(ang).astype(np.float32).astype(ml_dtypes.bfloat16)
    c["c_id"] = np.eye(128, dtype=np.float32).astype(ml_dtypes.bfloat16)
    return c



class Bld:

    def __init__(self, nc):
        self.nc = nc
        self.q = {e: [] for e in ENGINES}
        self.cnt = {}
        self._cms = []

    def sem(self, name):
        cm = self.nc.semaphore(name)
        h = cm.__enter__()
        self._cms.append(cm)
        self.cnt[h] = 0
        return h

    def close_sems(self):
        for cm in reversed(self._cms):
            cm.__exit__(None, None, None)

    def op(self, eng, method, *args, waits=(), inc=None, inc_by=1, lp=None, **kw):
        after = None
        if inc is not None:
            self.cnt[inc] += inc_by
            after = self.cnt[inc]
        wl = tuple(waits)

        def f(e, method=method, args=args, kw=kw, wl=wl, inc=inc, inc_by=inc_by, lp=lp):
            for s, v in wl:
                e.wait_ge(s, v)
            if lp:
                with self.nc.allow_low_precision(reason=lp):
                    ins = getattr(e, method)(*args, **kw)
            else:
                ins = getattr(e, method)(*args, **kw)
            if inc is not None:
                ins.then_inc(inc, inc_by)
        self.q[eng].append(f)
        return after

    def wait(self, eng, s, v):
        def f(e, s=s, v=v):
            e.wait_ge(s, v)
        self.q[eng].append(f)

    def raw(self, eng, fn):
        self.q[eng].append(fn)

    def render(self):
        nc = self.nc
        with nc.Block() as block:
            @block.sync
            def _(e):
                for f in self.q["sync"]:
                    f(e)

            @block.tensor
            def _(e):
                for f in self.q["tensor"]:
                    f(e)

            @block.vector
            def _(e):
                for f in self.q["vector"]:
                    f(e)

            @block.scalar
            def _(e):
                for f in self.q["scalar"]:
                    f(e)

            @block.gpsimd
            def _(e):
                for f in self.q["gpsimd"]:
                    f(e)


def w_chunk_dma(b, dst_ap_args, src, rows, cols, sem, split=2, eng="gpsimd"):
    t, col0 = dst_ap_args
    nch = rows // 128
    per = max(1, nch // split)
    total = 0
    ch = 0
    while ch < nch:
        n = min(per, nch - ch)
        dst = bass.AP(t, col0 + cols * ch, [[t.shape[1], 128], [cols, n], [1, cols]])
        sap = bass.AP(src, 128 * cols * ch, [[cols, 128], [128 * cols, n], [1, cols]])
        b.op(eng, "dma_start", dst, sap, inc=sem, inc_by=16)
        total += 16
        ch += n
    return total



def build_program(n_sub=4, dbg=False, cut=None):
    nc = bass.Bass("TRN2", target_bir_lowering=False)
    b = Bld(nc)

    dr = {}
    dr["x"] = nc.dram_tensor("x", [NT, D], F32, kind="ExternalInput")
    dr["y"] = nc.dram_tensor("y", [NT, D], F32, kind="ExternalOutput")
    for nm, shp, dt in [("c_oh", [32, 4095], BF16), ("c_cos", [D, D], BF16),
                        ("c_sin", [D, D], BF16), ("c_id", [128, 128], BF16)]:
        dr[nm] = nc.dram_tensor(nm, shp, dt, kind="ExternalInput")
    WNAMES = [
        ("L0_attn_wg_w", [D, D]), ("L0_attn_wg_b", [D]),
        ("L0_attn_wq_w", [D, D]), ("L0_attn_wq_b", [D]),
        ("L0_attn_wk_w", [D, D]), ("L0_attn_wk_b", [D]),
        ("L0_attn_wv_w", [D, D]), ("L0_attn_wv_b", [D]),
        ("L0_attn_out_w", [D, D]), ("L0_attn_out_b", [D]),
        ("L0_ffn_w_freq_real", [D, D]), ("L0_ffn_w_freq_imag", [D, D]),
        ("L0_ffn_bias", [D]),
        ("L1_attn_Wq", [D, D]), ("L1_attn_bq", [D]),
        ("L1_attn_Wk", [D, D]), ("L1_attn_bk", [D]),
        ("L1_attn_Wv", [D, D]), ("L1_attn_bv", [D]),
        ("L1_attn_Wo", [D, D]), ("L1_attn_bo", [D]),
        ("L1_attn_rel_embed", [32, 16]), ("L1_attn_gru_w", [64, 8]),
        ("L1_attn_gru_b", [8]), ("L1_attn_gru_const", [1, 16, 1, 1]),
        ("L1_ffn_Wr", [D, 8]), ("L1_ffn_Wg", [8, D, 512]),
        ("L1_ffn_Wu", [8, D, 512]), ("L1_ffn_Wd", [8, 512, D]),
        ("L1_ffn_Sg", [D, D]), ("L1_ffn_Su", [D, D]), ("L1_ffn_Sd", [D, D]),
        ("L1_ffn_expert_bias", [8]),
    ]
    for nm, shp in WNAMES:
        dr[nm] = nc.dram_tensor(nm, shp, F32, kind="ExternalInput")
    dr["h_pb"] = nc.dram_tensor("h_pb", [16, 4095], BF16, kind="Internal")
    dr["h_PT"] = nc.dram_tensor("h_PT", [D, D], BF16, kind="Internal")
    dr["h_QT"] = nc.dram_tensor("h_QT", [D, D], BF16, kind="Internal")
    dr["h_M"] = nc.dram_tensor("h_M", [D, D], BF16, kind="Internal")
    dr["h_x0"] = nc.dram_tensor("h_x0", [NT, D], F32, kind="Internal")
    dr["h_x1"] = nc.dram_tensor("h_x1", [NT, D], F32, kind="Internal")
    dr["h_qt"] = nc.dram_tensor("h_qt", [D, NT], BF16, kind="Internal")
    dr["h_avt"] = nc.dram_tensor("h_avt", [D, NT], BF16, kind="Internal")
    dr["h_v"] = nc.dram_tensor("h_v", [NT, H * 65], BF16, kind="Internal")
    dr["h_gr"] = nc.dram_tensor("h_gr", [48, NT], BF16, kind="Internal")

    import contextlib
    ctx = contextlib.ExitStack()

    def sb(name, shape, dt):
        return ctx.enter_context(nc.sbuf_tensor(name, shape, dt))

    def psum(name, shape, dt=F32):
        return ctx.enter_context(nc.psum_tensor(name, shape, dt))

    RES = sb("RES", [128, NTC * D], F32)
    RESB = RES.bitcast(BF16)
    XNB = sb("XNB", [128, 16640], BF16)
    POOL = sb("POOL", [128, 35072], BF16)
    SMALL = sb("SMALL", [128, 3072], F32)
    STAT = sb("STAT", [128, 64], F32)
    RW = sb("RW", [128, 256], F32)
    BROW = sb("BROW", [128, 2048], BF16)
    ONES = sb("ONES", [128, 128], BF16)
    IDBF = sb("IDBF", [128, 128], BF16)
    KVS = sb("KVS", [128, 520], BF16)
    KVACC = sb("KVACC", [128, 1040], F32)
    BIASC = sb("BIASC", [128, 16], F32)
    GW2B = sb("GW2B", [128, 2], BF16)
    GB2R = sb("GB2R", [128, 4], BF16)
    BD = sb("BD", [128, 4], BF16)
    CREP = sb("CREP", [128, 16], F32)
    GATE = sb("GATE", [128, 256], BF16)
    GREP = sb("GREP", [128, 4096], BF16)
    VH = sb("VH", [128, 2080], BF16)
    AVST = sb("AVST", [128, 1024], BF16)
    RECB = sb("RECB", [128, 512], BF16)
    REPS = sb("REPS", [128, 512], BF16)
    GRUW = sb("GRUW", [128, 8], F32)
    GRUB = sb("GRUB", [128, 8], F32)
    C16 = sb("C16", [128, 16], F32)

    P = [psum(f"P{i}", [128, 512]) for i in range(6)]
    T0 = psum("T0", [128, 128], BF16)
    T1 = psum("T1", [128, 128], BF16)
    TPS = [T0, T1]

    S = {n: b.sem(n) for n in ["dw", "da", "pe", "ve", "ac", "gp"]}
    dw, da, pe, ve, ac = S["dw"], S["da"], S["pe"], S["ve"], S["ac"]

    st = dict(nc=nc, b=b, dr=dr, S=S, RES=RES, RESB=RESB, XNB=XNB, POOL=POOL,
              SMALL=SMALL, STAT=STAT, RW=RW, BROW=BROW, ONES=ONES, IDBF=IDBF,
              KVS=KVS, KVACC=KVACC, BIASC=BIASC, GW2B=GW2B, GB2R=GB2R, BD=BD,
              CREP=CREP, GATE=GATE, GREP=GREP, VH=VH, AVST=AVST,
              RECB=RECB, REPS=REPS, GRUW=GRUW, GRUB=GRUB, C16=C16,
              P=P, TPS=TPS)

    if dbg:
        dr["o_xnb"] = nc.dram_tensor("o_xnb", [128, 16384], BF16, kind="ExternalOutput")
        dr["o_kvs"] = nc.dram_tensor("o_kvs", [128, 520], BF16, kind="ExternalOutput")
        dr["o_kvacc"] = nc.dram_tensor("o_kvacc", [128, 1040], F32, kind="ExternalOutput")
        dr["o_rw"] = nc.dram_tensor("o_rw", [128, 256], F32, kind="ExternalOutput")
        dr["o_out"] = nc.dram_tensor("o_out", [NT, D], F32, kind="ExternalOutput")
        if dbg == 2:
            dr["o_vb"] = nc.dram_tensor("o_vb", [128, 16640], BF16, kind="ExternalOutput")
            dr["o_kq"] = nc.dram_tensor("o_kq", [128, 1024], BF16, kind="ExternalOutput")
            dr["o_brow"] = nc.dram_tensor("o_brow", [128, 2048], BF16, kind="ExternalOutput")
            dr["o_w1"] = nc.dram_tensor("o_w1", [128, 8192], BF16, kind="ExternalOutput")
    st["dbg"] = dbg
    st["n_sub"] = n_sub
    st["cut"] = cut

    prog(st, n_sub)

    if dbg:
        vv, aa, pp = b.cnt[S["ve"]], b.cnt[S["ac"]], b.cnt[S["pe"]]
        b.wait("gpsimd", S["ve"], vv)
        b.wait("gpsimd", S["ac"], aa)
        b.wait("gpsimd", S["pe"], pp)
        dd = b.op("gpsimd", "dma_start", dr["o_xnb"].ap(), XNB[:, 0:16384],
                  inc=S["dw"], inc_by=16)
        dd = b.op("gpsimd", "dma_start", dr["o_kvs"].ap(), KVS[:, :], inc=S["dw"], inc_by=16)
        dd = b.op("gpsimd", "dma_start", dr["o_kvacc"].ap(), st["KVACC"][:, :], inc=S["dw"], inc_by=16)
        dd = b.op("gpsimd", "dma_start", dr["o_rw"].ap(), st["RW"][:, :], inc=S["dw"], inc_by=16)
        for i in range(4):
            dd = b.op("gpsimd", "dma_start",
                      bass.AP(dr["o_out"], 4096 * 128 * i if False else 4 * 128 * 1024 * i,
                              [[1024, 128], [128 * 1024, 4], [1, 1024]]),
                      bass.AP(RES, 4096 * i, [[RES.shape[1], 128], [1024, 4], [1, 1024]]),
                      inc=S["dw"], inc_by=16)
        if dbg == 2:
            dd = b.op("gpsimd", "dma_start", dr["o_vb"].ap(),
                      bass.AP(POOL, 0, [[POOL.shape[1], 128], [1, 16640]]),
                      inc=S["dw"], inc_by=16)
            dd = b.op("gpsimd", "dma_start", dr["o_kq"].ap(),
                      bass.AP(POOL, 33024, [[POOL.shape[1], 128], [1, 1024]]),
                      inc=S["dw"], inc_by=16)
            dd = b.op("gpsimd", "dma_start", dr["o_brow"].ap(), st["BROW"][:, :],
                      inc=S["dw"], inc_by=16)
            dd = b.op("gpsimd", "dma_start", dr["o_w1"].ap(),
                      bass.AP(POOL, 16640, [[POOL.shape[1], 128], [1, 8192]]),
                      inc=S["dw"], inc_by=16)
        b.wait("gpsimd", S["dw"], dd)

    b.render()
    ctx.close()
    b.close_sems()
    return nc



def ln_stats(b, S, STAT, src_ap_halves, waits_v=()):
    ve, ac = S["ve"], S["ac"]
    b.op("vector", "bn_stats", STAT[:, 4:10], src_ap_halves[0], waits=waits_v)
    b.op("vector", "bn_stats", STAT[:, 10:16], src_ap_halves[1])
    b.op("vector", "drain")
    b.op("vector", "bn_aggr", STAT[:, 0:2], STAT[:, 4:16])
    b.op("vector", "drain")
    b.op("vector", "tensor_scalar_add", STAT[:, 2:3], STAT[:, 1:2], EPS)
    b.op("vector", "drain")
    b.op("vector", "reciprocal", STAT[:, 2:3], STAT[:, 2:3])
    v1 = b.op("vector", "drain", inc=ve)
    a1 = b.op("scalar", "activation", STAT[:, 2:3], STAT[:, 2:3], AF.Sqrt,
              waits=[(ve, v1)], inc=ac)
    b.op("vector", "scalar_tensor_tensor", STAT[:, 3:4], STAT[:, 0:2][:, 0:1], -1.0,
         STAT[:, 2:3], ALU.mult, ALU.mult, waits=[(ac, a1)])
    v2 = b.op("vector", "drain", inc=ve)
    return v2


def preln_batch(b, st, x_src, tile_ap, xnb_dst_fn):
    S, STAT, RW, XNB = st["S"], st["STAT"], st["RW"], st["XNB"]
    ve, ac, pe = S["ve"], S["ac"], S["pe"]
    for t in range(NTC):
        b.op("vector", "bn_stats", RW[:, 12 * t:12 * t + 6],
             x_src[:, 1024 * t:1024 * t + 512])
        b.op("vector", "bn_stats", RW[:, 12 * t + 6:12 * t + 12],
             x_src[:, 1024 * t + 512:1024 * t + 1024])
    b.op("vector", "drain")
    for t in range(NTC):
        b.op("vector", "bn_aggr", STAT[:, 2 * t:2 * t + 2], RW[:, 12 * t:12 * t + 12])
    b.op("vector", "drain")
    varv = bass.AP(STAT, 1, [[STAT.shape[1], 128], [2, 16]])
    meanv = bass.AP(STAT, 0, [[STAT.shape[1], 128], [2, 16]])
    b.op("vector", "tensor_scalar_add", STAT[:, 32:48], varv, EPS)
    b.op("vector", "drain")
    b.op("vector", "reciprocal", STAT[:, 32:48], STAT[:, 32:48])
    v1 = b.op("vector", "drain", inc=ve)
    a1 = b.op("scalar", "activation", STAT[:, 32:48], STAT[:, 32:48], AF.Sqrt,
              waits=[(ve, v1)], inc=ac)
    b.op("vector", "scalar_tensor_tensor", STAT[:, 48:64], meanv, -1.0,
         STAT[:, 32:48], ALU.mult, ALU.mult, waits=[(ac, a1)])
    b.op("vector", "drain")
    done_ac = {}
    for t in range(NTC):
        v3 = b.op("vector", "tensor_scalar", tile_ap[:, 0:1024],
                  x_src[:, 1024 * t:1024 * t + 1024],
                  STAT[:, 32 + t:33 + t], STAT[:, 48 + t:49 + t], ALU.mult, ALU.add,
                  waits=[(pe, st.get("_tp_pe", 0))], inc=ve)
        for f in range(NFC):
            wl = [(ve, v3)]
            if f >= 2:
                wl.append((ac, st["_tp_ac"][f % 2]))
            a = transpose_tile(b, st, tile_ap[:, 128 * f:128 * f + 128],
                               xnb_dst_fn(t, f), f, waits_t=wl)
            st["_tp_ac"][f % 2] = a
        done_ac[t] = max(st["_tp_ac"][0], st["_tp_ac"][1])
    return done_ac


def transpose_tile(b, st, src_bf16_ap, dst_bf16_ap, tp_idx, waits_t=(), m=128):
    S, TPS, IDBF = st["S"], st["TPS"], st["IDBF"]
    pe, ac = S["pe"], S["ac"]
    tb = TPS[tp_idx % 2]
    p1 = b.op("tensor", "transpose", tb[0:m, 0:128], src_bf16_ap, IDBF[:, :],
              waits=waits_t, inc=pe)
    a1 = b.op("scalar", "copy", dst_bf16_ap, tb[0:m, 0:128], waits=[(pe, p1)], inc=ac)
    st["_tp_pe"] = p1
    return a1


def preln_chunk(b, st, t, x_src, xn_dst_tile, emit_transposes=True):
    S, STAT, XNB = st["S"], st["STAT"], st["XNB"]
    ve, ac = S["ve"], S["ac"]
    v2 = ln_stats(b, S, STAT,
                  (x_src[:, 1024 * t:1024 * t + 512], x_src[:, 1024 * t + 512:1024 * t + 1024]))
    v3 = b.op("vector", "tensor_scalar", xn_dst_tile[:, 0:1024],
              x_src[:, 1024 * t:1024 * t + 1024],
              STAT[:, 2:3], STAT[:, 3:4], ALU.mult, ALU.add,
              waits=[(S["pe"], st.get("_tp_pe", 0))], inc=ve)
    if emit_transposes:
        last = None
        for f in range(NFC):
            waits_t = [(ve, v3)]
            if f >= 2:
                waits_t.append((S["ac"], st["_tp_ac"][f % 2]))
            a = transpose_tile(b, st, xn_dst_tile[:, 128 * f:128 * f + 128],
                               XNB[:, 2048 * f + 128 * t:2048 * f + 128 * t + 128],
                               f, waits_t=waits_t)
            st["_tp_ac"][f % 2] = a
    return v3


def postln_residual_chunk(b, st, t, psum_pair, waits_v=()):
    S, STAT, SMALL, RES = st["S"], st["STAT"], st["SMALL"], st["RES"]
    ve = S["ve"]
    Pa, Pb = psum_pair
    v2 = ln_stats(b, S, STAT, (Pa[:, 0:512], Pb[:, 0:512]), waits_v=waits_v)
    b.op("vector", "tensor_scalar", SMALL[:, 0:512], Pa[:, 0:512],
         STAT[:, 2:3], STAT[:, 3:4], ALU.mult, ALU.add)
    v3 = b.op("vector", "tensor_scalar", SMALL[:, 512:1024], Pb[:, 0:512],
              STAT[:, 2:3], STAT[:, 3:4], ALU.mult, ALU.add, inc=ve)
    v4 = b.op("vector", "tensor_tensor", RES[:, 1024 * t:1024 * t + 1024],
              RES[:, 1024 * t:1024 * t + 1024], SMALL[:, 0:1024], ALU.add, inc=ve)
    return v3, v4



def prog(st, n_sub):
    b, S, dr = st["b"], st["S"], st["dr"]
    nc = st["nc"]
    P, TPS = st["P"], st["TPS"]
    RES, RESB, XNB, POOL, SMALL, STAT = (st["RES"], st["RESB"], st["XNB"],
                                         st["POOL"], st["SMALL"], st["STAT"])
    ONES, IDBF, BROW = st["ONES"], st["IDBF"], st["BROW"]
    dw, da, pe, ve, ac = S["dw"], S["da"], S["pe"], S["ve"], S["ac"]
    st["_tp_ac"] = [0, 0]

    WA, WB, WC = 0, 8192, 16384
    REST = 24576

    xin = 0
    for i in range(4):
        dst = bass.AP(RES, 4096 * i, [[RES.shape[1], 128], [1024, 4], [1, 1024]])
        src = bass.AP(dr["x"], 4 * 128 * 1024 * i, [[1024, 128], [128 * 1024, 4], [1, 1024]])
        xin = b.op("sync", "dma_start", dst, src, inc=da, inc_by=16)
    b.op("sync", "dma_start", IDBF[:, :], dr["c_id"].ap(), inc=da, inc_by=16)
    cosin = 0
    for i in range(2):
        dst = bass.AP(POOL, WA + 4096 * i, [[POOL.shape[1], 128], [1024, 4], [1, 1024]])
        src = bass.AP(dr["c_cos"], 4 * 128 * 1024 * i, [[1024, 128], [128 * 1024, 4], [1, 1024]])
        b.op("sync", "dma_start", dst, src, inc=da, inc_by=16)
        dst = bass.AP(POOL, WB + 4096 * i, [[POOL.shape[1], 128], [1024, 4], [1, 1024]])
        src = bass.AP(dr["c_sin"], 4 * 128 * 1024 * i, [[1024, 128], [128 * 1024, 4], [1, 1024]])
        cosin = b.op("sync", "dma_start", dst, src, inc=da, inc_by=16)
    ohc = b.op("sync", "dma_start",
               bass.AP(POOL, REST, [[POOL.shape[1], 32], [1, 4095]]),
               dr["c_oh"].ap(), inc=da, inc_by=16)
    relc = b.op("gpsimd", "dma_start",
                bass.AP(POOL, REST + 4096, [[POOL.shape[1], 32], [1, 16]]),
                dr["L1_attn_rel_embed"].ap(), inc=dw, inc_by=16)
    w_wr = 0
    w_wr += w_chunk_dma(b, (POOL, WC), dr["L0_ffn_w_freq_real"], D, D, dw)
    w_wi_base = REST + 8192
    w_wr += w_chunk_dma(b, (XNB, 0), dr["L0_ffn_w_freq_imag"], D, D, dw)
    wr_done = b.cnt[dw]
    vones = b.op("vector", "memset", ONES[:, :], 1.0, inc=ve)

    PBB0 = REST + 5120
    pbecnt = []
    for i in range(8):
        nwide = 512 if i < 7 else 4095 - 7 * 512
        bank = P[i % 2]
        waits = [(dw, relc), (da, ohc)]
        if i >= 2:
            waits.append((ac, pbecnt[i - 2]))
        pcur = b.op("tensor", "matmul", bank[0:16, 0:nwide],
                    bass.AP(POOL, REST + 4096, [[POOL.shape[1], 32], [1, 16]]),
                    bass.AP(POOL, REST + 512 * i, [[POOL.shape[1], 32], [1, nwide]]),
                    start=True, stop=True, waits=waits, inc=pe)
        a = b.op("scalar", "copy",
                 bass.AP(POOL, PBB0 + 512 * i, [[POOL.shape[1], 16], [1, nwide]]),
                 bank[0:16, 0:nwide], waits=[(pe, pcur)], inc=ac)
        pbecnt.append(a)
    pbh = b.op("sync", "dma_start", dr["h_pb"].ap(),
               bass.AP(POOL, PBB0, [[POOL.shape[1], 16], [1, 4095]]),
               waits=[(ac, pbecnt[-1])], inc=da, inc_by=16)
    st["pbh"] = pbh

    g1 = b.op("gpsimd", "dma_start", st["GRUW"][0:64, 0:8], dr["L1_attn_gru_w"].ap(),
              inc=dw, inc_by=16)
    g2 = b.op("gpsimd", "dma_start", st["GRUW"][64:128, 0:8], dr["L1_attn_gru_w"].ap(),
              inc=dw, inc_by=16)
    g3 = b.op("gpsimd", "dma_start", st["GRUB"][0:1, 0:8],
              bass.AP(dr["L1_attn_gru_b"], 0, [[8, 1], [1, 8]]), inc=dw, inc_by=16)
    g4 = b.op("gpsimd", "dma_start", st["C16"][0:1, 0:16],
              bass.AP(dr["L1_attn_gru_const"], 0, [[16, 1], [1, 16]]), inc=dw, inc_by=16)
    vg = b.op("vector", "tensor_reduce", st["GW2B"][:, 0:2],
              bass.AP(st["GRUW"], 0, [[st["GRUW"].shape[1], 128], [4, 2], [1, 4]]),
              axis=AX.X, op=ALU.add, waits=[(dw, g2)], lp="pair sums to bf16")
    b.op("vector", "drain")
    vg2 = b.op("vector", "tensor_reduce", st["STAT"][0:1, 32:34],
               bass.AP(st["GRUB"], 0, [[GRUBROW := st["GRUB"].shape[1], 1], [4, 2], [1, 4]]),
               axis=AX.X, op=ALU.add, waits=[(dw, g3)])
    vg3 = b.op("vector", "drain", inc=ve)
    a_g1 = b.op("scalar", "copy", st["GB2R"][0:1, 0:2], st["STAT"][0:1, 32:34],
                waits=[(ve, vg3)], lp="gb2 bf16")
    a_g2 = b.op("scalar", "copy", st["GB2R"][0:1, 2:4], st["STAT"][0:1, 32:34],
                lp="gb2 bf16")
    vb0 = b.op("vector", "memset", st["BD"][:, :], 0.0, inc=ve)
    b.op("vector", "drain")
    a_bd1 = b.op("scalar", "copy", st["BD"][0:64, 0:2], st["GW2B"][0:64, 0:2],
                 waits=[(ve, b.cnt[ve])])
    a_bd = b.op("scalar", "copy", st["BD"][64:128, 2:4], st["GW2B"][64:128, 0:2], inc=ac)
    st["a_bd"] = a_bd
    a_c16 = b.op("scalar", "copy", st["GREP"][0:1, 0:16], st["C16"][0:1, 0:16],
                 waits=[(dw, g4)], inc=ac, lp="c16 bf16")
    pc = b.op("tensor", "matmul", P[2][0:128, 0:16], ONES[0:1, 0:128],
              st["GREP"][0:1, 0:16], start=True, stop=True,
              waits=[(ac, a_c16), (ve, vones)], inc=pe)
    a_crep = b.op("scalar", "copy", st["CREP"][:, 0:16], P[2][0:128, 0:16],
                  waits=[(pe, pc)], inc=ac)
    st["a_crep"] = a_crep

    MST = REST + 4096
    def mb_stage(src_banks, dst_hbm, m, wait_extra=()):
        a1 = b.op("scalar", "copy",
                  bass.AP(POOL, REST, [[POOL.shape[1], 128], [1, 512]]),
                  src_banks[0][:, 0:512], waits=list(wait_extra), inc=ac)
        a2 = b.op("scalar", "copy",
                  bass.AP(POOL, REST + 512, [[POOL.shape[1], 128], [1, 512]]),
                  src_banks[1][:, 0:512], inc=ac)
        d = b.op("sync", "dma_start", dst_hbm[128 * m:128 * m + 128, :],
                 bass.AP(POOL, REST, [[POOL.shape[1], 128], [1, 1024]]),
                 waits=[(ac, a2)], inc=da, inc_by=16)
        return a2, d

    if n_sub >= 2:
        last_stage_d = 0
        pt_evac = {}
        for m in range(8):
            for half in range(2):
                bank = P[half]
                for f in range(8):
                    wl = []
                    if f == 0 and half == 0:
                        wl = [(dw, wr_done), (da, cosin)]
                        if m == 0:
                            wl.append((ac, pbecnt[-1]))
                        if m >= 1:
                            wl.append((ac, pt_evac[m - 1]))
                    b.op("tensor", "matmul", bank[:, 0:512],
                         bass.AP(POOL, WC + 1024 * f + 128 * m,
                                 [[POOL.shape[1], 128], [1, 128]]),
                         bass.AP(POOL, WA + 1024 * f + 512 * half,
                                 [[POOL.shape[1], 128], [1, 512]]),
                         start=(f == 0), stop=False, waits=wl)
                    pcur = b.op("tensor", "matmul", bank[:, 0:512],
                                bass.AP(XNB, 1024 * f + 128 * m,
                                        [[XNB.shape[1], 128], [1, 128]]),
                                bass.AP(POOL, WB + 1024 * f + 512 * half,
                                        [[POOL.shape[1], 128], [1, 512]]),
                                start=False, stop=(f == 7), inc=(pe if f == 7 else None))
                b.wait("scalar", pe, pcur)
            a2, last_stage_d = mb_stage([P[0], P[1]], dr["h_PT"], m,
                                        wait_extra=([(da, last_stage_d)] if m >= 1 else []))
            pt_evac[m] = a2
        qt_v = {}
        for m in range(8):
            for half in range(2):
                for f in range(8):
                    wl = []
                    if f == 0 and half == 0 and m >= 1:
                        wl.append((ve, qt_v[m - 1]))
                    b.op("tensor", "matmul", P[2 + half][:, 0:512],
                         bass.AP(XNB, 1024 * f + 128 * m, [[XNB.shape[1], 128], [1, 128]]),
                         bass.AP(POOL, WA + 1024 * f + 512 * half,
                                 [[POOL.shape[1], 128], [1, 512]]),
                         start=(f == 0), stop=(f == 7), waits=wl)
                    pcur = b.op("tensor", "matmul", P[4 + half][:, 0:512],
                                bass.AP(POOL, WC + 1024 * f + 128 * m,
                                        [[POOL.shape[1], 128], [1, 128]]),
                                bass.AP(POOL, WB + 1024 * f + 512 * half,
                                        [[POOL.shape[1], 128], [1, 512]]),
                                start=(f == 0), stop=(f == 7),
                                inc=(pe if f == 7 else None))
            b.wait("vector", pe, pcur)
            b.wait("vector", da, last_stage_d)
            a1 = b.op("scalar", "copy", SMALL[:, 0:512], P[2][:, 0:512],
                      waits=[(pe, pcur)], inc=ac)
            a2 = b.op("scalar", "copy", SMALL[:, 512:1024], P[3][:, 0:512], inc=ac)
            b.wait("vector", ac, a2)
            b.op("vector", "scalar_tensor_tensor",
                 bass.AP(POOL, REST, [[POOL.shape[1], 128], [1, 512]]),
                 P[4][:, 0:512], -1.0, SMALL[:, 0:512], ALU.mult, ALU.add)
            v = b.op("vector", "scalar_tensor_tensor",
                     bass.AP(POOL, REST + 512, [[POOL.shape[1], 128], [1, 512]]),
                     P[5][:, 0:512], -1.0, SMALL[:, 512:1024], ALU.mult, ALU.add, inc=ve)
            qt_v[m] = v
            last_stage_d = b.op("sync", "dma_start", dr["h_QT"][128 * m:128 * m + 128, :],
                                bass.AP(POOL, REST, [[POOL.shape[1], 128], [1, 1024]]),
                                waits=[(ve, v)], inc=da, inc_by=16)
        for m in range(8):
            dl = []
            for which, hbm in ((0, dr["h_PT"]), (1, dr["h_QT"])):
                for g in range(2):
                    dd = b.op("sync", "dma_start",
                              bass.AP(POOL, REST + 1024 + 1024 * which + 512 * g,
                                      [[POOL.shape[1], 128], [128, 4], [1, 128]]),
                              bass.AP(hbm, 128 * m + 4 * g * 128 * 1024,
                                      [[1024, 128], [128 * 1024, 4], [1, 128]]),
                              waits=[(da, last_stage_d)] if (which == 0 and g == 0) else [],
                              inc=da, inc_by=16)
                    dl.append(dd)
            for half in range(2):
                for f in range(8):
                    wl = [(da, dl[-1])] if f == 0 and half == 0 else []
                    b.op("tensor", "matmul", P[0 + half][:, 0:512],
                         bass.AP(POOL, REST + 1024 + 128 * f,
                                 [[POOL.shape[1], 128], [1, 128]]),
                         bass.AP(POOL, WA + 1024 * f + 512 * half,
                                 [[POOL.shape[1], 128], [1, 512]]),
                         start=(f == 0), stop=(f == 7), waits=wl)
                    pcur = b.op("tensor", "matmul", P[2 + half][:, 0:512],
                                bass.AP(POOL, REST + 2048 + 128 * f,
                                        [[POOL.shape[1], 128], [1, 128]]),
                                bass.AP(POOL, WB + 1024 * f + 512 * half,
                                        [[POOL.shape[1], 128], [1, 512]]),
                                start=(f == 0), stop=(f == 7),
                                inc=(pe if f == 7 else None))
            b.wait("scalar", pe, pcur)
            a1 = b.op("scalar", "activation", SMALL[:, 0:512], P[0][:, 0:512],
                      AF.Copy, bias=0.0, scale=1.0 / D, waits=[(pe, pcur)], inc=ac)
            a2 = b.op("scalar", "activation", SMALL[:, 512:1024], P[1][:, 0:512],
                      AF.Copy, bias=0.0, scale=1.0 / D, inc=ac)
            b.wait("vector", ac, a2)
            b.op("vector", "scalar_tensor_tensor",
                 bass.AP(POOL, REST, [[POOL.shape[1], 128], [1, 512]]),
                 P[2][:, 0:512], -1.0 / D, SMALL[:, 0:512], ALU.mult, ALU.add)
            v = b.op("vector", "scalar_tensor_tensor",
                     bass.AP(POOL, REST + 512, [[POOL.shape[1], 128], [1, 512]]),
                     P[3][:, 0:512], -1.0 / D, SMALL[:, 512:1024], ALU.mult, ALU.add,
                     inc=ve)
            last_stage_d = b.op("sync", "dma_start", dr["h_M"][128 * m:128 * m + 128, :],
                                bass.AP(POOL, REST, [[POOL.shape[1], 128], [1, 1024]]),
                                waits=[(ve, v)], inc=da, inc_by=16)
        st["m_done_da"] = last_stage_d
    else:
        st["m_done_da"] = 0
    mb_pe = b.cnt[pe]

    b.wait("vector", da, xin)
    b.wait("vector", ac, st["a_bd"])
    XT1 = 34048
    preln_batch(b, st, RES, bass.AP(POOL, XT1, [[POOL.shape[1], 128], [1, 1024]]),
                lambda t, f: XNB[:, 2048 * f + 128 * t:2048 * f + 128 * t + 128])
    st["preln0_ac"] = b.cnt[ac]
    if n_sub >= 2:
        b.wait("vector", pe, mb_pe)
        b.wait("scalar", pe, mb_pe)
        b.wait("gpsimd", pe, mb_pe)

    l0_attn(st, WA, WB, WC, REST)
    if n_sub >= 2:
        l0_ffn(st, WA, WB, WC, REST)
    if n_sub >= 3:
        l1_attn(st, WA, WB, WC, REST)
    if n_sub >= 4:
        l1_ffn(st, WA, WB, WC, REST)

    if n_sub >= 3:
        pass
    else:
        vv = b.cnt[ve]
        dy = 0
        for i in range(4):
            src = bass.AP(RES, 4096 * i, [[RES.shape[1], 128], [1024, 4], [1, 1024]])
            dst = bass.AP(dr["y"], 4 * 128 * 1024 * i, [[1024, 128], [128 * 1024, 4], [1, 1024]])
            dy = b.op("sync", "dma_start", dst, src, waits=[(ve, vv)] if i == 0 else [],
                      inc=da, inc_by=16)
        b.wait("sync", da, dy)



def l0_attn(st, WA, WB, WC, REST):
    b, S, dr, P = st["b"], st["S"], st["dr"], st["P"]
    POOL, XNB, RES, SMALL, STAT = st["POOL"], st["XNB"], st["RES"], st["SMALL"], st["STAT"]
    ONES, BROW, KVS, KVACC = st["ONES"], st["BROW"], st["KVS"], st["KVACC"]
    dw, da, pe, ve, ac = S["dw"], S["da"], S["pe"], S["ve"], S["ac"]
    nc = st["nc"]

    VB = REST
    TIL = REST + 16640
    T_KQ = TIL
    T_QT = TIL + 1024
    T_SO = TIL + 2048
    T_G = TIL + 3072
    T_GS = TIL + 4096
    T_GT = TIL + 5120
    VB = 0
    W1 = 16640
    W2 = 24832
    TIL = 33024
    T_KQ = TIL
    T_QT = TIL + 1024
    T_G = VB
    T_GS = VB + 1024
    T_GT = VB + 2048
    T_SO = VB + 3072

    def pool_ap(col, width, parts=128, p0=0):
        return bass.AP(POOL, p0 * POOL.shape[1] + col,
                       [[POOL.shape[1], parts], [1, width]])

    b.wait("vector", da, st["pbh"])
    b.wait("scalar", da, st["pbh"])
    b.wait("gpsimd", da, st["pbh"])
    b.wait("gpsimd", da, st["m_done_da"])

    wk_d = w_chunk_dma(b, (POOL, W1), dr["L0_attn_wk_w"], D, D, dw)
    wk_done = b.cnt[dw]
    bk_d = b.op("gpsimd", "dma_start", BROW[0:1, 0:1024],
                bass.AP(dr["L0_attn_wk_b"], 0, [[D, 1], [1, D]]), inc=dw, inc_by=16)
    w_chunk_dma(b, (POOL, W2), dr["L0_attn_wv_w"], D, D, dw)
    wv_done = b.cnt[dw]
    bv_d = b.op("gpsimd", "dma_start", BROW[0:1, 1024:2048],
                bass.AP(dr["L0_attn_wv_b"], 0, [[D, 1], [1, D]]), inc=dw, inc_by=16)

    vvb = b.op("vector", "memset",
               bass.AP(POOL, VB + 64, [[POOL.shape[1], 128], [1040, 16], [65, 16], [1, 1]]),
               1.0, inc=ve)

    kvacc_init = [False] * 8
    evac_k_ve = 0
    evac_v_ac = 0
    kv_dve = {}
    for t in range(NTC):
        for half in range(2):
            for f in range(NFC):
                wl = []
                if t == 0 and half == 0 and f == 0:
                    wl = [(dw, wk_done + 16), (ac, st["preln0_ac"])]
                if t > 0 and half == 0 and f == 0:
                    wl = [(ve, kv_dve.get(("kdone", t - 1), 0))]
                b.op("tensor", "matmul", P[half][:, 0:512],
                     bass.AP(XNB, 2048 * f + 128 * t, [[XNB.shape[1], 128], [1, 128]]),
                     bass.AP(POOL, W1 + 1024 * f + 512 * half,
                             [[POOL.shape[1], 128], [1, 512]]),
                     start=(f == 0), stop=False, waits=wl)
            pk = b.op("tensor", "matmul", P[half][:, 0:512], ONES[0:1, 0:128],
                      BROW[0:1, 512 * half:512 * half + 512],
                      start=False, stop=True, inc=pe)
        b.wait("vector", pe, pk)
        b.op("vector", "tensor_scalar_min", SMALL[:, 0:512], P[0][:, 0:512], 0.0)
        v1 = b.op("vector", "tensor_scalar_min", SMALL[:, 512:1024], P[1][:, 0:512],
                  0.0, inc=ve)
        a1 = b.op("scalar", "activation", SMALL[:, 1024:1536], SMALL[:, 0:512],
                  AF.Exp, waits=[(ve, v1)])
        a2 = b.op("scalar", "activation", SMALL[:, 1536:2048], SMALL[:, 512:1024],
                  AF.Exp, inc=ac)
        b.wait("vector", ac, a2)
        b.op("vector", "scalar_tensor_tensor", pool_ap(T_KQ, 512),
             P[0][:, 0:512], 0.0, SMALL[:, 1024:1536], ALU.max, ALU.add,
             lp="bf16 matmul operand")
        v2 = b.op("vector", "scalar_tensor_tensor", pool_ap(T_KQ + 512, 512),
                  P[1][:, 0:512], 0.0, SMALL[:, 1536:2048], ALU.max, ALU.add,
                  inc=ve, lp="bf16 matmul operand")
        kv_dve[("kphi", t)] = v2
        kv_dve[("kdone", t)] = v2
        for half in range(2):
            for f in range(NFC):
                wl = []
                if t == 0 and half == 0 and f == 0:
                    wl = [(dw, wv_done + 16)]
                if t > 0 and half == 0 and f == 0:
                    wl = [(ac, kv_dve.get(("vdone", t - 1), 0))]
                b.op("tensor", "matmul", P[2 + half][:, 0:512],
                     bass.AP(XNB, 2048 * f + 128 * t, [[XNB.shape[1], 128], [1, 128]]),
                     bass.AP(POOL, W2 + 1024 * f + 512 * half,
                             [[POOL.shape[1], 128], [1, 512]]),
                     start=(f == 0), stop=False, waits=wl)
            pv = b.op("tensor", "matmul", P[2 + half][:, 0:512], ONES[0:1, 0:128],
                      BROW[0:1, 1024 + 512 * half:1536 + 512 * half],
                      start=False, stop=True, inc=pe)
        wl = [(pe, pv)]
        if t == 0:
            wl.append((ve, vvb))
        b.op("scalar", "copy",
             bass.AP(POOL, VB + 1040 * t, [[POOL.shape[1], 128], [65, 8], [1, 64]]),
             P[2][:, 0:512], waits=wl, lp="bf16 v")
        av = b.op("scalar", "copy",
                  bass.AP(POOL, VB + 1040 * t + 520, [[POOL.shape[1], 128], [65, 8], [1, 64]]),
                  P[3][:, 0:512], inc=ac, lp="bf16 v")
        kv_dve[("vdone", t)] = av
        for j in range(8):
            bank = P[4 + (j % 2)]
            wl = [(ve, v2), (ac, av)]
            if (t * 8 + j) >= 2:
                wl.append((ve, kv_dve[("kvadd", t * 8 + j - 2)]))
            pj = b.op("tensor", "matmul", bank[:, 0:130],
                      pool_ap(T_KQ + 128 * j, 128),
                      bass.AP(POOL, VB + 1040 * t + 130 * j,
                              [[POOL.shape[1], 128], [1, 130]]),
                      start=True, stop=True, waits=wl, inc=pe)
            if not kvacc_init[j]:
                vv = b.op("vector", "tensor_copy", KVACC[:, 130 * j:130 * j + 130],
                          bank[:, 0:130], waits=[(pe, pj)], inc=ve)
                kvacc_init[j] = True
            else:
                vv = b.op("vector", "tensor_tensor", KVACC[:, 130 * j:130 * j + 130],
                          KVACC[:, 130 * j:130 * j + 130], bank[:, 0:130], ALU.add,
                          waits=[(pe, pj)], inc=ve)
            kv_dve[("kvadd", t * 8 + j)] = vv
    vlast = kv_dve[("kvadd", NTC * 8 - 1)]
    b.wait("scalar", ve, vlast)
    for j in range(8):
        b.op("scalar", "copy", KVS[0:64, 65 * j:65 * j + 65],
             KVACC[0:64, 130 * j:130 * j + 65], lp="bf16 kv")
        akv = b.op("scalar", "copy", KVS[64:128, 65 * j:65 * j + 65],
                   KVACC[64:128, 130 * j + 65:130 * j + 130], inc=ac, lp="bf16 kv")

    if st.get("dbg") == 2:
        st["l0a_done_ve"] = b.cnt[ve]
        return
    last_pe_pass1 = b.cnt[pe]
    b.wait("gpsimd", pe, last_pe_pass1)
    w_chunk_dma(b, (POOL, W1), dr["L0_attn_wq_w"], D, D, dw)
    wq_done = b.cnt[dw]
    b.op("gpsimd", "dma_start", BROW[0:1, 0:1024],
         bass.AP(dr["L0_attn_wq_b"], 0, [[D, 1], [1, D]]), inc=dw, inc_by=16)
    w_chunk_dma(b, (POOL, W2), dr["L0_attn_wg_w"], D, D, dw)
    wg_done = b.cnt[dw]
    b.op("gpsimd", "dma_start", BROW[0:1, 1024:2048],
         bass.AP(dr["L0_attn_wg_b"], 0, [[D, 1], [1, D]]), inc=dw, inc_by=16)
    OW = VB + 4096
    b.wait("gpsimd", ac, akv)
    w_chunk_dma(b, (POOL, OW), dr["L0_attn_out_w"], D, D, dw)
    ow_done = b.cnt[dw]
    ob_done = b.op("gpsimd", "dma_start", st["GREP"][0:1, 2048:3072],
                   bass.AP(dr["L0_attn_out_b"], 0, [[D, 1], [1, D]]), inc=dw, inc_by=16)

    p2state = {}
    for t in range(NTC):
        for half in range(2):
            for f in range(NFC):
                wl = []
                if t == 0 and half == 0 and f == 0:
                    wl = [(dw, wq_done + 16), (ve, vlast)]
                if t > 0 and half == 0 and f == 0:
                    wl = [(ac, p2state.get(("qT", t - 1), 0))]
                b.op("tensor", "matmul", P[half][:, 0:512],
                     bass.AP(XNB, 2048 * f + 128 * t, [[XNB.shape[1], 128], [1, 128]]),
                     bass.AP(POOL, W1 + 1024 * f + 512 * half,
                             [[POOL.shape[1], 128], [1, 512]]),
                     start=(f == 0), stop=False, waits=wl)
            pq = b.op("tensor", "matmul", P[half][:, 0:512], ONES[0:1, 0:128],
                      BROW[0:1, 512 * half:512 * half + 512],
                      start=False, stop=True, inc=pe)
        b.wait("vector", pe, pq)
        b.op("vector", "tensor_scalar_min", SMALL[:, 0:512], P[0][:, 0:512], 0.0)
        v1 = b.op("vector", "tensor_scalar_min", SMALL[:, 512:1024], P[1][:, 0:512],
                  0.0, inc=ve)
        a1 = b.op("scalar", "activation", SMALL[:, 1024:1536], SMALL[:, 0:512],
                  AF.Exp, waits=[(ve, v1)])
        a2 = b.op("scalar", "activation", SMALL[:, 1536:2048], SMALL[:, 512:1024],
                  AF.Exp, inc=ac)
        b.wait("vector", ac, a2)
        b.op("vector", "scalar_tensor_tensor", pool_ap(T_KQ, 512),
             P[0][:, 0:512], 0.0, SMALL[:, 1024:1536], ALU.max, ALU.add,
             lp="bf16 matmul operand")
        v2 = b.op("vector", "scalar_tensor_tensor", pool_ap(T_KQ + 512, 512),
                  P[1][:, 0:512], 0.0, SMALL[:, 1536:2048], ALU.max, ALU.add,
                  inc=ve, lp="bf16 matmul operand")
        for f in range(NFC):
            wl = [(ve, v2)]
            if f >= 2:
                wl.append((ac, st["_tp_ac"][f % 2]))
            a = transpose_tile(b, st, pool_ap(T_KQ + 128 * f, 128),
                               pool_ap(T_QT + 128 * f, 128), f, waits_t=wl)
            st["_tp_ac"][f % 2] = a
        aqT = st["_tp_ac"][(NFC - 1) % 2]
        aqT2 = st["_tp_ac"][(NFC - 2) % 2]
        p2state[("qT", t)] = max(aqT, aqT2)
        for half in range(2):
            for f in range(NFC):
                wl = []
                if t == 0 and half == 0 and f == 0:
                    wl = [(dw, wg_done + 16)]
                if t > 0 and half == 0 and f == 0:
                    wl = [(ve, p2state.get(("gs", t - 1), 0))]
                b.op("tensor", "matmul", P[2 + half][:, 0:512],
                     bass.AP(XNB, 2048 * f + 128 * t, [[XNB.shape[1], 128], [1, 128]]),
                     bass.AP(POOL, W2 + 1024 * f + 512 * half,
                             [[POOL.shape[1], 128], [1, 512]]),
                     start=(f == 0), stop=False, waits=wl)
            pg = b.op("tensor", "matmul", P[2 + half][:, 0:512], ONES[0:1, 0:128],
                      BROW[0:1, 1024 + 512 * half:1536 + 512 * half],
                      start=False, stop=True, inc=pe)
        b.wait("scalar", pe, pg)
        b.op("scalar", "activation", pool_ap(T_G, 512), P[2][:, 0:512], AF.Sigmoid,
             lp="bf16 gate")
        ag = b.op("scalar", "activation", pool_ap(T_G + 512, 512), P[3][:, 0:512],
                  AF.Sigmoid, inc=ac, lp="bf16 gate")
        sc_evac = {}
        for h in range(16):
            bank = P[2 + (h % 2)]
            p0 = 64 * (h % 2)
            hh = h // 2
            wl = [(ac, ag)]
            if h >= 2:
                wl.append((ac, sc_evac[h - 2]))
            psc = b.op("tensor", "matmul", bank[:, 0:65],
                       pool_ap(T_QT + 128 * hh, 128, parts=64, p0=p0),
                       bass.AP(KVS, p0 * KVS.shape[1] + 65 * hh,
                               [[KVS.shape[1], 64], [1, 65]]),
                       start=True, stop=True, waits=wl, inc=pe)
            sc_evac[h] = b.op("scalar", "copy", SMALL[:, 65 * h:65 * h + 65],
                              bank[:, 0:65], waits=[(pe, psc)], inc=ac)
        b.wait("vector", ac, sc_evac[15])
        densv = bass.AP(SMALL, 64, [[SMALL.shape[1], 128], [65, 16]])
        b.op("vector", "tensor_scalar_add", STAT[:, 20:36], densv, 1e-6)
        b.op("vector", "drain")
        b.op("vector", "reciprocal", STAT[:, 20:36], STAT[:, 20:36])
        b.op("vector", "drain")
        for h in range(16):
            vs = b.op("vector", "tensor_scalar", pool_ap(T_SO + 64 * h, 64),
                      SMALL[:, 65 * h:65 * h + 64], STAT[:, 20 + h:21 + h], None,
                      ALU.mult, inc=(ve if h == 15 else None), lp="bf16 so")
        vgs = b.op("vector", "tensor_tensor", pool_ap(T_GS, 1024), pool_ap(T_G, 1024),
                   pool_ap(T_SO, 1024), ALU.mult, inc=ve, lp="bf16 gs")
        p2state[("gs", t)] = vgs
        for f in range(NFC):
            wl = [(ve, vgs)]
            if f >= 2:
                wl.append((ac, st["_tp_ac"][f % 2]))
            a = transpose_tile(b, st, pool_ap(T_GS + 128 * f, 128),
                               pool_ap(T_GT + 128 * f, 128), f, waits_t=wl)
            st["_tp_ac"][f % 2] = a
        agT = max(st["_tp_ac"][0], st["_tp_ac"][1])
        for half in range(2):
            for f in range(NFC):
                wl = [(ac, agT), (dw, ow_done + 16)] if f == 0 else []
                if f == 0 and t > 0:
                    wl.append((ve, p2state.get(("postv3", t - 1), 0)))
                b.op("tensor", "matmul", P[4 + half][:, 0:512],
                     pool_ap(T_GT + 128 * f, 128),
                     bass.AP(POOL, OW + 1024 * f + 512 * half,
                             [[POOL.shape[1], 128], [1, 512]]),
                     start=(f == 0), stop=False, waits=wl)
            po = b.op("tensor", "matmul", P[4 + half][:, 0:512], ONES[0:1, 0:128],
                      st["GREP"][0:1, 2048 + 512 * half:2560 + 512 * half],
                      start=False, stop=True, inc=pe)
        b.wait("vector", pe, po)
        v3, v4 = postln_residual_chunk(b, st, t, (P[4], P[5]))
        p2state[("postv3", t)] = v3
        p2state[("postln", t)] = v4
    st["l0a_done_ve"] = b.cnt[ve]



def l0_ffn(st, WA, WB, WC, REST):
    b, S, dr, P = st["b"], st["S"], st["dr"], st["P"]
    POOL, XNB, RES, SMALL = st["POOL"], st["XNB"], st["RES"], st["SMALL"]
    ONES, BROW = st["ONES"], st["BROW"]
    dw, da, pe, ve, ac = S["dw"], S["da"], S["pe"], S["ve"], S["ac"]

    b.wait("sync", ve, st["l0a_done_ve"])
    b.wait("sync", da, st["m_done_da"])
    md = 0
    for i in range(2):
        dst = bass.AP(POOL, 4096 * i, [[POOL.shape[1], 128], [1024, 4], [1, 1024]])
        src = bass.AP(dr["h_M"], 4 * 128 * 1024 * i, [[1024, 128], [128 * 1024, 4], [1, 1024]])
        md = b.op("sync", "dma_start", dst, src, inc=da, inc_by=16)
    fb = b.op("gpsimd", "dma_start", BROW[0:1, 0:1024],
              bass.AP(dr["L0_ffn_bias"], 0, [[D, 1], [1, D]]),
              waits=[(ve, st["l0a_done_ve"])], inc=dw, inc_by=16)

    XT1 = 8192
    XNBl = st["XNB"]
    pl_ac = preln_batch(b, st, RES,
                        bass.AP(POOL, XT1, [[POOL.shape[1], 128], [1, 1024]]),
                        lambda tt, ff: XNBl[:, 2048 * ff + 128 * tt:2048 * ff + 128 * tt + 128])
    post = {}
    for t in range(NTC):
        pair = (P[0], P[1]) if t % 2 == 0 else (P[2], P[3])
        for half in range(2):
            for f in range(NFC):
                wl = []
                if f == 0:
                    wl = [(ac, pl_ac[t])]
                    if t == 0 and half == 0:
                        wl += [(da, md), (dw, fb), (ve, st["l0a_done_ve"])]
                    if t == 1 and half == 0:
                        wl.append((ve, st["l0a_done_ve"]))
                    if t >= 2 and half == 0:
                        wl.append((ve, post[t - 2]))
                b.op("tensor", "matmul", pair[half][:, 0:512],
                     bass.AP(XNB, 2048 * f + 128 * t, [[XNB.shape[1], 128], [1, 128]]),
                     bass.AP(POOL, 1024 * f + 512 * half, [[POOL.shape[1], 128], [1, 512]]),
                     start=(f == 0), stop=False, waits=wl)
            pm = b.op("tensor", "matmul", pair[half][:, 0:512], ONES[0:1, 0:128],
                      BROW[0:1, 512 * half:512 * half + 512],
                      start=False, stop=True, inc=pe)
        b.wait("vector", pe, pm)
        v3, v4 = postln_residual_chunk(b, st, t, pair)
        post[t] = v4
    st["l0f_done_ve"] = b.cnt[ve]




def l1_attn(st, WA, WB, WC, REST):
    b, S, dr, P = st["b"], st["S"], st["dr"], st["P"]
    POOL, XNB, RES, RESB = st["POOL"], st["XNB"], st["RES"], st["RESB"]
    SMALL, STAT, RW = st["SMALL"], st["STAT"], st["RW"]
    ONES, BROW, GREP, VH, AVST = st["ONES"], st["BROW"], st["GREP"], st["VH"], st["AVST"]
    RECB, REPS, BD, GB2R, CREP, GATE = (st["RECB"], st["REPS"], st["BD"], st["GB2R"],
                                        st["CREP"], st["GATE"])
    dw, da, pe, ve, ac = S["dw"], S["da"], S["pe"], S["ve"], S["ac"]
    n_sub = st["n_sub"]
    SMALLB = SMALL.bitcast(BF16)

    def pool_ap(col, width, parts=128, p0=0):
        return bass.AP(POOL, p0 * POOL.shape[1] + col,
                       [[POOL.shape[1], parts], [1, width]])

    A0, A1 = 0, 8192
    ROLL = [16384, 20352]
    WSLOT = 24576
    XT1 = 32768
    ABUF = [A0, A1]

    start_ve = st["l0f_done_ve"]
    b.wait("vector", ve, start_ve)

    sp = 0
    for i in range(4):
        src_ap = bass.AP(RES, 4096 * i, [[RES.shape[1], 128], [1024, 4], [1, 1024]])
        dst = bass.AP(dr["h_x0"], 4 * 128 * 1024 * i,
                      [[1024, 128], [128 * 1024, 4], [1, 1024]])
        sp = b.op("sync", "dma_start", dst, src_ap,
                  waits=[(ve, start_ve)] if i == 0 else [], inc=da, inc_by=16)
    st["x0_spill_da"] = sp

    preln_batch(b, st, RES, bass.AP(POOL, XT1, [[POOL.shape[1], 128], [1, 1024]]),
                lambda t, f: XNB[:, 2048 * f + 128 * t:2048 * f + 128 * t + 128])
    preln2_ac = b.cnt[ac]

    bqc = b.op("sync", "dma_start", st["BIASC"][:, 0:8],
               bass.AP(dr["L1_attn_bq"], 0, [[1, 128], [128, 8]]), inc=da, inc_by=16,
               allow_slow_non_contiguous=True)
    bkc = b.op("sync", "dma_start", st["BIASC"][:, 8:16],
               bass.AP(dr["L1_attn_bk"], 0, [[1, 128], [128, 8]]), inc=da, inc_by=16,
               allow_slow_non_contiguous=True)
    wq_d0 = b.cnt[dw]
    w_chunk_dma(b, (POOL, WSLOT), dr["L1_attn_Wq"], D, D, dw)
    wq_done = b.cnt[dw]

    cn_d = b.op("gpsimd", "dma_start", st["KVACC"][0:16, 0:1],
                bass.AP(dr["h_pb"], 3000, [[4095, 16], [1, 1]]),
                waits=[(da, st["pbh"])], inc=dw, inc_by=16,
                allow_slow_non_contiguous=True)
    cn_d = b.op("gpsimd", "dma_start", st["KVACC"][0:16, 1:2],
                bass.AP(dr["h_pb"], 1000, [[4095, 16], [1, 1]]),
                inc=dw, inc_by=16, allow_slow_non_contiguous=True)
    st["cn_dw"] = cn_d

    gate_ac = {}
    sg_evac = {}
    for t in range(NTC):
        for f in range(NFC):
            bank = P[f % 6] if f < 6 else P[f - 6]
            wl = []
            if t == 0 and f == 0:
                wl = [(ac, st["a_bd"]), (ac, st["a_crep"]),
                      (ve, st["l0f_done_ve"])]
            if f >= 6:
                wl.append((ac, sg_evac[(t, f - 6)]))
            elif t > 0 and f == 0:
                wl.append((ac, sg_evac[(t - 1, 7)]))
            b.op("tensor", "matmul", bank[:, 0:4],
                 bass.AP(XNB, 2048 * f + 128 * t, [[XNB.shape[1], 128], [1, 128]]),
                 BD[:, 0:4], start=True, stop=False, waits=wl)
            pg = b.op("tensor", "matmul", bank[:, 0:4], ONES[0:1, 0:128],
                      GB2R[0:1, 0:4], start=False, stop=True, inc=pe)
            a = b.op("scalar", "activation", RW[:, f * 4:f * 4 + 4], bank[:, 0:4],
                     AF.Sigmoid, waits=[(pe, pg)], inc=ac)
            sg_evac[(t, f)] = a
        b.wait("vector", ac, sg_evac[(t, 7)])
        sa = bass.AP(RW, 0, [[RW.shape[1], 128], [2, 16]])
        sb = bass.AP(RW, 1, [[RW.shape[1], 128], [2, 16]])
        b.op("vector", "tensor_tensor", STAT[:, 32:48], sa, sb, ALU.mult)
        b.op("vector", "drain")
        b.op("vector", "tensor_tensor", STAT[:, 32:48], STAT[:, 32:48], CREP[:, 0:16],
             ALU.mult)
        b.op("vector", "drain")
        b.op("vector", "tensor_tensor", STAT[:, 32:48], STAT[:, 32:48], sa,
             ALU.subtract)
        b.op("vector", "drain")
        vgt = b.op("vector", "tensor_scalar_add", GATE[:, 16 * t:16 * t + 16],
                   STAT[:, 32:48], 2.0, inc=ve, lp="bf16 gate")
        wl = [(ve, vgt)]
        if t >= 2:
            wl.append((ac, st["_tp_ac"][t % 2]))
        a = transpose_tile(b, st, GATE[:, 16 * t:16 * t + 16],
                           bass.AP(GREP, 128 * t, [[GREP.shape[1], 16], [1, 128]]),
                           t, waits_t=wl, m=16)
        st["_tp_ac"][t % 2] = a
        gate_ac[t] = a
    if st["cut"] == "gates":
        st["l1a_done_ve"] = b.cnt[ve]
        st["l1a_preln3_ac"] = b.cnt[ac]
        return
    b.op("vector", "tensor_scalar", GREP[0:16, 2048:4096],
         bass.AP(GREP, 0, [[GREP.shape[1], 16], [1, 2048]]),
         st["KVACC"][0:16, 0:1], None, ALU.mult,
         waits=[(ac, gate_ac[NTC - 1]), (dw, st["cn_dw"])], lp="bf16 gcn")
    hgr0 = b.op("sync", "dma_start", dr["h_gr"][0:16, :],
                bass.AP(GREP, 0, [[GREP.shape[1], 16], [1, 2048]]),
                waits=[(ac, gate_ac[NTC - 1])], inc=da, inc_by=16)
    vgcp = b.op("vector", "tensor_scalar", GREP[0:16, 0:2048],
                bass.AP(GREP, 0, [[GREP.shape[1], 16], [1, 2048]]),
                st["KVACC"][0:16, 1:2], None, ALU.mult,
                waits=[(da, hgr0)], inc=ve, lp="bf16 gcp")
    hgr = b.op("sync", "dma_start", dr["h_gr"][16:32, :],
               bass.AP(GREP, 2048, [[GREP.shape[1], 16], [1, 2048]]),
               waits=[(ve, vgcp)], inc=da, inc_by=16)
    hgr = b.op("sync", "dma_start", dr["h_gr"][32:48, :],
               bass.AP(GREP, 0, [[GREP.shape[1], 16], [1, 2048]]),
               waits=[(ve, vgcp)], inc=da, inc_by=16)

    proj_ac = 0
    for which, wname, bias_c0, dst0, wdone_pre in (
            (0, "L1_attn_Wq", 0, 0, wq_done),
            (1, "L1_attn_Wk", 8, 16384, None)):
        if which == 1:
            b.wait("gpsimd", pe, st["_l1_lastq_pe"])
            w_chunk_dma(b, (POOL, WSLOT), dr["L1_attn_Wk"], D, D, dw)
            wdone_pre = b.cnt[dw]
        for m in range(NFC):
            for g in range(4):
                bank = P[(m * 4 + g) % 4]
                for f in range(NFC):
                    wl = []
                    if f == 0:
                        idx = m * 4 + g
                        if idx == 0:
                            wl = [(dw, wdone_pre), (ac, preln2_ac),
                                  (da, st["x0_spill_da"]), (da, bqc), (da, bkc),
                                  (ac, gate_ac[NTC - 1])]
                        if idx >= 4:
                            wl.append((ac, st[f"_l1_proj_ac{(idx - 4) % 4}"]))
                    b.op("tensor", "matmul", bank[:, 0:512],
                         bass.AP(POOL, WSLOT + 1024 * f + 128 * m,
                                 [[POOL.shape[1], 128], [1, 128]]),
                         bass.AP(XNB, 2048 * f + 512 * g, [[XNB.shape[1], 128], [1, 512]]),
                         start=(f == 0), stop=(f == 7),
                         inc=(pe if f == 7 else None), waits=wl)
                pq = b.cnt[pe]
                a = b.op("vector", "tensor_scalar",
                         bass.AP(RESB, dst0 + 2048 * m + 512 * g,
                                 [[RESB.shape[1], 128], [1, 512]]),
                         bank[:, 0:512],
                         st["BIASC"][:, bias_c0 + m:bias_c0 + m + 1], None, ALU.add,
                         waits=[(pe, pq)], inc=ac, lp="bf16 qkT")
                st[f"_l1_proj_ac{(m * 4 + g) % 4}"] = a
        st["_l1_lastq_pe"] = b.cnt[pe]
    proj_ac = b.cnt[ac]
    b.wait("gpsimd", pe, st["_l1_lastq_pe"])
    w_chunk_dma(b, (POOL, WSLOT), dr["L1_attn_Wv"], D, D, dw)
    wv_done = b.cnt[dw]
    bv_d = b.op("gpsimd", "dma_start", BROW[0:1, 0:1024],
                bass.AP(dr["L1_attn_bv"], 0, [[D, 1], [1, D]]), inc=dw, inc_by=16)
    vmem = b.op("vector", "memset",
                bass.AP(POOL, A0 + 64, [[POOL.shape[1], 128], [1040, 2], [65, 16], [1, 1]]),
                1.0, inc=ve)
    vdma = {}
    for t in range(NTC):
        stg = A0 + 1040 * (t % 2)
        pair = (P[4], P[5])
        for half in range(2):
            for f in range(NFC):
                wl = []
                if half == 0 and f == 0:
                    wl = [(dw, wv_done + 16)] if t == 0 else                          [(ac, st[f"_l1_v_ac{(t - 1) % 2}"])] if t >= 1 else []
                b.op("tensor", "matmul", pair[half][:, 0:512],
                     bass.AP(XNB, 2048 * f + 128 * t, [[XNB.shape[1], 128], [1, 128]]),
                     bass.AP(POOL, WSLOT + 1024 * f + 512 * half,
                             [[POOL.shape[1], 128], [1, 512]]),
                     start=(f == 0), stop=False, waits=wl)
            pv = b.op("tensor", "matmul", pair[half][:, 0:512], ONES[0:1, 0:128],
                      BROW[0:1, 512 * half:512 * half + 512],
                      start=False, stop=True, inc=pe)
        wl = [(pe, pv)]
        if t == 0:
            wl.append((ve, vmem))
        if t >= 2:
            wl.append((da, vdma[t - 2]))
        b.op("scalar", "copy",
             bass.AP(POOL, stg, [[POOL.shape[1], 128], [65, 8], [1, 64]]),
             P[4][:, 0:512], waits=wl, lp="bf16 v")
        a = b.op("scalar", "copy",
                 bass.AP(POOL, stg + 520, [[POOL.shape[1], 128], [65, 8], [1, 64]]),
                 P[5][:, 0:512], inc=ac, lp="bf16 v")
        st[f"_l1_v_ac{t % 2}"] = a
        vdma[t] = b.op("sync", "dma_start", dr["h_v"][128 * t:128 * t + 128, :],
                       bass.AP(POOL, stg, [[POOL.shape[1], 128], [1, 1040]]),
                       waits=[(ac, a)], inc=da, inc_by=16)
    if st["cut"] == "proj":
        st["l1a_done_ve"] = b.cnt[ve]
        st["l1a_preln3_ac"] = b.cnt[ac]
        return
    bo_d = b.op("gpsimd", "dma_start", BROW[0:1, 1024:2048],
                bass.AP(dr["L1_attn_bo"], 0, [[D, 1], [1, D]]), inc=dw, inc_by=16)

    SC = 1.0 / 8.0
    TB = 0
    SCBC = [1024, 1536, 2048, 2560]
    roll_d = {}
    grep_d = {}
    vh_d = {}

    def prefetch(h):
        if h >= 16:
            return
        wl_r = [(da, st["pbh"])]
        wl_g = [(da, hgr)]
        wl_v = [(da, vdma[NTC - 1])]
        if h >= 2:
            wl_r.append((ve, st.get(("tt1last", h - 2), 0)))
            wl_g.append((ve, st.get(("tt1last", h - 2), 0)))
            wl_v.append((pe, st.get(("avlast", "hq", (h - 2) * 4 + 3), 0)))
        roll_d[h] = b.op("sync", "dma_start",
                         pool_ap(ROLL[h % 2], 3968),
                         bass.AP(dr["h_pb"], 4095 * h, [[1, 128], [1, 3968]]),
                         waits=wl_r, inc=da, inc_by=16)
        grep_d[h] = b.op("sync", "dma_start", GREP[:, 2048 * (h % 2):2048 * (h % 2) + 2048],
                         bass.AP(dr["h_gr"], 2048 * h, [[0, 128], [1, 2048]]),
                         waits=wl_g, inc=da, inc_by=16)
        b.op("sync", "dma_start",
             bass.AP(POOL, WSLOT + 4096 * (h % 2), [[POOL.shape[1], 128], [1, 2048]]),
             bass.AP(dr["h_gr"], 2048 * (16 + h), [[0, 128], [1, 2048]]),
             waits=wl_g, inc=da, inc_by=16)
        grep_d[h] = b.op("sync", "dma_start",
                         bass.AP(POOL, WSLOT + 4096 * (h % 2) + 2048,
                                 [[POOL.shape[1], 128], [1, 2048]]),
                         bass.AP(dr["h_gr"], 2048 * (32 + h), [[0, 128], [1, 2048]]),
                         waits=wl_g, inc=da, inc_by=16)
        vh_d[h] = b.op("sync", "dma_start",
                       bass.AP(VH, 1040 * (h % 2), [[VH.shape[1], 128], [65, 16], [1, 65]]),
                       bass.AP(dr["h_v"], 65 * h, [[1040, 128], [128 * 1040, 16], [1, 65]]),
                       waits=wl_v, inc=da, inc_by=16)

    prefetch(0)
    prefetch(1)
    avst_d = {}
    exp_cnt = {}
    tt2_cnt = {}
    for h in range(16):
        p0 = 64 * (h % 2)
        hh = h // 2
        gslot = 2048 * (h % 2)
        b.wait("vector", da, roll_d[h])
        b.wait("vector", da, grep_d[h])
        def av_tail(hqx):
            hx = hqx // 4
            qgx = hqx % 4
            Abx = ABUF[hqx % 2]
            avb = P[4 + hqx % 2]
            repb = P[4 + (hqx + 1) % 2]
            for ktav in range(NTC):
                wl_av = []
                if ktav == 0:
                    wl_av = [(da, vh_d[hx]), (ac, exp_cnt[(hqx, NTC - 1)])]
                    if hqx >= 1:
                        wl_av.append((ac, st.get(("reps_ac", hqx - 1), 0)))
                pav = b.op("tensor", "matmul", avb[0:65, 0:512],
                           bass.AP(VH, 1040 * (hx % 2) + 65 * ktav,
                                   [[VH.shape[1], 128], [1, 65]]),
                           pool_ap(Abx + 512 * ktav, 512),
                           start=(ktav == 0), stop=(ktav == NTC - 1),
                           waits=wl_av, inc=(pe if ktav == NTC - 1 else None))
            st[("avlast", "hq", hqx)] = pav
            if qgx == 3:
                st[("avlast", hx)] = pav
            b.op("vector", "reciprocal", RECB[64:65, 0:512], avb[64:65, 0:512],
                 waits=[(pe, pav)], lp="bf16 recip")
            vr = b.op("vector", "drain", inc=ve)
            prep = b.op("tensor", "matmul", repb[0:64, 0:512], ONES[64:65, 0:64],
                        RECB[64:65, 0:512], start=True, stop=True,
                        waits=[(ve, vr)], inc=pe)
            arep = b.op("scalar", "copy", REPS[0:64, 0:512], repb[0:64, 0:512],
                        waits=[(pe, prep)], inc=ac, lp="bf16 reps")
            st[("reps_ac", hqx)] = arep
            aslot = 512 * (hqx % 2)
            wl_avt = [(ac, arep)]
            if hqx >= 2:
                wl_avt.append((da, avst_d[hqx - 2]))
            vavt = b.op("vector", "tensor_tensor", AVST[0:64, aslot:aslot + 512],
                        avb[0:64, 0:512], REPS[0:64, 0:512], ALU.mult,
                        waits=wl_avt, inc=ve, lp="bf16 avt")
            st[("avt_tt", hqx)] = vavt
            avst_d[hqx] = b.op("sync", "dma_start",
                               bass.AP(dr["h_avt"], 2048 * 64 * hx + 512 * qgx,
                                       [[2048, 64], [1, 512]]),
                               AVST[0:64, aslot:aslot + 512],
                               waits=[(ve, vavt)], inc=da, inc_by=16)

        for qg in range(4):
            hq = h * 4 + qg
            Ab = ABUF[hq % 2]
            for kt in range(NTC):
                bank = P[kt % 4]
                wl = []
                if kt >= 4:
                    wl.append((ve, tt2_cnt[(hq, kt - 4)]))
                elif hq >= 1:
                    wl.append((ve, tt2_cnt[(hq - 1, kt + 12)]))
                else:
                    wl.append((ac, proj_ac))
                if kt == 0 and hq >= 2:
                    wl.append((pe, st.get(("avlast", "hq", hq - 2), 0)))
                pqk = b.op("tensor", "matmul", bank[:, 0:512],
                           bass.AP(RESB, 16384 + 2048 * hh + 128 * kt + p0 * RESB.shape[1],
                                   [[RESB.shape[1], 64], [1, 128]]),
                           bass.AP(RESB, 2048 * hh + 512 * qg + p0 * RESB.shape[1],
                                   [[RESB.shape[1], 64], [1, 512]]),
                           start=True, stop=True, waits=wl, inc=pe)
                is_left = (128 * kt + 127) <= (512 * qg - 91)
                is_right = (128 * kt) >= (512 * qg + 511 + 91)
                if is_left or is_right:
                    coff = WSLOT + 4096 * (h % 2) + (0 if is_left else 2048)
                    bias_ap = bass.AP(POOL, coff + 512 * qg,
                                      [[POOL.shape[1], 128], [1, 512]])
                else:
                    rv = bass.AP(POOL, ROLL[h % 2] + 2047 + 128 * kt - 512 * qg,
                                 [[POOL.shape[1], 128], [-1, 512]])
                    wl_t = []
                    if hq >= 2 and kt == 0:
                        wl_t.append((ac, exp_cnt.get((hq - 2, NTC - 1), 0)))
                    b.op("vector", "tensor_tensor",
                         bass.AP(SMALLB, TB, [[SMALLB.shape[1], 128], [1, 512]]),
                         GREP[:, gslot + 512 * qg:gslot + 512 * qg + 512], rv, ALU.mult,
                         waits=wl_t, lp="bf16 bias")
                    bias_ap = bass.AP(SMALLB, TB, [[SMALLB.shape[1], 128], [1, 512]])
                wl2 = [(pe, pqk)]
                if kt >= 4:
                    wl2.append((ac, exp_cnt[(hq, kt - 4)]))
                elif hq >= 1:
                    wl2.append((ac, exp_cnt[(hq - 1, 12 + kt)]))
                v2 = b.op("vector", "scalar_tensor_tensor",
                          bass.AP(SMALLB, SCBC[kt % 4], [[SMALLB.shape[1], 128], [1, 512]]),
                          bank[:, 0:512], SC, bias_ap,
                          ALU.mult, ALU.add, inc=ve, lp="bf16 scores")
                tt2_cnt[(hq, kt)] = v2
                a = b.op("scalar", "activation",
                         pool_ap(Ab + 512 * kt, 512),
                         bass.AP(SMALLB, SCBC[kt % 4], [[SMALLB.shape[1], 128], [1, 512]]),
                         AF.Exp, waits=[(ve, v2)], inc=ac, lp="bf16 softmax")
                exp_cnt[(hq, kt)] = a
                if st["cut"] == "qkonly" and kt == NTC - 1:
                    st["l1a_done_ve"] = b.cnt[ve]
                    st["l1a_preln3_ac"] = b.cnt[ac]
                    return
            if qg == 3:
                st[("tt1last", h)] = tt2_cnt[(hq, NTC - 1)]
            if hq >= 1:
                av_tail(hq - 1)
            if qg == 0 and h >= 1:
                prefetch(h + 1)

    av_tail(63)
    b.wait("gpsimd", ve, b.cnt[ve])
    w_chunk_dma(b, (POOL, WSLOT), dr["L1_attn_Wo"], D, D, dw)
    wo_done = b.cnt[dw]
    if st["cut"] == "mainloop":
        st["l1a_done_ve"] = b.cnt[ve]
        st["l1a_preln3_ac"] = b.cnt[ac]
        return
    AVL = 0
    post = {}
    xstream = {}
    for t in range(NTC):
        wl = [(da, avst_d[63])] if t == 0 else []
        dls = []
        for g2 in range(2):
            dd = b.op("sync", "dma_start",
                      bass.AP(POOL, AVL + 1024 * (t % 2) + 512 * g2,
                              [[POOL.shape[1], 128], [128, 4], [1, 128]]),
                      bass.AP(dr["h_avt"], 128 * t + 4 * g2 * 128 * 2048,
                              [[2048, 128], [128 * 2048, 4], [1, 128]]),
                      waits=wl + ([(ve, post.get(t - 2, 0))] if t >= 2 else []),
                      inc=da, inc_by=16)
            dls.append(dd)
        dx = b.op("sync", "dma_start",
                  bass.AP(SMALL, 1024, [[SMALL.shape[1], 128], [1, 1024]]),
                  bass.AP(dr["h_x0"], 128 * 1024 * t, [[1024, 128], [1, 1024]]),
                  waits=[(ve, xstream.get(t - 1, 0))], inc=da, inc_by=16)
        pair = (P[0], P[1]) if t % 2 == 0 else (P[2], P[3])
        for half in range(2):
            for f in range(NFC):
                wl = []
                if f == 0:
                    wl = [(da, dls[-1])]
                    if t == 0 and half == 0:
                        wl += [(dw, wo_done)]
                    if t < 2 and half == 0:
                        wl.append((ve, st[("avt_tt", 63)]))
                    if t >= 2 and half == 0:
                        wl.append((ve, post.get(t - 2, 0)))
                b.op("tensor", "matmul", pair[half][:, 0:512],
                     bass.AP(POOL, AVL + 1024 * (t % 2) + 128 * f,
                             [[POOL.shape[1], 128], [1, 128]]),
                     bass.AP(POOL, WSLOT + 1024 * f + 512 * half,
                             [[POOL.shape[1], 128], [1, 512]]),
                     start=(f == 0), stop=False, waits=wl)
            po = b.op("tensor", "matmul", pair[half][:, 0:512], ONES[0:1, 0:128],
                      BROW[0:1, 1024 + 512 * half:1536 + 512 * half],
                      start=False, stop=True, inc=pe)
        b.wait("vector", pe, po)
        v2 = ln_stats(b, S, STAT, (pair[0][:, 0:512], pair[1][:, 0:512]))
        b.op("vector", "tensor_scalar", SMALL[:, 0:512], pair[0][:, 0:512],
             STAT[:, 2:3], STAT[:, 3:4], ALU.mult, ALU.add)
        vln = b.op("vector", "tensor_scalar", SMALL[:, 512:1024], pair[1][:, 0:512],
                   STAT[:, 2:3], STAT[:, 3:4], ALU.mult, ALU.add, inc=ve)
        b.wait("vector", da, dx)
        vxn = b.op("vector", "tensor_tensor", SMALL[:, 2048:3072], SMALL[:, 1024:2048],
                   SMALL[:, 0:1024], ALU.add, inc=ve)
        post[t] = vxn
        dst = dr["y"] if n_sub == 3 else dr["h_x1"]
        dxo = b.op("sync", "dma_start",
                   bass.AP(dst, 128 * 1024 * t, [[1024, 128], [1, 1024]]),
                   bass.AP(SMALL, 2048, [[SMALL.shape[1], 128], [1, 1024]]),
                   waits=[(ve, vxn)], inc=da, inc_by=16)
        xstream[t] = vxn
        st[("x1_dma", t)] = dxo
        if n_sub >= 4:
            v2b = ln_stats(b, S, STAT, (SMALL[:, 2048:2560], SMALL[:, 2560:3072]))
            v3 = b.op("vector", "tensor_scalar",
                      bass.AP(POOL, XT1, [[POOL.shape[1], 128], [1, 1024]]),
                      SMALL[:, 2048:3072], STAT[:, 2:3], STAT[:, 3:4],
                      ALU.mult, ALU.add,
                      waits=[(pe, st.get("_tp_pe", 0))], inc=ve)
            xstream[t] = v3
            for f in range(NFC):
                wl = [(ve, v3)]
                if f >= 2:
                    wl.append((ac, st["_tp_ac"][f % 2]))
                a = transpose_tile(b, st, bass.AP(POOL, XT1 + 128 * f,
                                                  [[POOL.shape[1], 128], [1, 128]]),
                                   XNB[:, 2048 * f + 128 * t:2048 * f + 128 * t + 128],
                                   f, waits_t=wl)
                st["_tp_ac"][f % 2] = a
    if n_sub == 3 and st["cut"] is None:
        last = st[("x1_dma", NTC - 1)]
        b.wait("sync", da, last)
    st["l1a_done_ve"] = b.cnt[ve]
    st["l1a_preln3_ac"] = b.cnt[ac]


def l1_ffn(st, WA, WB, WC, REST):
    b, S, dr, P = st["b"], st["S"], st["dr"], st["P"]
    POOL, XNB, RES, SMALL, STAT, RW = (st["POOL"], st["XNB"], st["RES"], st["SMALL"],
                                       st["STAT"], st["RW"])
    ONES, BROW = st["ONES"], st["BROW"]
    dw, da, pe, ve, ac = S["dw"], S["da"], S["pe"], S["ve"], S["ac"]
    OUT = RES

    WS1, WS2, HT, WRC = 0, 8192, 16384, 32768

    start_ve = st["l1a_done_ve"]
    wr_d = b.op("gpsimd", "dma_start",
                bass.AP(POOL, WRC, [[POOL.shape[1], 128], [8, 8], [1, 8]]),
                bass.AP(dr["L1_ffn_Wr"], 0, [[8, 128], [128 * 8, 8], [1, 8]]),
                waits=[(ve, start_ve)], inc=dw, inc_by=16)
    eb_d = b.op("gpsimd", "dma_start", BROW[0:1, 0:8],
                bass.AP(dr["L1_ffn_expert_bias"], 0, [[8, 1], [1, 8]]),
                inc=dw, inc_by=16)
    w_chunk_dma(b, (POOL, WS1), dr["L1_ffn_Sg"], D, D, dw)
    sg_done = b.cnt[dw]
    w_chunk_dma(b, (POOL, WS2), dr["L1_ffn_Su"], D, D, dw)
    su_done = b.cnt[dw]

    rt_evac = {}
    for t in range(NTC):
        bank = P[4 + t % 2]
        wl = [(dw, eb_d), (ac, st["l1a_preln3_ac"])] if t == 0 else              [(ve, rt_evac[t - 2])] if t >= 2 else []
        for f in range(NFC):
            b.op("tensor", "matmul", bank[:, 0:8],
                 bass.AP(XNB, 2048 * f + 128 * t, [[XNB.shape[1], 128], [1, 128]]),
                 bass.AP(POOL, WRC + 8 * f, [[POOL.shape[1], 128], [1, 8]]),
                 start=(f == 0), stop=False, waits=wl if f == 0 else [])
        pr = b.op("tensor", "matmul", bank[:, 0:8], ONES[0:1, 0:128],
                  BROW[0:1, 0:8], start=False, stop=True, inc=pe)
        vr = b.op("vector", "tensor_copy", RW[:, 8 * t:8 * t + 8], bank[:, 0:8],
                  waits=[(pe, pr)], inc=ve)
        rt_evac[t] = vr
    for t in range(NTC):
        L = RW[:, 8 * t:8 * t + 8]
        b.op("vector", "tensor_reduce", STAT[:, 32:33], L, axis=AX.X, op=ALU.max)
        b.op("vector", "drain")
        b.op("vector", "tensor_scalar", STAT[:, 34:42], L, STAT[:, 32:33], None,
             ALU.is_ge)
        b.op("vector", "drain")
        b.op("vector", "scalar_tensor_tensor", STAT[:, 42:50], STAT[:, 34:42], -1e30,
             L, ALU.mult, ALU.add)
        b.op("vector", "drain")
        b.op("vector", "tensor_reduce", STAT[:, 33:34], STAT[:, 42:50], axis=AX.X,
             op=ALU.max)
        b.op("vector", "drain")
        b.op("vector", "tensor_scalar", STAT[:, 34:42], L, STAT[:, 33:34], None,
             ALU.is_ge)
        b.op("vector", "tensor_scalar", STAT[:, 50:51], STAT[:, 32:33], -1.0, None,
             ALU.mult)
        v1 = b.op("vector", "drain", inc=ve)
        a1 = b.op("scalar", "activation", STAT[:, 42:50], L, AF.Exp,
                  bias=STAT[:, 50:51], scale=1.0, waits=[(ve, v1)], inc=ac)
        b.op("vector", "tensor_tensor", STAT[:, 42:50], STAT[:, 42:50],
             STAT[:, 34:42], ALU.mult, waits=[(ac, a1)])
        b.op("vector", "drain")
        b.op("vector", "tensor_reduce", STAT[:, 51:52], STAT[:, 42:50], axis=AX.X,
             op=ALU.add)
        b.op("vector", "drain")
        b.op("vector", "reciprocal", STAT[:, 51:52], STAT[:, 51:52])
        b.op("vector", "drain")
        vrw = b.op("vector", "tensor_scalar", RW[:, 128 + 8 * t:136 + 8 * t],
                   STAT[:, 42:50], STAT[:, 51:52], None, ALU.mult, inc=ve)
    rw_done = b.cnt[ve]

    h_done = {}
    for m in range(NFC):
        for g in range(4):
            pg_b = P[0 + (g % 2)]
            pu_b = P[2 + (g % 2)]
            idx = m * 4 + g
            for f in range(NFC):
                wl = []
                if f == 0:
                    if idx == 0:
                        wl = [(dw, sg_done), (ve, st["l1a_done_ve"])]
                    if idx >= 2:
                        wl.append((ve, h_done[idx - 2]))
                b.op("tensor", "matmul", pg_b[:, 0:512],
                     bass.AP(POOL, WS1 + 1024 * f + 128 * m,
                             [[POOL.shape[1], 128], [1, 128]]),
                     bass.AP(XNB, 2048 * f + 512 * g, [[XNB.shape[1], 128], [1, 512]]),
                     start=(f == 0), stop=(f == 7), waits=wl,
                     inc=(pe if f == 7 else None))
            pgc = b.cnt[pe]
            for f in range(NFC):
                wl = [(dw, su_done)] if (idx == 0 and f == 0) else []
                b.op("tensor", "matmul", pu_b[:, 0:512],
                     bass.AP(POOL, WS2 + 1024 * f + 128 * m,
                             [[POOL.shape[1], 128], [1, 128]]),
                     bass.AP(XNB, 2048 * f + 512 * g, [[XNB.shape[1], 128], [1, 512]]),
                     start=(f == 0), stop=(f == 7), waits=wl,
                     inc=(pe if f == 7 else None))
            puc = b.cnt[pe]
            sbuf0 = 512 * (idx % 2)
            wl_s = [(pe, pgc)]
            if idx >= 2:
                wl_s.append((ve, h_done[idx - 2]))
            a1 = b.op("scalar", "activation", SMALL[:, sbuf0:sbuf0 + 512],
                      pg_b[:, 0:512], AF.Silu, waits=wl_s, inc=ac)
            vh = b.op("vector", "tensor_tensor",
                      bass.AP(POOL, HT + 2048 * m + 512 * g,
                              [[POOL.shape[1], 128], [1, 512]]),
                      pu_b[:, 0:512], SMALL[:, sbuf0:sbuf0 + 512], ALU.mult,
                      waits=[(ac, a1), (pe, puc)], inc=ve, lp="bf16 h")
            h_done[idx] = vh
    sh_h_ve = b.cnt[ve]
    b.wait("gpsimd", pe, b.cnt[pe])
    w_chunk_dma(b, (POOL, WS2), dr["L1_ffn_Sd"], D, D, dw)
    sd_done = b.cnt[dw]
    dsh = {}
    for t in range(NTC):
        g, r = divmod(t, 4)
        pair = (P[4], P[5]) if t % 2 == 0 else (P[0], P[1])
        for half in range(2):
            for m in range(NFC):
                wl = []
                if m == 0:
                    if t < 2 and half == 0:
                        wl = [(dw, sd_done), (ve, sh_h_ve)]
                    if t >= 2 and half == 0:
                        wl.append((ac, dsh[t - 2]))
                b.op("tensor", "matmul", pair[half][:, 0:512],
                     bass.AP(POOL, HT + 2048 * m + 128 * t,
                             [[POOL.shape[1], 128], [1, 128]]),
                     bass.AP(POOL, WS2 + 1024 * m + 512 * half,
                             [[POOL.shape[1], 128], [1, 512]]),
                     start=(m == 0), stop=(m == 7),
                     inc=(pe if m == 7 else None), waits=wl)
        pd = b.cnt[pe]
        b.op("scalar", "copy", OUT[:, 1024 * t:1024 * t + 512], pair[0][:, 0:512],
             waits=[(pe, pd)])
        a = b.op("scalar", "copy", OUT[:, 1024 * t + 512:1024 * t + 1024],
                 pair[1][:, 0:512], inc=ac)
        dsh[t] = a
    out_init_ac = b.cnt[ac]

    ex_state = {}
    EWS = [0, 12288]
    HTE = 24576
    exw = {}

    def load_expert(e):
        if e >= 8:
            return
        base2 = EWS[e % 2]
        if e < 2:
            b.wait("gpsimd", pe, b.cnt[pe])
        else:
            b.wait("gpsimd", pe, exw[("lastpe", e - 2)])
        for nm, off in (("L1_ffn_Wg", base2), ("L1_ffn_Wu", base2 + 4096)):
            dstap = bass.AP(POOL, off, [[POOL.shape[1], 128], [512, 8], [1, 512]])
            srcap = bass.AP(dr[nm], e * D * 512, [[512, 128], [128 * 512, 8], [1, 512]])
            b.op("gpsimd", "dma_start", dstap, srcap, inc=dw, inc_by=16)
        wd_dst = bass.AP(POOL, base2 + 8192, [[POOL.shape[1], 128], [1024, 4], [1, 1024]])
        wd_src = bass.AP(dr["L1_ffn_Wd"], e * 512 * D,
                         [[1024, 128], [128 * 1024, 4], [1, 1024]])
        b.op("gpsimd", "dma_start", wd_dst, wd_src, inc=dw, inc_by=16)
        exw[("dw", e)] = b.cnt[dw]

    for e in range(8):
        base = EWS[e % 2]
        if e == 0:
            load_expert(0)
            load_expert(1)
        wg_d = exw[("dw", e)]
        wd_d = wg_d
        eh_done = {}
        for m in range(4):
            for g in range(4):
                idx = m * 4 + g
                pg_b = P[0 + (g % 2)]
                pu_b = P[2 + (g % 2)]
                for f in range(NFC):
                    wl = []
                    if f == 0:
                        if idx == 0:
                            wl = [(dw, wg_d)]
                            if e == 0:
                                wl.append((ac, out_init_ac))
                            else:
                                wl.append((ve, ex_state[e - 1]))
                        if idx >= 2:
                            wl.append((ve, eh_done[idx - 2]))
                    b.op("tensor", "matmul", pg_b[:, 0:512],
                         bass.AP(POOL, base + 512 * f + 128 * m,
                                 [[POOL.shape[1], 128], [1, 128]]),
                         bass.AP(XNB, 2048 * f + 512 * g, [[XNB.shape[1], 128], [1, 512]]),
                         start=(f == 0), stop=(f == 7), waits=wl,
                         inc=(pe if f == 7 else None))
                pgc = b.cnt[pe]
                for f in range(NFC):
                    b.op("tensor", "matmul", pu_b[:, 0:512],
                         bass.AP(POOL, base + 4096 + 512 * f + 128 * m,
                                 [[POOL.shape[1], 128], [1, 128]]),
                         bass.AP(XNB, 2048 * f + 512 * g, [[XNB.shape[1], 128], [1, 512]]),
                         start=(f == 0), stop=(f == 7),
                         inc=(pe if f == 7 else None))
                puc = b.cnt[pe]
                sbuf0 = 512 * (idx % 2)
                wl_s = [(pe, pgc)]
                if idx >= 2:
                    wl_s.append((ve, eh_done[idx - 2]))
                elif e == 0:
                    wl_s.append((ve, h_done[30]))
                else:
                    wl_s.append((ve, ex_state[e - 1]))
                a1 = b.op("scalar", "activation", SMALL[:, sbuf0:sbuf0 + 512],
                          pg_b[:, 0:512], AF.Silu, waits=wl_s, inc=ac)
                vh = b.op("vector", "tensor_tensor",
                          bass.AP(POOL, HTE + 2048 * m + 512 * g,
                                  [[POOL.shape[1], 128], [1, 512]]),
                          pu_b[:, 0:512], SMALL[:, sbuf0:sbuf0 + 512], ALU.mult,
                          waits=[(ac, a1), (pe, puc)], inc=ve, lp="bf16 h")
                eh_done[idx] = vh
        he_ve = b.cnt[ve]
        dacc = {}
        for t in range(NTC):
            pair = (P[4], P[5]) if t % 2 == 0 else (P[0], P[1])
            for half in range(2):
                for m in range(4):
                    wl = []
                    if m == 0:
                        if t < 2 and half == 0:
                            wl = [(dw, wd_d), (ve, he_ve)]
                        if t >= 2 and half == 0:
                            wl.append((ve, dacc[t - 2]))
                    b.op("tensor", "matmul", pair[half][:, 0:512],
                         bass.AP(POOL, HTE + 2048 * m + 128 * t,
                                 [[POOL.shape[1], 128], [1, 128]]),
                         bass.AP(POOL, base + 8192 + 1024 * m + 512 * half,
                                 [[POOL.shape[1], 128], [1, 512]]),
                         start=(m == 0), stop=(m == 3),
                         inc=(pe if m == 3 else None), waits=wl)
            pd = b.cnt[pe]
            wl = [(pe, pd)]
            if e == 0 and t < 2:
                wl.append((ve, rw_done))
            b.op("vector", "scalar_tensor_tensor", OUT[:, 1024 * t:1024 * t + 512],
                 pair[0][:, 0:512], RW[:, 128 + 8 * t + e:129 + 8 * t + e],
                 OUT[:, 1024 * t:1024 * t + 512], ALU.mult, ALU.add, waits=wl)
            vd = b.op("vector", "scalar_tensor_tensor",
                      OUT[:, 1024 * t + 512:1024 * t + 1024],
                      pair[1][:, 0:512], RW[:, 128 + 8 * t + e:129 + 8 * t + e],
                      OUT[:, 1024 * t + 512:1024 * t + 1024], ALU.mult, ALU.add,
                      inc=ve)
            dacc[t] = vd
        ex_state[e] = b.cnt[ve]
        exw[("lastpe", e)] = b.cnt[pe]
        load_expert(e + 2)

    ydma = {}
    for t in range(NTC):
        dx = b.op("sync", "dma_start",
                  bass.AP(SMALL, 1024, [[SMALL.shape[1], 128], [1, 1024]]),
                  bass.AP(dr["h_x1"], 128 * 1024 * t, [[1024, 128], [1, 1024]]),
                  waits=[(da, st[("x1_dma", t)]), (ve, ex_state[7])] if t == 0
                  else [(ve, ydma.get(t - 1, 0))], inc=da, inc_by=16)
        v2 = ln_stats(b, S, STAT,
                      (OUT[:, 1024 * t:1024 * t + 512],
                       OUT[:, 1024 * t + 512:1024 * t + 1024]),
                      waits_v=[(ve, ex_state[7])] if t == 0 else ())
        b.op("vector", "tensor_scalar", SMALL[:, 0:512], OUT[:, 1024 * t:1024 * t + 512],
             STAT[:, 2:3], STAT[:, 3:4], ALU.mult, ALU.add)
        b.op("vector", "tensor_scalar", SMALL[:, 512:1024],
             OUT[:, 1024 * t + 512:1024 * t + 1024],
             STAT[:, 2:3], STAT[:, 3:4], ALU.mult, ALU.add)
        b.wait("vector", da, dx)
        vy = b.op("vector", "tensor_tensor", SMALL[:, 2048:3072], SMALL[:, 1024:2048],
                  SMALL[:, 0:1024], ALU.add, inc=ve)
        dy = b.op("sync", "dma_start",
                  bass.AP(dr["y"], 128 * 1024 * t, [[1024, 128], [1, 1024]]),
                  bass.AP(SMALL, 2048, [[SMALL.shape[1], 128], [1, 1024]]),
                  waits=[(ve, vy)], inc=da, inc_by=16)
        ydma[t] = vy
        st[("y_dma", t)] = dy
    b.wait("sync", da, st[("y_dma", NTC - 1)])



def make_in_maps(inputs, n_cores=8):
    consts = host_constants()
    x = np.ascontiguousarray(inputs["L0_attn_x"], dtype=np.float32)
    base = {}
    for k, v in inputs.items():
        if k == "L0_attn_x":
            continue
        base[k] = np.ascontiguousarray(v, dtype=np.float32)
    for k, v in consts.items():
        base[k] = v
    maps = []
    for c in range(n_cores):
        m = dict(base)
        m["x"] = x[c]
        maps.append(m)
    return maps


def kernel(**inputs):
    nc = build_program(n_sub=4)
    in_maps = make_in_maps(inputs)
    res = run_bass_kernel_spmd(nc, in_maps, core_ids=list(range(8)))
    out = np.stack([np.asarray(r["y"], dtype=np.float32) for r in res.results], axis=0)
    return out
